# version 04
# baseline (speedup 1.0000x reference)
import math
import numpy as np
import ml_dtypes
import concourse.bass as bass
import concourse.mybir as mybir
from bass_rust import ReduceOp
from concourse.bass_utils import run_bass_kernel_spmd

F32 = mybir.dt.float32
BF16 = mybir.dt.bfloat16
AF = mybir.ActivationFunctionType
ALU = mybir.AluOpType

SEQ = 2048
NSEQ = 2
T = SEQ * NSEQ
D = 1024
NT = T // 128
HD = 64
NH = 16
BH = NSEQ * NH
NKT = SEQ // 128
NBLK = 64
BLK = 32
NUM_BLOCK = 1024
BIG = float(2.0 ** 30)
CMARGIN = 30.0
LN_EPS = 1e-5
WINDOW = 512
MAXREL = 128


def _split_bf16(x):
    x = np.asarray(x, np.float32)
    hi = x.astype(ml_dtypes.bfloat16)
    lo = (x - hi.astype(np.float32)).astype(ml_dtypes.bfloat16)
    return hi, lo


def host_consts():
    c = {}
    idx = np.arange(D, dtype=np.float64)
    ang = 2.0 * np.pi * np.outer(idx, idx) / D
    cos = np.cos(ang).astype(np.float32)
    sin = np.sin(ang).astype(np.float32)
    c["C_COS_HI"], c["C_COS_LO"] = _split_bf16(cos)
    c["C_SIN_HI"], c["C_SIN_LO"] = _split_bf16(sin)
    c["C_SINN_HI"], c["C_SINN_LO"] = _split_bf16(-sin)
    coss = (cos.astype(np.float64) / D).astype(np.float32)
    sinns = (-sin.astype(np.float64) / D).astype(np.float32)
    c["C_COSS_HI"], c["C_COSS_LO"] = _split_bf16(coss)
    c["C_SINNS_HI"], c["C_SINNS_LO"] = _split_bf16(sinns)
    ii = np.arange(NBLK)
    c["C_BAND"] = (5000.0 * (np.abs(ii[:, None] - ii[None, :]) <= 1)).astype(np.float32)
    t = np.arange(SEQ)
    c["C_ONEHOT"] = ((t[None, :] // BLK) == ii[:, None]).astype(ml_dtypes.bfloat16)
    c["C_ID128"] = np.eye(128, dtype=np.float32)
    c["C_ID128B"] = np.eye(128, dtype=ml_dtypes.bfloat16)
    q = np.arange(128)
    c["C_P4"] = ((q[:, None] // 4) == (q[None, :] // 4)).astype(np.float32)
    i64 = np.arange(NBLK)
    p128 = np.arange(128)
    c["C_E4"] = ((i64[:, None] % 4) == (p128[None, :] // 32)).astype(np.float32)
    c["C_SEL"] = ((i64[:, None] // 4) == np.arange(16)[None, :]).astype(np.float32)
    kk = np.arange(128)
    c["C_TRI4"] = np.where(kk[None, :] > kk[:, None], -BIG, 0.0).astype(np.float32)
    c["C_TRI0"] = np.where(kk[None, :] < kk[:, None], -BIG, 0.0).astype(np.float32)
    c["C_ONES2"] = np.ones((2, 512), dtype=ml_dtypes.bfloat16)
    c["C_ONES1"] = np.ones((1, 128), dtype=ml_dtypes.bfloat16)
    return c


CONSTS = host_consts()


ENGINES = ("te", "act", "dve", "pool", "sp")


class _Op:
    __slots__ = ("eng", "fn", "waits", "need_inc", "inc_idx", "dma_sem", "dma_val", "tag")

    def __init__(self, eng, fn, waits, dma_sem=None, tag=""):
        self.eng = eng
        self.fn = fn
        self.waits = list(waits)
        self.need_inc = False
        self.inc_idx = None
        self.dma_sem = dma_sem
        self.dma_val = None
        self.tag = tag


class Prog:
    def __init__(self, nc):
        self.nc = nc
        self.ops = {e: [] for e in ENGINES}
        self.all_ops = []
        self.dma_sems = {}
        self.eng_sem = {}
        self._sem_guards = []

    def _get_dma_sem(self, name):
        if name not in self.dma_sems:
            g = self.nc.semaphore("dsem_" + name)
            h = g.__enter__()
            self._sem_guards.append(g)
            self.dma_sems[name] = [h, 0]
        return self.dma_sems[name]

    def op(self, eng, fn, after=(), tag=""):
        o = _Op(eng, fn, after, tag=tag)
        self.ops[eng].append(o)
        self.all_ops.append(o)
        return o

    def dma(self, eng, sem_name, fn, after=(), tag=""):
        sem_name = eng + "_" + sem_name
        ent = self._get_dma_sem(sem_name)
        ent[1] += 16
        o = _Op(eng, fn, after, dma_sem=sem_name, tag=tag)
        o.dma_val = ent[1]
        self.ops[eng].append(o)
        self.all_ops.append(o)
        return o

    def finalize(self):
        nc = self.nc
        for o in self.all_ops:
            for w in o.waits:
                if isinstance(w, _Op) and w.dma_sem is None:
                    w.need_inc = True
        eng_counts = {e: 0 for e in ENGINES}
        for e in ENGINES:
            for o in self.ops[e]:
                if o.dma_sem is None and o.need_inc:
                    eng_counts[e] += 1
                    o.inc_idx = eng_counts[e]
        for e in ENGINES:
            if eng_counts[e] > 0:
                g = self.nc.semaphore("esem_" + e)
                self.eng_sem[e] = g.__enter__()
                self._sem_guards.append(g)

        def emit(e, engine_obj):
            highwater = {}
            for o in self.ops[e]:
                for w in o.waits:
                    if isinstance(w, _Op):
                        if w.dma_sem is not None:
                            sem, val = self.dma_sems[w.dma_sem][0], w.dma_val
                        else:
                            assert w.inc_idx is not None
                            sem, val = self.eng_sem[w.eng], w.inc_idx
                            if w.eng == e and w.inc_idx is not None and o.dma_sem is None:
                                pass
                    else:
                        sem, val = w
                    key = id(sem)
                    if highwater.get(key, -1) < val:
                        engine_obj.wait_ge(sem, val)
                        highwater[key] = val
                inst = o.fn(engine_obj)
                if o.dma_sem is not None:
                    inst.then_inc(self.dma_sems[o.dma_sem][0], 16)
                elif o.need_inc:
                    inst.then_inc(self.eng_sem[e], 1)

        with nc.Block() as block:
            @block.sync
            def _(sync):
                emit("sp", sync)

            @block.tensor
            def _(tensor):
                emit("te", tensor)

            @block.scalar
            def _(scalar):
                emit("act", scalar)

            @block.vector
            def _(vector):
                emit("dve", vector)

            @block.gpsimd
            def _(gpsimd):
                emit("pool", gpsimd)


def build_program(debug_taps=()):
    nc = bass.Bass("TRN2", target_bir_lowering=False)

    def dram(name, shape, dtype, kind="Internal"):
        if kind == "Internal" and name in debug_taps:
            kind = "ExternalOutput"
        return nc.dram_tensor(name, shape, dtype, kind=kind).ap()

    XIN = dram("XIN", [T, D], F32, kind="ExternalInput")
    WQ = dram("WQ", [D, D], F32, kind="ExternalInput")
    WK = dram("WK", [D, D], F32, kind="ExternalInput")
    WV = dram("WV", [D, D], F32, kind="ExternalInput")
    BQ = dram("BQ", [D], F32, kind="ExternalInput")
    BK = dram("BK", [D], F32, kind="ExternalInput")
    BV = dram("BV", [D], F32, kind="ExternalInput")
    WQKV = dram("WQKV", [3 * D, D], F32, kind="ExternalInput")
    WPROJ = dram("WPROJ", [D, D], F32, kind="ExternalInput")
    BPROJ = dram("BPROJ", [D], F32, kind="ExternalInput")
    RELB = dram("RELB", [257, NH], F32, kind="ExternalInput")
    WFR = dram("WFR", [D, D], F32, kind="ExternalInput")
    WFI = dram("WFI", [D, D], F32, kind="ExternalInput")
    BFFN = dram("BFFN", [D], F32, kind="ExternalInput")
    CIN = {k: dram(k, list(v.shape), mybir.dt.from_np(v.dtype) if v.dtype != np.float32 else F32,
                   kind="ExternalInput") for k, v in CONSTS.items()}
    OUT = dram("OUT", [T, D], F32, kind="ExternalOutput")

    WT_Q_HI = dram("WT_Q_HI", [D, D], BF16); WT_Q_LO = dram("WT_Q_LO", [D, D], BF16)
    WT_K_HI = dram("WT_K_HI", [D, D], BF16); WT_K_LO = dram("WT_K_LO", [D, D], BF16)
    WT_V_HI = dram("WT_V_HI", [D, D], BF16); WT_V_LO = dram("WT_V_LO", [D, D], BF16)
    WT_B_HI = dram("WT_B_HI", [D, 3 * D], BF16); WT_B_LO = dram("WT_B_LO", [D, 3 * D], BF16)
    WT_P_HI = dram("WT_P_HI", [D, D], BF16); WT_P_LO = dram("WT_P_LO", [D, D], BF16)
    A_HI = dram("A_HI", [D, D], BF16); A_LO = dram("A_LO", [D, D], BF16)
    TRT_HI = dram("TRT_HI", [D, D], BF16); TRT_LO = dram("TRT_LO", [D, D], BF16)
    TIT_HI = dram("TIT_HI", [D, D], BF16); TIT_LO = dram("TIT_LO", [D, D], BF16)
    QA_HI = dram("QA_HI", [D, T], BF16); QA_LO = dram("QA_LO", [D, T], BF16)
    KA_HI = dram("KA_HI", [D, T], BF16); KA_LO = dram("KA_LO", [D, T], BF16)
    VA_HI = dram("VA_HI", [T, D], BF16); VA_LO = dram("VA_LO", [T, D], BF16)
    QB_HI = dram("QB_HI", [D, T], BF16); QB_LO = dram("QB_LO", [D, T], BF16)
    KB_HI = dram("KB_HI", [D, T], BF16); KB_LO = dram("KB_LO", [D, T], BF16)
    VB_HI = dram("VB_HI", [T, D], BF16); VB_LO = dram("VB_LO", [T, D], BF16)
    YB_HI = dram("YB_HI", [BH, HD, SEQ], BF16); YB_LO = dram("YB_LO", [BH, HD, SEQ], BF16)
    YA2 = dram("YA2", [T, D], F32)
    XS1 = dram("XS1", [T, D], F32)
    XS2 = dram("XS2", [T, D], F32)
    NORMF = dram("NORMF", [BH * 4096], F32)
    DBG_CALL = dram("DBG_CALL", [128, BH], F32)
    DBG_RET = dram("DBG_RET", [128, BH * 16], F32)
    DBG_LOT = dram("DBG_LOT", [128, BH * 64], F32)
    DBG_OEV = dram("DBG_OEV", [128, SEQ], F32)
    DBG_XBM = dram("DBG_XBM", [128, 16], F32)
    DBG_M8 = dram("DBG_M8", [128, SEQ], F32)
    DBG_PT = dram("DBG_PT", [128, SEQ], F32)
    MASKF = dram("MASKF", [BH * 4096], BF16)
    Z32F = dram("Z32F", [BH * 4096], F32)

    sb = nc.alloc_sbuf_tensor
    c_oneh = sb("c_oneh", [64, SEQ], BF16).ap()
    c_id = sb("c_id", [128, 128], F32).ap()
    c_idb = sb("c_idb", [128, 128], BF16).ap()
    c_band = sb("c_band", [64, 64], F32).ap()
    c_p4 = sb("c_p4", [128, 128], F32).ap()
    c_e4 = sb("c_e4", [64, 128], F32).ap()
    c_sel = sb("c_sel", [64, 16], F32).ap()
    c_tri4 = sb("c_tri4", [128, 128], F32).ap()
    c_tri0 = sb("c_tri0", [128, 128], F32).ap()
    c_ones2 = sb("c_ones2", [2, 512], BF16).ap()
    c_ones1 = sb("c_ones1", [1, 128], BF16).ap()
    c_ones1f = sb("c_ones1f", [1, 128], F32).ap()
    xt_g = sb("xt_g", [128, 2 * 4 * D], F32).ap()
    xn_t = sb("xn_t", [128, 2 * D], F32).ap()
    t32 = xn_t[:, 0:512]
    vhs = xn_t[:, D:2 * D]
    xnT_hi = sb("xnT_hi", [128, 8 * 512], BF16).ap()
    xnT_lo = sb("xnT_lo", [128, 8 * 512], BF16).ap()
    st6 = sb("st6", [128, 12], F32).ap()
    st2 = sb("st2", [128, 2], F32).ap()
    lnsc = sb("lnsc", [128, 16], F32).ap()
    wg_hi = sb("wg_hi", [128, 2 * 8 * 256], BF16).ap()
    wg_lo = sb("wg_lo", [128, 2 * 8 * 256], BF16).ap()
    wstg = sb("wstg", [128, D], F32).ap()
    wtb_hi = sb("wtb_hi", [128, D], BF16).ap()
    wtb_lo = sb("wtb_lo", [128, D], BF16).ap()
    brow_q = sb("brow_q", [2, D], BF16).ap()
    brow_k = sb("brow_k", [2, D], BF16).ap()
    brow_v = sb("brow_v", [2, D], BF16).ap()
    brow_p = sb("brow_p", [2, D], BF16).ap()
    brow_f = sb("brow_f", [2, D], BF16).ap()

    pev_hi = sb("pev_hi", [128, 2 * 512], BF16).ap()
    pev_lo = sb("pev_lo", [128, 2 * 512], BF16).ap()
    wfr_hi = sb("wfr_hi", [128, 8 * 256], BF16).ap()
    wfr_lo = sb("wfr_lo", [128, 8 * 256], BF16).ap()
    wfi_hi = sb("wfi_hi", [128, 8 * 256], BF16).ap()
    wfi_lo = sb("wfi_lo", [128, 8 * 256], BF16).ap()
    omg = sb("omg", [128, 2048], BF16).ap()
    qt_hi = sb("qt_hi", [128, SEQ], BF16).ap()
    qt_lo = sb("qt_lo", [128, SEQ], BF16).ap()
    kt_hi = sb("kt_hi", [128, SEQ], BF16).ap()
    kt_lo = sb("kt_lo", [128, SEQ], BF16).ap()
    vsb_hi = sb("vsb_hi", [128, 16 * 65], BF16).ap()
    vsb_lo = sb("vsb_lo", [128, 16 * 65], BF16).ap()
    p32 = sb("p32", [128, 2 * 512], F32).ap()
    ptil = sb("ptil", [128, 2 * SEQ], BF16).ap()
    ptil_hi = ptil[:, 0:SEQ]
    ptil_lo = ptil[:, SEQ:2 * SEQ]
    lowall = ptil.bitcast(F32)[0:64, :]
    m8 = sb("m8", [128, SEQ], F32).ap()
    normall = m8[0:64, :]
    xbm = sb("xbm", [128, 16], F32).ap()
    oev = sb("oev", [128, SEQ], F32).ap()
    oev2 = None
    bisf = None
    mb64 = sb("mb64", [64, 64], BF16).ap()
    z64 = sb("z64", [64, 64], F32).ap()
    lot = sb("lot", [128, BH * 64], F32).ap()
    ret = sb("ret", [128, BH * 16], F32).ap()
    call_a = sb("call_a", [128, BH], F32).ap()
    call_b = sb("call_b", [128, BH], F32).ap()
    call_bc = sb("call_bc", [128, BH], F32).ap()
    qht = None
    kht = None
    xnh = sb("xnh", [128, 8 * 128], F32).ap()
    xnh_hi = sb("xnh_hi", [128, 8 * 128], BF16).ap()
    xnh_lo = sb("xnh_lo", [128, 8 * 128], BF16).ap()
    rmcol = sb("rmcol", [64, BH], F32).ap()
    bcols = sb("bcols", [128, 8], F32).ap()
    vha = sb("vha", [64, 65], F32).ap()
    lat = sb("lat", [64, 64], F32).ap()
    sc_a = sb("sc_a", [64, 64], F32).ap()
    sc_b = sb("sc_b", [64, 16], F32).ap()
    sc_c = sb("sc_c", [128, 65], F32).ap()
    sc_d = sb("sc_d", [64, 16], F32).ap()
    yast = sb("yast", [128, 2 * 64], F32).ap()
    rbext = sb("rbext", [1, 512], F32).ap()
    crow_t = sb("crow_t", [1, 32], F32).ap()
    b0t = sb("b0t", [128, 128], F32).ap()
    b1t = sb("b1t", [128, 128], F32).ap()
    rcs = sb("rcs", [1, 512], F32).ap()
    ybc_hi = sb("ybc_hi", [128, 8 * 128], BF16).ap()
    ybc_lo = sb("ybc_lo", [128, 8 * 128], BF16).ap()
    yt32 = sb("yt32", [64, 512], F32).ap()
    rrb = sb("rrb", [64, 512], F32).ap()
    xc_a = ptil.bitcast(F32)
    xc_b = m8

    qht = kt_hi.bitcast(F32)
    kht = kt_lo.bitcast(F32)
    bisf = oev[:, 0:1024]
    m01 = oev[:, 1024:2048]
    bisj = qt_hi[:, 0:1024]
    mbf = qt_hi[:, 1024:2048]
    z32f_s = qt_lo.bitcast(F32)

    ps_big = nc.alloc_psum_tensor("ps_big", [128, 2048], F32).ap()
    ps_s0 = nc.alloc_psum_tensor("ps_s0", [128, 512], F32).ap()
    ps_s1 = nc.alloc_psum_tensor("ps_s1", [128, 512], F32).ap()
    ps_t = nc.alloc_psum_tensor("ps_t", [128, 512], F32).ap()
    ps_m = nc.alloc_psum_tensor("ps_m", [128, 512], F32).ap()

    p = Prog(nc)
    return _build_phases(nc, p, locals())


class Slot:
    __slots__ = ("ap", "w", "r")

    def __init__(self, ap):
        self.ap = ap
        self.w = []
        self.r = []

    def dep_write(self):
        return list(self.r) + list(self.w)

    def wrote(self, *ops):
        self.w = list(ops)
        self.r = []

    def read(self, *ops):
        self.r.extend(ops)


def _build_phases(nc, p, v):
    g = dict(v)

    def V(name):
        return g[name]

    H = {}

    def mm3(ps_ap, lhi, llo, rhi, rlo, start, stop, after=(), tag=""):
        o1 = p.op("te", lambda t, a=ps_ap, b=lhi, c=rhi: t.matmul(a, b, c, start=start, stop=False),
                  after=after, tag=tag + "/hh")
        o2 = p.op("te", lambda t, a=ps_ap, b=lhi, c=rlo: t.matmul(a, b, c, start=False, stop=False),
                  tag=tag + "/hl")
        o3 = p.op("te", lambda t, a=ps_ap, b=llo, c=rhi: t.matmul(a, b, c, start=False, stop=stop),
                  tag=tag + "/lh")
        return [o1, o2, o3]

    def split_evac(ps_ap, hi_ap, lo_ap, scale, after, tag=""):
        oh = p.op("act", lambda s, a=hi_ap, b=ps_ap, sc=scale: s.activation(a, b, AF.Identity, bias=0.0, scale=sc),
                  after=after, tag=tag + "/hi")
        ol = p.op("dve", lambda d, a=lo_ap, b=ps_ap, c=hi_ap, sc=scale: d.scalar_tensor_tensor(
            out=a, in0=b, scalar=sc, in1=c, op0=ALU.mult, op1=ALU.subtract),
            after=after + [oh], tag=tag + "/lo")
        return oh, ol

    const_ops = {}
    for name in ("C_ONEHOT", "C_ID128", "C_ID128B", "C_BAND", "C_P4", "C_E4", "C_SEL",
                 "C_TRI4", "C_TRI0", "C_ONES2", "C_ONES1"):
        dst = {"C_ONEHOT": "c_oneh", "C_ID128": "c_id", "C_ID128B": "c_idb",
               "C_BAND": "c_band", "C_P4": "c_p4", "C_E4": "c_e4", "C_SEL": "c_sel",
               "C_TRI4": "c_tri4", "C_TRI0": "c_tri0", "C_ONES2": "c_ones2",
               "C_ONES1": "c_ones1"}[name]
        const_ops[dst] = p.dma("sp", "consts", lambda e, a=V(dst), b=V("CIN")[name]: e.dma_start(out=a[:, :], in_=b[:, :]),
                               tag="ld_" + dst)
    o = p.op("pool", lambda gp: gp.memset(V("c_ones1f")[:, :], 1.0), tag="ones1f")
    const_ops["c_ones1f"] = o

    bias_srcs = [("BQ", "brow_q"), ("BK", "brow_k"), ("BV", "brow_v"), ("BPROJ", "brow_p"),
                 ("BFFN", "brow_f")]
    wstg_slot = Slot(V("wstg"))
    wtbh_slot = Slot(V("wtb_hi"))
    wtbl_slot = Slot(V("wtb_lo"))
    H["bias_rows"] = []
    for i, (srcn, dst) in enumerate(bias_srcs):
        ld = p.dma("sp", "bias", lambda e, a=V("wstg"), b=V(srcn): e.dma_start(
            out=a[0:1, :], in_=b[:].rearrange("(a b) -> a b", a=1)), after=wstg_slot.dep_write(),
            tag="ldb")
        wstg_slot.wrote(ld)
        oh = p.op("act", lambda s, a=V("wtb_hi"), b=V("wstg"): s.activation(
            a[0:1, :], b[0:1, :], AF.Identity, bias=0.0, scale=1.0),
            after=[ld] + wtbh_slot.dep_write(), tag="bsp_h")
        ol = p.op("dve", lambda d, a=V("wtb_lo"), b=V("wstg"), c=V("wtb_hi"): d.scalar_tensor_tensor(
            out=a[0:1, :], in0=b[0:1, :], scalar=1.0, in1=c[0:1, :],
            op0=ALU.mult, op1=ALU.subtract), after=[ld, oh] + wtbl_slot.dep_write(), tag="bsp_l")
        wstg_slot.read(oh, ol)
        wtbh_slot.wrote(oh)
        wtbl_slot.wrote(ol)
        m1 = p.dma("pool", "bias_m", lambda e, a=V(dst), b=V("wtb_hi"): e.dma_start(
            out=a[0:1, :], in_=b[0:1, :]), after=[oh], tag="bm_h")
        m2 = p.dma("pool", "bias_m", lambda e, a=V(dst), b=V("wtb_lo"): e.dma_start(
            out=a[1:2, :], in_=b[0:1, :]), after=[ol], tag="bm_l")
        wtbh_slot.read(m1)
        wtbl_slot.read(m2)
        H["bias_rows"] += [m1, m2]

    wjobs = [
        (V("WQ"), D, V("WT_Q_HI"), V("WT_Q_LO"), 0),
        (V("WK"), D, V("WT_K_HI"), V("WT_K_LO"), 0),
        (V("WV"), D, V("WT_V_HI"), V("WT_V_LO"), 0),
        (V("WQKV"), 3 * D, V("WT_B_HI"), V("WT_B_LO"), 0),
        (V("WPROJ"), D, V("WT_P_HI"), V("WT_P_LO"), 0),
    ]
    pst_slots = [Slot(V("ps_t")), Slot(V("ps_m"))]
    H["wprep_store"] = {id(V("WT_Q_HI")): [], id(V("WT_K_HI")): [], id(V("WT_V_HI")): [],
                        id(V("WT_B_HI")): [], id(V("WT_P_HI")): []}
    idc = const_ops["c_id"]
    for src, orows, dhi, dlo, _ in wjobs:
        for oc in range(orows // 128):
            ld = p.dma("sp", "wstg", lambda e, a=V("wstg"), b=src, oc=oc: e.dma_start(
                out=a[:, :], in_=b[oc * 128:(oc + 1) * 128, :]), after=wstg_slot.dep_write(), tag="wp_ld")
            wstg_slot.wrote(ld)
            st_h, st_l = [], []
            for dt in range(8):
                sl = pst_slots[dt % 2]
                tr = p.op("te", lambda t, ps=sl.ap, a=V("wstg"), i=V("c_id"), dt=dt: t.transpose(
                    ps[:, 0:128], a[:, dt * 128:(dt + 1) * 128], i[:, :]),
                    after=[ld, idc] + sl.dep_write(), tag="wp_tr")
                sl.wrote(tr)
                oh, ol = split_evac(sl.ap[:, 0:128], V("wtb_hi")[:, dt * 128:(dt + 1) * 128],
                                    V("wtb_lo")[:, dt * 128:(dt + 1) * 128], 1.0,
                                    [tr] + wtbh_slot.dep_write() + wtbl_slot.dep_write(), tag="wp_sp")
                sl.read(oh, ol)
                s1 = p.dma("sp", "wtst", lambda e, a=dhi, b=V("wtb_hi"), dt=dt, oc=oc: e.dma_start(
                    out=a[dt * 128:(dt + 1) * 128, oc * 128:(oc + 1) * 128], in_=b[:, dt * 128:(dt + 1) * 128]),
                    after=[oh], tag="wp_sth")
                s2 = p.dma("sp", "wtst", lambda e, a=dlo, b=V("wtb_lo"), dt=dt, oc=oc: e.dma_start(
                    out=a[dt * 128:(dt + 1) * 128, oc * 128:(oc + 1) * 128], in_=b[:, dt * 128:(dt + 1) * 128]),
                    after=[ol], tag="wp_stl")
                st_h.append(s1); st_l.append(s2)
                H["wprep_store"][id(dhi)] += [s1, s2]
            wtbh_slot.wrote(*st_h)
            wtbh_slot.r = []
            wtbl_slot.wrote(*st_l)
            wtbl_slot.r = []
            wtbh_slot.w = []
            wtbh_slot.r = st_h
            wtbl_slot.w = []
            wtbl_slot.r = st_l

    wf_slots = [Slot(V(n)) for n in ("wfr_hi", "wfr_lo", "wfi_hi", "wfi_lo")]
    pss = [Slot(V("ps_s0")), Slot(V("ps_s1"))]
    t32_slot = Slot(V("t32"))
    CIN = V("CIN")
    trt_st = []
    for gc in range(4):
        wf_w = []
        for mi, (srcw, bh_, bl_) in enumerate([(V("WFR"), V("wfr_hi"), V("wfr_lo")),
                                               (V("WFI"), V("wfi_hi"), V("wfi_lo"))]):
            for ft in range(8):
                ld = p.dma("sp", "wfs", lambda e, a=V("wstg"), b=srcw, ft=ft, gc=gc: e.dma_start(
                    out=a[:, 0:256], in_=b[ft * 128:(ft + 1) * 128, gc * 256:(gc + 1) * 256]),
                    after=wstg_slot.dep_write(), tag="wf_ld")
                wstg_slot.wrote(ld)
                oh, ol = split_evac(V("wstg")[:, 0:256], bh_[:, ft * 256:(ft + 1) * 256],
                                    bl_[:, ft * 256:(ft + 1) * 256], 1.0,
                                    [ld] + wf_slots[2 * mi].dep_write() + wf_slots[2 * mi + 1].dep_write(),
                                    tag="wf_sp")
                wstg_slot.read(oh, ol)
                wf_w += [oh, ol]
        for s in wf_slots:
            s.wrote(*wf_w)
        omg_slots = [Slot(V("omg")[:, 0:1024]), Slot(V("omg")[:, 1024:2048])]
        for dc in range(8):
            first = True
            last_r = last_i = None
            for ft in range(8):
                osl = omg_slots[ft % 2]
                sl = (ft % 2) * 8
                OM = V("omg")
                oms = []
                for k, cn in enumerate(("C_COS_HI", "C_COS_LO", "C_SIN_HI", "C_SIN_LO",
                                        "C_SINN_HI", "C_SINN_LO")):
                    o = p.dma("sp", "omg", lambda e, a=OM, b=CIN[cn], ft=ft, dc=dc, sl=sl, k=k: e.dma_start(
                        out=a[:, (sl + k) * 128:(sl + k + 1) * 128],
                        in_=b[ft * 128:(ft + 1) * 128, dc * 128:(dc + 1) * 128]),
                        after=osl.dep_write(), tag="om_ld")
                    oms.append(o)
                osl.wrote(*oms)
                ch = OM[:, (sl + 0) * 128:(sl + 1) * 128]
                cl = OM[:, (sl + 1) * 128:(sl + 2) * 128]
                sh = OM[:, (sl + 2) * 128:(sl + 3) * 128]
                slo = OM[:, (sl + 3) * 128:(sl + 4) * 128]
                snh = OM[:, (sl + 4) * 128:(sl + 5) * 128]
                snl = OM[:, (sl + 5) * 128:(sl + 6) * 128]
                wrh = V("wfr_hi")[:, ft * 256:(ft + 1) * 256]
                wrl = V("wfr_lo")[:, ft * 256:(ft + 1) * 256]
                wih = V("wfi_hi")[:, ft * 256:(ft + 1) * 256]
                wil = V("wfi_lo")[:, ft * 256:(ft + 1) * 256]
                dep = oms + wf_w + (pss[0].dep_write() if first else [])
                a1 = mm3(pss[0].ap[:, 0:256], ch, cl, wrh, wrl, first, False, after=dep, tag="s1r_c")
                a2 = mm3(pss[0].ap[:, 0:256], sh, slo, wih, wil, False, ft == 7, tag="s1r_s")
                dep = (pss[1].dep_write() if first else [])
                a3 = mm3(pss[1].ap[:, 0:256], ch, cl, wih, wil, first, False, after=dep, tag="s1i_c")
                a5 = mm3(pss[1].ap[:, 0:256], snh, snl, wrh, wrl, False, ft == 7, tag="s1i_s")
                first = False
                last_r = a2[-1]
                last_i = a5[-1]
                osl.r = list(a1 + a2 + a3 + a5)
                osl.w = oms
            cp_r = p.op("dve", lambda d, a=V("t32"), b=pss[0].ap: d.tensor_copy(a[:, 0:256], b[:, 0:256]),
                        after=[last_r] + t32_slot.dep_write(), tag="s1ev_r")
            cp_i = p.op("dve", lambda d, a=V("t32"), b=pss[1].ap: d.tensor_copy(a[:, 256:512], b[:, 0:256]),
                        after=[last_i], tag="s1ev_i")
            pss[0].wrote(cp_r); pss[0].r = [cp_r]; pss[0].w = []
            pss[1].wrote(cp_i); pss[1].r = [cp_i]; pss[1].w = []
            t32_slot.wrote(cp_r, cp_i)
            for mi, (dsth, dstl, coff) in enumerate([(V("TRT_HI"), V("TRT_LO"), 0),
                                                     (V("TIT_HI"), V("TIT_LO"), 256)]):
                for j in range(2):
                    gt2 = gc * 256 + j * 128
                    sl2 = pst_slots[(j * 2 + mi) % 2]
                    tr = p.op("te", lambda t, ps=sl2.ap, a=V("t32"), i=V("c_id"), j=j, coff=coff: t.transpose(
                        ps[:, 0:128], a[:, coff + j * 128:coff + (j + 1) * 128], i[:, :]),
                        after=[cp_r, cp_i] + sl2.dep_write(), tag="s1_tr")
                    sl2.wrote(tr)
                    t32_slot.read(tr)
                    oh, ol = split_evac(sl2.ap[:, 0:128], V("wtb_hi")[:, 0:128], V("wtb_lo")[:, 0:128],
                                        1.0, [tr] + wtbh_slot.dep_write() + wtbl_slot.dep_write(),
                                        tag="s1_tsp")
                    sl2.read(oh, ol)
                    wtbh_slot.wrote(oh)
                    wtbl_slot.wrote(ol)
                    s1_ = p.dma("sp", "trt_st", lambda e, a=dsth, b=V("wtb_hi"), gt2=gt2, dc=dc: e.dma_start(
                        out=a[gt2:gt2 + 128, dc * 128:(dc + 1) * 128], in_=b[:, 0:128]),
                        after=[oh], tag="s1_sth")
                    s2_ = p.dma("sp", "trt_st", lambda e, a=dstl, b=V("wtb_lo"), gt2=gt2, dc=dc: e.dma_start(
                        out=a[gt2:gt2 + 128, dc * 128:(dc + 1) * 128], in_=b[:, 0:128]),
                        after=[ol], tag="s1_stl")
                    wtbh_slot.read(s1_)
                    wtbl_slot.read(s2_)
                    trt_st += [s1_, s2_]
    a_store = []
    pev_slot = Slot((V("pev_hi"), V("pev_lo"), 0))
    pev_slot2 = Slot((V("pev_hi"), V("pev_lo"), 512))
    psum8 = [Slot(V("ps_big")[:, i * 512:(i + 1) * 512]) for i in range(4)] + \
            [pss[0], pss[1], pst_slots[0], pst_slots[1]]
    wg_slots = [Slot((V("wg_hi"), V("wg_lo"), 0)), Slot((V("wg_hi"), V("wg_lo"), 2048))]
    omg_slot = Slot(V("omg"))
    pevs = [pev_slot, pev_slot2]
    pev_i = [0]
    for oc in range(2):
        ev_prev = []
        mm_last = {}
        for gt in range(8):
            wsl0, wsl1 = wg_slots
            wh0, wl0, b0_ = wsl0.ap
            wh1, wl1, b1_ = wsl1.ap
            l1 = p.dma("sp", "wg", lambda e, a=wh0, b=V("TRT_HI"), gt=gt, b0_=b0_: e.dma_start(
                out=a[:, b0_:b0_ + 1024], in_=b[gt * 128:(gt + 1) * 128, :]),
                after=wsl0.dep_write() + (trt_st if (gt == 0 and oc == 0) else []), tag="s2_l1")
            l2 = p.dma("sp", "wg", lambda e, a=wl0, b=V("TRT_LO"), gt=gt, b0_=b0_: e.dma_start(
                out=a[:, b0_:b0_ + 1024], in_=b[gt * 128:(gt + 1) * 128, :]), tag="s2_l2")
            l3 = p.dma("sp", "wg", lambda e, a=wh1, b=V("TIT_HI"), gt=gt, b1_=b1_: e.dma_start(
                out=a[:, b1_:b1_ + 1024], in_=b[gt * 128:(gt + 1) * 128, :]),
                after=wsl1.dep_write(), tag="s2_l3")
            l4 = p.dma("sp", "wg", lambda e, a=wl1, b=V("TIT_LO"), gt=gt, b1_=b1_: e.dma_start(
                out=a[:, b1_:b1_ + 1024], in_=b[gt * 128:(gt + 1) * 128, :]), tag="s2_l4")
            wsl0.wrote(l1, l2); wsl1.wrote(l3, l4)
            oms = []
            for k, cn in enumerate(("C_COSS_HI", "C_COSS_LO", "C_SINNS_HI", "C_SINNS_LO")):
                o = p.dma("sp", "omg", lambda e, a=V("omg"), b=CIN[cn], gt=gt, oc=oc, k=k: e.dma_start(
                    out=a[:, k * 512:(k + 1) * 512],
                    in_=b[gt * 128:(gt + 1) * 128, oc * 512:(oc + 1) * 512]),
                    after=(omg_slot.dep_write() if k == 0 else []), tag="s2_om")
                oms.append(o)
            omg_slot.wrote(*oms)
            OM = V("omg")
            for dc in range(8):
                ps = psum8[dc]
                dep = [l1, l2, l3, l4] + oms + ((ps.dep_write() + ev_prev) if gt == 0 else [])
                m1 = mm3(ps.ap[:, :], wh0[:, b0_ + dc * 128:b0_ + (dc + 1) * 128],
                         wl0[:, b0_ + dc * 128:b0_ + (dc + 1) * 128],
                         OM[:, 0:512], OM[:, 512:1024], gt == 0, False, after=dep, tag="s2_r")
                m2 = mm3(ps.ap[:, :], wh1[:, b1_ + dc * 128:b1_ + (dc + 1) * 128],
                         wl1[:, b1_ + dc * 128:b1_ + (dc + 1) * 128],
                         OM[:, 1024:1536], OM[:, 1536:2048], False, gt == 7, tag="s2_i")
                wsl0.read(*m1); wsl1.read(*m2)
                omg_slot.read(*(m1 + m2))
                mm_last[dc] = m2[-1]
        ev_prev = []
        for dc in range(8):
            ps = psum8[dc]
            pe = pevs[pev_i[0]]; pev_i[0] ^= 1
            ph_, pl_, pbase = pe.ap
            oh, ol = split_evac(ps.ap[:, :], ph_[:, pbase:pbase + 512], pl_[:, pbase:pbase + 512],
                                1.0, [mm_last[dc]] + pe.dep_write(), tag="s2_sp")
            ps.wrote(oh); ps.r = [oh, ol]; ps.w = []
            s1 = p.dma("sp", "a_st", lambda e, a=V("A_HI"), b=ph_, dc=dc, oc=oc, pbase=pbase: e.dma_start(
                out=a[dc * 128:(dc + 1) * 128, oc * 512:(oc + 1) * 512], in_=b[:, pbase:pbase + 512]),
                after=[oh], tag="a_sth")
            s2 = p.dma("sp", "a_st", lambda e, a=V("A_LO"), b=pl_, dc=dc, oc=oc, pbase=pbase: e.dma_start(
                out=a[dc * 128:(dc + 1) * 128, oc * 512:(oc + 1) * 512], in_=b[:, pbase:pbase + 512]),
                after=[ol], tag="a_stl")
            pe.w = []; pe.r = [s1, s2]
            a_store += [s1, s2]
            ev_prev += [oh, ol]
    H["a_done"] = a_store
    xtg_slots = [Slot(V("xt_g")[:, 0:4 * D]), Slot(V("xt_g")[:, 4 * D:8 * D])]
    xnt_slots = [Slot(V("xn_t")[:, 0:D]), Slot(V("xn_t")[:, D:2 * D])]
    xnt_slots[0].r += list(t32_slot.r) + list(t32_slot.w)
    psum_rot = psum8[:6]
    xnh_slot = Slot(V("xnh"))
    xev_slot = Slot(V("xc_b"))

    state = {"wg_i": 0, "pev_i": 0, "ps_i": 0, "xnh_ops": []}

    def ln_group(tg, srcs, want_xnh, extra_dep=(), tag="ln"):
        gsl = xtg_slots[tg % 2]
        XG = gsl.ap
        lds = []
        dep0 = gsl.dep_write() + list(extra_dep)
        for k in range(4):
            tt = tg * 4 + k
            o1 = p.dma("sp", "xin", lambda e, a=XG, b=srcs[0], k=k, tt=tt: e.dma_start(
                out=a[:, k * D:(k + 1) * D], in_=b[tt * 128:(tt + 1) * 128, :]),
                after=dep0, tag=tag + "_ldx")
            lds.append(o1)
            if len(srcs) > 1:
                o2 = p.dma("sp", "xin2", lambda e, a=V("xc_a"), b=srcs[1], k=k, tt=tt: e.dma_start(
                    out=a[:, (k % 2) * D:((k % 2) + 1) * D], in_=b[tt * 128:(tt + 1) * 128, :]),
                    after=[x for x in state.get("xca_r_%d" % (k % 2), [])] + list(extra_dep),
                    tag=tag + "_ldy")
                o3 = p.op("dve", lambda d, a=XG, b=V("xc_a"), k=k: d.tensor_tensor(
                    out=a[:, k * D:(k + 1) * D], in0=a[:, k * D:(k + 1) * D],
                    in1=b[:, (k % 2) * D:((k % 2) + 1) * D], op=ALU.add),
                    after=[o1, o2], tag=tag + "_addy")
                state["xca_r_%d" % (k % 2)] = [o3]
                lds.append(o3)
        split_ops = []
        for k in range(4):
            tt = tg * 4 + k
            xsl = xnt_slots[k % 2]
            xin_ap = XG[:, k * D:(k + 1) * D]
            dk = [lds[k * (len(srcs))] if len(srcs) == 1 else lds[2 * k + 1]]
            LS = V("lnsc")
            s1 = p.op("dve", lambda d, a=V("st6"), b=xin_ap: d.bn_stats(a[:, 0:6], b[:, 0:512]),
                      after=dk + [(x) for x in state.get("st_r", [])], tag=tag + "_bs1")
            s2 = p.op("dve", lambda d, a=V("st6"), b=xin_ap: d.bn_stats(a[:, 6:12], b[:, 512:1024]),
                      after=dk, tag=tag + "_bs2")
            s3 = p.op("dve", lambda d, a=V("st2"), b=V("st6"): d.bn_aggr(a[:, :], b[:, 0:12]),
                      after=[s1, s2], tag=tag + "_ba")
            s4 = p.op("dve", lambda d, a=LS, b=V("st2"): d.tensor_scalar(
                out=a[:, 0:1], in0=b[:, 1:2], scalar1=LN_EPS, scalar2=None, op0=ALU.add),
                after=[s3], tag=tag + "_veps")
            s5 = p.op("dve", lambda d, a=LS: d.reciprocal(a[:, 1:2], a[:, 0:1]), after=[s4], tag=tag + "_rc")
            s6 = p.op("act", lambda s, a=LS: s.activation(a[:, 2:3], a[:, 1:2], AF.Ln), after=[s5], tag=tag + "_ln")
            s7 = p.op("act", lambda s, a=LS: s.activation(a[:, 3:4], a[:, 2:3], AF.Exp, bias=0.0, scale=0.5),
                      after=[s6], tag=tag + "_ex")
            s8 = p.op("dve", lambda d, a=LS: d.tensor_tensor(out=a[:, 4:5], in0=a[:, 3:4], in1=a[:, 3:4],
                                                             op=ALU.mult), after=[s7], tag=tag + "_n1")
            s9 = p.op("dve", lambda d, a=LS: d.tensor_tensor(out=a[:, 5:6], in0=a[:, 4:5], in1=a[:, 0:1],
                                                             op=ALU.mult), after=[s8, s4], tag=tag + "_n2")
            s10 = p.op("dve", lambda d, a=LS: d.tensor_scalar(
                out=a[:, 6:7], in0=a[:, 5:6], scalar1=-0.5, scalar2=1.5, op0=ALU.mult, op1=ALU.add),
                after=[s9], tag=tag + "_n3")
            s11 = p.op("dve", lambda d, a=LS: d.tensor_tensor(out=a[:, 7:8], in0=a[:, 3:4], in1=a[:, 6:7],
                                                              op=ALU.mult), after=[s10], tag=tag + "_n4")
            s12 = p.op("dve", lambda d, a=LS, b=V("st2"): d.tensor_tensor(out=a[:, 8:9], in0=b[:, 0:2][:, 0:1],
                                                                          in1=a[:, 7:8], op=ALU.mult),
                       after=[s11, s3], tag=tag + "_mr")
            s13 = p.op("dve", lambda d, a=LS: d.tensor_scalar(out=a[:, 9:10], in0=a[:, 8:9], scalar1=-1.0,
                                                              scalar2=None, op0=ALU.mult), after=[s12], tag=tag + "_nm")
            s14 = p.op("act", lambda s, a=xsl.ap, b=xin_ap, c=LS: s.activation(
                a[:, :], b[:, :], AF.Identity, bias=c[:, 9:10], scale=c[:, 7:8]),
                after=[s13, s11] + xsl.dep_write(), tag=tag + "_nrm")
            state["st_r"] = [s3, s12, s14]
            xsl.wrote(s14)
            gsl.read(s14)
            for dt in range(8):
                sl = pst_slots[dt % 2]
                tr = p.op("te", lambda t, ps=sl.ap, a=xsl.ap, i=V("c_id"), dt=dt: t.transpose(
                    ps[:, 0:128], a[:, dt * 128:(dt + 1) * 128], i[:, :]),
                    after=[s14] + sl.dep_write(), tag=tag + "_tr")
                sl.wrote(tr)
                xsl.read(tr)
                co = dt * 512 + k * 128
                oh, ol = split_evac(sl.ap[:, 0:128], V("xnT_hi")[:, co:co + 128],
                                    V("xnT_lo")[:, co:co + 128], 1.0, [tr], tag=tag + "_sp")
                sl.read(oh, ol)
                split_ops += [oh, ol]
                if want_xnh:
                    tt_g = tg * 4 + k
                    rd = p.op("dve", lambda d, a=V("xnh"), b=sl.ap, dt=dt, tt_g=tt_g: d.tensor_reduce(
                        out=a[:, dt * 128 + tt_g * 4:dt * 128 + tt_g * 4 + 4],
                        in_=b[:, 0:128].rearrange("p (s k) -> p s k", k=32),
                        axis=mybir.AxisListType.X, op=ALU.add),
                        after=[tr] + (xnh_slot.dep_write() if (tt_g == 0 and dt == 0) else []),
                        tag=tag + "_xnh")
                    sl.read(rd)
                    state["xnh_ops"].append(rd)
        return split_ops, gsl

    def stream_wgran(src_hi, src_lo, oc, width, dep, tag):
        i = state["wg_i"]; state["wg_i"] ^= 1
        sl = wg_slots[i]
        wh, wl, base = sl.ap
        d1 = p.dma("sp", "wg", lambda e, a=wh, b=src_hi, oc=oc, base=base, width=width: e.dma_start(
            out=a[:, base:base + 8 * width].rearrange("p (dt o) -> p dt o", o=width),
            in_=b.rearrange("(dt q) o -> q dt o", q=128)[:, :, oc * width:(oc + 1) * width]),
            after=sl.dep_write() + list(dep), tag=tag + "_wh")
        d2 = p.dma("sp", "wg", lambda e, a=wl, b=src_lo, oc=oc, base=base, width=width: e.dma_start(
            out=a[:, base:base + 8 * width].rearrange("p (dt o) -> p dt o", o=width),
            in_=b.rearrange("(dt q) o -> q dt o", q=128)[:, :, oc * width:(oc + 1) * width]),
            after=[], tag=tag + "_wl")
        sl.wrote(d1, d2)
        return sl, (wh, wl, base), [d1, d2]

    def next_psum():
        i = state["ps_i"]; state["ps_i"] = (state["ps_i"] + 1) % len(psum_rot)
        return psum_rot[i]

    def next_pev():
        i = state["pev_i"]; state["pev_i"] ^= 1
        return pevs[i]

    def proj_fmajor(tg, split_ops, src_hi, src_lo, brow, scale, dst_hi, dst_lo, ocs, st_sem, tag, wdep=(), oc_base=0):
        tglob = tg * 512
        stores = []
        for oc in ocs:
            wsl, (wh, wl, base), wld = stream_wgran(src_hi, src_lo, oc, 128, wdep, tag + "_wg%d" % oc)
            ps = next_psum()
            last = None
            for dt in range(8):
                dep = wld + split_ops + (ps.dep_write() if dt == 0 else [])
                ms = mm3(ps.ap[:, :], wh[:, base + dt * 128:base + (dt + 1) * 128],
                         wl[:, base + dt * 128:base + (dt + 1) * 128],
                         V("xnT_hi")[:, dt * 512:(dt + 1) * 512],
                         V("xnT_lo")[:, dt * 512:(dt + 1) * 512], dt == 0,
                         (dt == 7 and brow is None),
                         after=dep, tag=tag + "_mm")
                wsl.read(*ms)
                last = ms[-1]
            if brow is not None:
                bm = p.op("te", lambda t, ps=ps.ap, b=brow, o2=V("c_ones2"), oc=oc: t.matmul(
                    ps[:, :], b[:, oc * 128:(oc + 1) * 128], o2[:, :], start=False, stop=True),
                    after=H["bias_rows"], tag=tag + "_bias")
                last = bm
            pe = next_pev()
            ph_, pl_, pbase = pe.ap
            oh, ol = split_evac(ps.ap[:, :], ph_[:, pbase:pbase + 512], pl_[:, pbase:pbase + 512],
                                scale, [last] + pe.dep_write(), tag=tag + "_ev")
            ps.wrote(oh); ps.r = [oh, ol]; ps.w = []
            od = oc - oc_base
            s1 = p.dma("sp", st_sem, lambda e, a=dst_hi, b=ph_, od=od, tglob=tglob, pbase=pbase: e.dma_start(
                out=a[od * 128:(od + 1) * 128, tglob:tglob + 512], in_=b[:, pbase:pbase + 512]),
                after=[oh], tag=tag + "_sth")
            s2 = p.dma("sp", st_sem, lambda e, a=dst_lo, b=pl_, od=od, tglob=tglob, pbase=pbase: e.dma_start(
                out=a[od * 128:(od + 1) * 128, tglob:tglob + 512], in_=b[:, pbase:pbase + 512]),
                after=[ol], tag=tag + "_stl")
            pe.w = []; pe.r = [s1, s2]
            stores += [s1, s2]
        return stores

    def proj_tmajor(tg, split_ops, src_hi, src_lo, o_off, brow, dst_hi, dst_lo, st_sem, tag,
                    n_oc=4, width=256, wdep=()):
        stores = []
        for oc in range(n_oc):
            oco = (o_off // width) + oc
            wsl, (wh, wl, base), wld = stream_wgran(src_hi, src_lo, oco, width, wdep, tag + "_wg%d" % oc)
            for k in range(4):
                tt = tg * 4 + k
                ps = next_psum()
                last = None
                for dt in range(8):
                    dep = wld + split_ops + (ps.dep_write() if dt == 0 else [])
                    ms = mm3(ps.ap[:, 0:width], V("xnT_hi")[:, dt * 512 + k * 128:dt * 512 + (k + 1) * 128],
                             V("xnT_lo")[:, dt * 512 + k * 128:dt * 512 + (k + 1) * 128],
                             wh[:, base + dt * width:base + (dt + 1) * width],
                             wl[:, base + dt * width:base + (dt + 1) * width], dt == 0,
                             (dt == 7 and brow is None),
                             after=dep, tag=tag + "_mm")
                    wsl.read(*ms)
                    last = ms[-1]
                if brow is not None:
                    bm = p.op("te", lambda t, ps=ps.ap, b=brow, o2=V("c_ones2"), oco=oco, width=width: t.matmul(
                        ps[:, 0:width], o2[:, 0:128], b[:, oco * width:(oco + 1) * width],
                        start=False, stop=True), after=H["bias_rows"], tag=tag + "_bias")
                    last = bm
                pe = next_pev()
                ph_, pl_, pbase = pe.ap
                oh, ol = split_evac(ps.ap[:, 0:width], ph_[:, pbase:pbase + width],
                                    pl_[:, pbase:pbase + width], 1.0,
                                    [last] + pe.dep_write(), tag=tag + "_ev")
                ps.wrote(oh); ps.r = [oh, ol]; ps.w = []
                s1 = p.dma("sp", st_sem, lambda e, a=dst_hi, b=ph_, tt=tt, odc=oc, pbase=pbase, width=width: e.dma_start(
                    out=a[tt * 128:(tt + 1) * 128, odc * width:(odc + 1) * width], in_=b[:, pbase:pbase + width]),
                    after=[oh], tag=tag + "_sth")
                s2 = p.dma("sp", st_sem, lambda e, a=dst_lo, b=pl_, tt=tt, odc=oc, pbase=pbase, width=width: e.dma_start(
                    out=a[tt * 128:(tt + 1) * 128, odc * width:(odc + 1) * width], in_=b[:, pbase:pbase + width]),
                    after=[ol], tag=tag + "_stl")
                pe.w = []; pe.r = [s1, s2]
                stores += [s1, s2]
        return stores

    H["qa_st"], H["ka_st"], H["va_st"] = [], [], []
    state["xnh_ops"] = []
    for tg in range(8):
        split_ops, gsl = ln_group(tg, [V("XIN")], True, tag="ln1")
        H["qa_st"] += proj_fmajor(tg, split_ops, V("WT_Q_HI"), V("WT_Q_LO"), V("brow_q"), 0.125,
                                  V("QA_HI"), V("QA_LO"), range(8), "qa_st", "pq",
                                  wdep=H["wprep_store"][id(V("WT_Q_HI"))])
        H["ka_st"] += proj_fmajor(tg, split_ops, V("WT_K_HI"), V("WT_K_LO"), V("brow_k"), 1.0,
                                  V("KA_HI"), V("KA_LO"), range(8), "ka_st", "pk",
                                  wdep=H["wprep_store"][id(V("WT_K_HI"))])
        H["va_st"] += proj_tmajor(tg, split_ops, V("WT_V_HI"), V("WT_V_LO"), 0, V("brow_v"),
                                  V("VA_HI"), V("VA_LO"), "va_st", "pv",
                                  wdep=H["wprep_store"][id(V("WT_V_HI"))])
    H["xnh1"] = list(state["xnh_ops"])

    def ph4(xnh_dep, wq_hi, wq_lo, wk_hi, wk_lo, wv_hi, wv_lo, bq, bk, bv, call_dst,
            full=True, tag="p4", k_ocbase=0, crow_ld=None, q_slot=None, k_slot=None,
            vh_slot=None):
        oh = p.op("act", lambda s, a=V("xnh_hi"), b=V("xnh"): s.activation(
            a[:, :], b[:, :], AF.Identity, bias=0.0, scale=1.0 / 32.0), after=xnh_dep, tag=tag + "_xh")
        ol = p.op("dve", lambda d, a=V("xnh_lo"), b=V("xnh"), c=V("xnh_hi"): d.scalar_tensor_tensor(
            out=a[:, :], in0=b[:, :], scalar=1.0 / 32.0, in1=c[:, :], op0=ALU.mult, op1=ALU.subtract),
            after=xnh_dep + [oh], tag=tag + "_xl")
        xnh_sp = [oh, ol]
        qk_ev = []
        for (srch, srcl, brw, dst, scale, wdep, ocb) in (
                (wq_hi, wq_lo, bq, V("qht"), 0.125, H["wprep_store"][id(wq_hi)], 0),
                (wk_hi, wk_lo, bk, V("kht"), 1.0, H["wprep_store"][id(wk_hi)], k_ocbase)):
            for ocr in range(8):
                oc = ocb + ocr
                wsl, (wh, wl, base), wld = stream_wgran(srch, srcl, oc, 128, wdep, tag + "_qkw")
                ps = next_psum()
                last = None
                for dt in range(8):
                    dep = wld + xnh_sp + (ps.dep_write() if dt == 0 else [])
                    ms = mm3(ps.ap[:, 0:128], wh[:, base + dt * 128:base + (dt + 1) * 128],
                             wl[:, base + dt * 128:base + (dt + 1) * 128],
                             V("xnh_hi")[:, dt * 128:(dt + 1) * 128],
                             V("xnh_lo")[:, dt * 128:(dt + 1) * 128], dt == 0, brw is None and dt == 7,
                             after=dep, tag=tag + "_qkmm")
                    wsl.read(*ms)
                    last = ms[-1]
                if brw is not None:
                    last = p.op("te", lambda t, ps=ps.ap, b=brw, o2=V("c_ones2"), oc=oc: t.matmul(
                        ps[:, 0:128], b[:, oc * 128:(oc + 1) * 128], o2[:, 0:128], start=False, stop=True),
                        after=H["bias_rows"], tag=tag + "_qkb")
                osl_ = q_slot if dst is V("qht") else k_slot
                ev = p.op("act", lambda s, a=dst, b=ps.ap, ocr=ocr, sc=scale: s.activation(
                    a[:, ocr * 128:(ocr + 1) * 128], b[:, 0:128], AF.Identity, bias=0.0, scale=sc),
                    after=[last] + (osl_.dep_write() if (osl_ is not None and ocr == 0) else []),
                    tag=tag + "_qkev")
                if osl_ is not None:
                    if ocr == 0:
                        osl_.wrote(ev)
                    else:
                        osl_.w.append(ev)
                ps.wrote(ev); ps.r = [ev]; ps.w = []
                qk_ev.append(ev)
        vh_ev = []
        for oc in (range(4) if full else []):
            wsl, (wh, wl, base), wld = stream_wgran(wv_hi, wv_lo, oc, 256,
                                                    H["wprep_store"][id(wv_hi)], tag + "_vw")
            ps = next_psum()
            last = None
            for dt in range(8):
                dep = wld + xnh_sp + (ps.dep_write() if dt == 0 else [])
                ms = mm3(ps.ap[:, 0:256], V("xnh_hi")[:, dt * 128:(dt + 1) * 128],
                         V("xnh_lo")[:, dt * 128:(dt + 1) * 128],
                         wh[:, base + dt * 256:base + (dt + 1) * 256],
                         wl[:, base + dt * 256:base + (dt + 1) * 256], dt == 0, bv is None and dt == 7,
                         after=dep, tag=tag + "_vmm")
                wsl.read(*ms)
                last = ms[-1]
            if bv is not None:
                last = p.op("te", lambda t, ps=ps.ap, b=bv, o2=V("c_ones2"), oc=oc: t.matmul(
                    ps[:, 0:256], o2[:, 0:128], b[:, oc * 256:(oc + 1) * 256], start=False, stop=True),
                    after=H["bias_rows"], tag=tag + "_vb")
            ev = p.op("act", lambda s, a=V("vhs"), b=ps.ap, oc=oc: s.activation(
                a[:, oc * 256:(oc + 1) * 256], b[:, 0:256], AF.Identity, bias=0.0, scale=1.0),
                after=[last] + (vh_slot.dep_write() if (vh_slot is not None and oc == 0) else []),
                tag=tag + "_vev")
            if vh_slot is not None:
                if oc == 0:
                    vh_slot.wrote(ev)
                else:
                    vh_slot.w.append(ev)
            ps.wrote(ev); ps.r = [ev]; ps.w = []
            vh_ev.append(ev)
        norm_ops = []
        call_ops = {}
        low_cp = {}
        for bh in range(BH):
            b, h = bh // NH, bh % NH
            pr = 64 * (h % 2)
            co = (h // 2) * 128 + b * 64
            ps = next_psum()
            lm = p.op("te", lambda t, ps=ps.ap, q=V("qht"), k=V("kht"), pr=pr, co=co: t.matmul(
                ps[0:64, 0:64], q[pr:pr + 64, co:co + 64], k[pr:pr + 64, co:co + 64],
                start=True, stop=True), after=qk_ev + ps.dep_write(), tag=tag + "_low")
            if q_slot is not None:
                q_slot.read(lm)
            if k_slot is not None:
                k_slot.read(lm)
            rm = p.op("dve", lambda d, a=V("rmcol"), b=ps.ap, bh=bh: d.reduce_max(
                a[0:64, bh:bh + 1], b[0:64, 0:64], axis=mybir.AxisListType.X), after=[lm], tag=tag + "_rm")
            cp = p.op("dve", lambda d, a=V("lowall"), b=ps.ap, bh=bh: d.tensor_copy(
                a[:, bh * 64:(bh + 1) * 64], b[0:64, 0:64]), after=[lm], tag=tag + "_lcp")
            low_cp[bh] = cp
            ps4 = next_psum()
            art = p.op("te", lambda t, ps=ps4.ap, b=V("rmcol"), i=V("c_id"), bh=bh: t.transpose(
                ps[0:1, 0:64], b[0:64, bh:bh + 1], i[0:64, 0:64]),
                after=[rm] + ps4.dep_write() + [x for x in call_ops.get(bh - 1, [])], tag=tag + "_art")
            ar = p.op("dve", lambda d, a=V("sc_b"), ps=ps4.ap: d.reduce_max(
                a[0:1, 0:1], ps[0:1, 0:64], axis=mybir.AxisListType.X), after=[art], tag=tag + "_ar")
            ps4.wrote(ar); ps4.r = [ar]; ps4.w = []
            ng = p.op("dve", lambda d, a=V("sc_b"): d.tensor_scalar(
                out=a[0:1, 1:2], in0=a[0:1, 0:1], scalar1=-1.0, scalar2=-CMARGIN,
                op0=ALU.mult, op1=ALU.add), after=[ar], tag=tag + "_ng")
            ps2 = next_psum()
            bc = p.op("te", lambda t, ps=ps2.ap, o1=V("c_ones1f"), b=V("sc_b"): t.matmul(
                ps[:, 0:1], o1[:, :], b[0:1, 1:2], start=True, stop=True),
                after=[ng, const_ops["c_ones1f"]] + ps2.dep_write(), tag=tag + "_bc")
            cc = p.op("dve", lambda d, a=call_dst, b=ps2.ap, bh=bh: d.tensor_copy(
                a[:, bh:bh + 1], b[:, 0:1]), after=[bc], tag=tag + "_cc")
            ps2.wrote(cc); ps2.r = [cc]; ps2.w = []
            call_ops[bh] = [cc, ng]
            if crow_ld is not None:
                ng2 = p.op("dve", lambda d, a=V("sc_b"), c=V("crow_t"), bh=bh: d.tensor_tensor(
                    out=a[0:1, 2:3], in0=a[0:1, 1:2], in1=c[0:1, bh:bh + 1], op=ALU.add),
                    after=[ng, crow_ld], tag=tag + "_ng2")
                ps3b = next_psum()
                bc2 = p.op("te", lambda t, ps=ps3b.ap, o1=V("c_ones1f"), b=V("sc_b"): t.matmul(
                    ps[:, 0:1], o1[:, :], b[0:1, 2:3], start=True, stop=True),
                    after=[ng2] + ps3b.dep_write(), tag=tag + "_bc2")
                cc2 = p.op("dve", lambda d, a=V("call_bc"), b=ps3b.ap, bh=bh: d.tensor_copy(
                    a[:, bh:bh + 1], b[:, 0:1]), after=[bc2], tag=tag + "_cc2")
                ps3b.wrote(cc2); ps3b.r = [cc2]; ps3b.w = []
                call_ops[bh] = [cc, ng, cc2]
            if full:
                n1 = p.op("dve", lambda d, a=V("normall"), b=ps.ap, r=V("rmcol"), bh=bh: d.tensor_scalar(
                    out=a[:, bh * 64:(bh + 1) * 64], in0=b[0:64, 0:64], scalar1=r[0:64, bh:bh + 1],
                    scalar2=None, op0=ALU.subtract), after=[lm, rm], tag=tag + "_nrm")
                n2 = p.op("dve", lambda d, a=V("normall"), c=V("c_band"), bh=bh: d.tensor_tensor(
                    out=a[:, bh * 64:(bh + 1) * 64], in0=a[:, bh * 64:(bh + 1) * 64], in1=c[:, :],
                    op=ALU.add), after=[n1, const_ops["c_band"]], tag=tag + "_bnd")
                norm_ops.append(n2)
                ps.wrote(n2); ps.r = [rm, cp, n1]; ps.w = []
            else:
                ps.wrote(cp); ps.r = [rm, cp]; ps.w = []
        return norm_ops, call_ops, low_cp, vh_ev

    qth_slot = Slot(V("qt_hi")); qtl_slot = Slot(V("qt_lo"))
    kth_slot = Slot(V("kt_hi")); ktl_slot = Slot(V("kt_lo"))
    vsh_slot = Slot(V("vsb_hi")); vsl_slot = Slot(V("vsb_lo"))
    p32_slots = [Slot(V("p32")[:, 0:512]), Slot(V("p32")[:, 512:1024])]
    pth_slot = Slot(V("ptil_hi")); ptl_slot = Slot(V("ptil_lo"))
    m8_slot = Slot(V("m8"))
    xbm_slot = Slot(V("xbm"))
    oev_slot = Slot(V("oev"))
    mb64_slot = Slot(V("mb64"))
    yast_slots = [Slot(V("yast")[:, 0:64]), Slot(V("yast")[:, 64:128])]
    obig = psum_rot[0:4]
    norm_ops, call_ops_a, low_cp, vh_ev = ph4(H["xnh1"], V("WT_Q_HI"), V("WT_Q_LO"), V("WT_K_HI"),
                                       V("WT_K_LO"), V("WT_V_HI"), V("WT_V_LO"),
                                       V("brow_q"), V("brow_k"), V("brow_v"), V("call_a"),
                                       full=True, tag="p4a", q_slot=kth_slot, k_slot=ktl_slot,
                                       vh_slot=xnt_slots[1])
    nf = p.dma("sp", "normf", lambda e, a=V("NORMF"), b=V("normall"): e.dma_start(
        out=a.rearrange("(bh i j) -> i bh j", i=64, j=64),
        in_=b[:, :].rearrange("i (bh j) -> i bh j", j=64)), after=norm_ops, tag="p4_nf")
    bf = p.dma("sp", "normf", lambda e, a=V("bisf"), b=V("NORMF"): e.dma_start(
        out=a[:, :], in_=b.rearrange("(q f) -> q f", f=1024)), after=[nf], tag="p4_bf")
    BC = V("bcols")
    i_lo = p.op("dve", lambda d, a=BC: d.memset(a[:, 0:1], -200.0), tag="bi_lo")
    i_hi = p.op("dve", lambda d, a=BC: d.memset(a[:, 1:2], 1.0), tag="bi_hi")
    prev = [i_lo, i_hi, bf]
    bisj_slot = Slot(V("bisj"))
    for it in range(41):
        m1 = p.op("dve", lambda d, a=BC: d.tensor_tensor(out=a[:, 2:3], in0=a[:, 0:1], in1=a[:, 1:2],
                                                         op=ALU.add), after=prev, tag="bi_m1")
        m2 = p.op("dve", lambda d, a=BC: d.tensor_scalar(out=a[:, 2:3], in0=a[:, 2:3], scalar1=0.5,
                                                         scalar2=None, op0=ALU.mult), after=[m1], tag="bi_m2")
        ct = p.op("dve", lambda d, a=V("bisj"), b=V("bisf"), c=BC: d.tensor_scalar(
            out=a[:, :], in0=b[:, :], scalar1=c[:, 2:3], scalar2=0.0, op0=ALU.is_ge, op1=ALU.add,
            accum_out=c[:, 3:4]), after=[m2, bf] + bisj_slot.dep_write(), tag="bi_ct")
        bisj_slot.wrote(ct)
        psm = next_psum()
        c4 = p.op("te", lambda t, ps=psm.ap, p4=V("c_p4"), c=BC: t.matmul(
            ps[:, 0:1], p4[:, :], c[:, 3:4], start=True, stop=True),
            after=[ct, const_ops["c_p4"]] + psm.dep_write(), tag="bi_c4")
        dr = p.op("dve", lambda d, a=BC, b=psm.ap: d.tensor_scalar(
            out=a[:, 4:5], in0=b[:, 0:1], scalar1=float(NUM_BLOCK), scalar2=None, op0=ALU.is_ge),
            after=[c4], tag="bi_dr")
        psm.wrote(dr); psm.r = [dr]; psm.w = []
        u1 = p.op("dve", lambda d, a=BC: d.tensor_tensor(out=a[:, 5:6], in0=a[:, 2:3], in1=a[:, 0:1],
                                                         op=ALU.subtract), after=[dr], tag="bi_u1")
        u2 = p.op("dve", lambda d, a=BC: d.tensor_tensor(out=a[:, 6:7], in0=a[:, 5:6], in1=a[:, 4:5],
                                                         op=ALU.mult), after=[u1], tag="bi_u2")
        u3 = p.op("dve", lambda d, a=BC: d.tensor_tensor(out=a[:, 0:1], in0=a[:, 0:1], in1=a[:, 6:7],
                                                         op=ALU.add), after=[u2], tag="bi_u3")
        u4 = p.op("dve", lambda d, a=BC: d.tensor_tensor(out=a[:, 5:6], in0=a[:, 1:2], in1=a[:, 2:3],
                                                         op=ALU.subtract), after=[u3], tag="bi_u4")
        u5 = p.op("dve", lambda d, a=BC: d.tensor_tensor(out=a[:, 6:7], in0=a[:, 5:6], in1=a[:, 4:5],
                                                         op=ALU.mult), after=[u4], tag="bi_u5")
        u6 = p.op("dve", lambda d, a=BC: d.tensor_tensor(out=a[:, 1:2], in0=a[:, 2:3], in1=a[:, 6:7],
                                                         op=ALU.add), after=[u5], tag="bi_u6")
        prev = [u6]
    thr_op = prev[0]
    mk1 = p.op("dve", lambda d, a=V("m01"), b=V("bisf"), c=BC: d.tensor_scalar(
        out=a[:, :], in0=b[:, :], scalar1=c[:, 0:1], scalar2=None, op0=ALU.is_ge),
        after=[thr_op], tag="p4_m01")
    mk2 = p.op("dve", lambda d, a=V("mbf"), b=V("m01"): d.tensor_scalar(
        out=a[:, :], in0=b[:, :], scalar1=BIG, scalar2=-BIG, op0=ALU.mult, op1=ALU.add),
        after=[mk1], tag="p4_mbf")
    mk3 = p.op("dve", lambda d, a=V("z32f_s"), b=V("m01"): d.tensor_scalar(
        out=a[:, :], in0=b[:, :], scalar1=-32.0, scalar2=32.0, op0=ALU.mult, op1=ALU.add),
        after=[mk1], tag="p4_z32")
    mf = p.dma("sp", "maskf", lambda e, a=V("MASKF"), b=V("mbf"): e.dma_start(
        out=a.rearrange("(q f) -> q f", f=1024), in_=b[:, :]), after=[mk2], tag="p4_mf")
    zf = p.dma("sp", "maskf", lambda e, a=V("Z32F"), b=V("z32f_s"): e.dma_start(
        out=a.rearrange("(q f) -> q f", f=1024), in_=b[:, :]), after=[mk3], tag="p4_zf")
    H["maskf"] = mf
    m8_slot.w = list(norm_ops); m8_slot.r = [nf]
    pth_slot.w = [low_cp[x] for x in low_cp]; pth_slot.r = []
    ptl_slot.w = [low_cp[x] for x in low_cp]; ptl_slot.r = []
    qth_slot.w = [mk2]; qth_slot.r = [mf]
    qtl_slot.w = [mk3]; qtl_slot.r = [zf]
    oev_slot.w = [mk1]; oev_slot.r = [mk2, mk3]
    vha_ones = p.op("pool", lambda gp, a=V("vha"): gp.memset(a[:, 64:65], 1.0), tag="p4_vo")
    lot_ops = {}
    ret_ops = {}
    z64_slot = Slot(V("z64"))
    sca_slot = Slot(V("sc_a"))
    lat_slot = Slot(V("lat"))
    vha_slot = Slot(V("vha"))
    scd_slot = Slot(V("sc_d"))
    for bh in range(BH):
        b, h = bh // NH, bh % NH
        zl = p.dma("sp", "z64", lambda e, a=V("z64"), b=V("Z32F"), bh=bh: e.dma_start(
            out=a[:, :], in_=b.rearrange("(bh i j) -> bh i j", i=64, j=64)[bh]),
            after=[zf] + z64_slot.dep_write(), tag="p5_zl")
        z64_slot.wrote(zl)
        le = p.op("act", lambda s, a=V("sc_a"), b=V("lowall"), c=V("call_a"), bh=bh: s.activation(
            a[:, :], b[:, bh * 64:(bh + 1) * 64], AF.Exp, bias=c[0:64, bh:bh + 1], scale=1.0),
            after=[low_cp[bh]] + call_ops_a[bh] + sca_slot.dep_write(), tag="p5_le")
        pth_slot.read(le); ptl_slot.read(le)
        la = p.op("dve", lambda d, a=V("sc_a"), b=V("z64"): d.tensor_tensor(
            out=a[:, :], in0=a[:, :], in1=b[:, :], op=ALU.mult), after=[le, zl], tag="p5_la")
        sca_slot.wrote(le, la)
        z64_slot.read(la)
        ps = next_psum()
        tr = p.op("te", lambda t, ps=ps.ap, a=V("sc_a"), i=V("c_id"): t.transpose(
            ps[0:64, 0:64], a[:, :], i[0:64, 0:64]), after=[la] + ps.dep_write(), tag="p5_lt")
        sca_slot.read(tr)
        lc = p.op("dve", lambda d, a=V("lat"), b=ps.ap: d.tensor_copy(a[:, :], b[0:64, 0:64]),
                  after=[tr] + lat_slot.dep_write(), tag="p5_ltc")
        lat_slot.wrote(lc)
        ps.wrote(lc); ps.r = [lc]; ps.w = []
        vc = p.op("dve", lambda d, a=V("vha"), b=V("vhs"), b_=b, h=h: d.tensor_copy(
            a[:, 0:64], b[b_ * 64:b_ * 64 + 64, h * 64:(h + 1) * 64]),
            after=vh_ev + vha_slot.dep_write(), tag="p5_vc")
        xnt_slots[1].read(vc)
        vha_slot.wrote(vc)
        ps2 = next_psum()
        lo_mm = p.op("te", lambda t, ps=ps2.ap, a=V("vha"), b=V("lat"): t.matmul(
            ps[0:65, 0:64], a[:, :], b[:, :], start=True, stop=True),
            after=[vc, lc, vha_ones] + ps2.dep_write(), tag="p5_lmm")
        vha_slot.read(lo_mm)
        lat_slot.read(lo_mm)
        lotc = p.op("dve", lambda d, a=V("lot"), b=ps2.ap, bh=bh: d.tensor_copy(
            a[0:65, bh * 64:(bh + 1) * 64], b[0:65, 0:64]), after=[lo_mm], tag="p5_lot")
        ps2.wrote(lotc); ps2.r = [lotc]; ps2.w = []
        lot_ops[bh] = lotc
        re1 = p.op("act", lambda s, a=V("sc_b"), b=V("rmcol"), c=V("call_a"), bh=bh: s.activation(
            a[0:64, 2:3], b[0:64, bh:bh + 1], AF.Exp, bias=c[0:64, bh:bh + 1], scale=1.0),
            after=call_ops_a[bh] + [x for x in ret_ops.get(bh - 1, [])], tag="p5_re1")
        re2 = p.op("dve", lambda d, a=V("sc_d"), b=V("c_sel"), c=V("sc_b"): d.tensor_tensor(
            out=a[:, :], in0=b[:, :], in1=c[0:64, 2:3].broadcast_to([64, 1, 16]), op=ALU.mult),
            after=[re1, const_ops["c_sel"]] + scd_slot.dep_write(), tag="p5_re2")
        scd_slot.wrote(re2)
        ps3 = next_psum()
        re3 = p.op("te", lambda t, ps=ps3.ap, a=V("c_e4"), b=V("sc_d"): t.matmul(
            ps[:, 0:16], a[:, :], b[:, :], start=True, stop=True),
            after=[re2, const_ops["c_e4"]] + ps3.dep_write(), tag="p5_re3")
        scd_slot.read(re3)
        re4 = p.op("dve", lambda d, a=V("ret"), b=ps3.ap, bh=bh: d.tensor_copy(
            a[:, bh * 16:(bh + 1) * 16], b[:, 0:16]), after=[re3], tag="p5_re4")
        ps3.wrote(re4); ps3.r = [re4]; ps3.w = []
        ret_ops[bh] = [re1, re4]
    H["lot"] = lot_ops
    H["ret"] = ret_ops

    def attn_main(QHI, QLO, KHI, KLO, VHI, VLO, q_st, k_st, v_st, call_src, call_deps, tag):
        ya_st = {0: [], 1: []}
        su1 = p.op("dve", lambda d, a=V("qt_hi"), c=V("c_oneh"): d.tensor_copy(
            a[64:128, :], c[:, :]), after=[const_ops["c_oneh"]] + qth_slot.dep_write(),
            tag=tag + "_su1")
        su2 = p.op("dve", lambda d, a=V("qt_lo"): d.memset(a[64:128, :], 0.0),
                   after=qtl_slot.dep_write(), tag=tag + "_su2")
        su3 = p.op("dve", lambda d, a=V("kt_lo"): d.memset(a[64:128, :], 0.0),
                   after=ktl_slot.dep_write(), tag=tag + "_su3")
        su4 = p.op("dve", lambda d, a=V("vsb_hi"): d.memset(
            a[:, :].rearrange("p (k c) -> p k c", c=65)[:, :, 64:65], 1.0),
            after=vsh_slot.dep_write(), tag=tag + "_su4")
        su5 = p.op("dve", lambda d, a=V("vsb_lo"): d.memset(
            a[:, :].rearrange("p (k c) -> p k c", c=65)[:, :, 64:65], 0.0),
            after=vsl_slot.dep_write(), tag=tag + "_su5")
        qth_slot.wrote(su1); qtl_slot.wrote(su2); ktl_slot.wrote(su3)
        vsh_slot.wrote(su4); vsl_slot.wrote(su5)
        setup_ops = [su1, su2, su3, su4, su5]
        for bh in range(BH):
            b, h = bh // NH, bh % NH
            dq1 = p.dma("sp", "qtl", lambda e, a=V("qt_hi"), s=QHI, b=b, h=h: e.dma_start(
                out=a[0:64, :], in_=s[h * 64:(h + 1) * 64, b * SEQ:(b + 1) * SEQ]),
                after=q_st + qth_slot.dep_write(), tag=tag + "_dq")
            dq2 = p.dma("sp", "qtl", lambda e, a=V("qt_lo"), s=QLO, b=b, h=h: e.dma_start(
                out=a[0:64, :], in_=s[h * 64:(h + 1) * 64, b * SEQ:(b + 1) * SEQ]),
                after=qtl_slot.dep_write(), tag=tag + "_dq2")
            dk1 = p.dma("sp", "ktl", lambda e, a=V("kt_hi"), s=KHI, b=b, h=h: e.dma_start(
                out=a[0:64, :], in_=s[h * 64:(h + 1) * 64, b * SEQ:(b + 1) * SEQ]),
                after=k_st + kth_slot.dep_write(), tag=tag + "_dk")
            dk2 = p.dma("sp", "ktl", lambda e, a=V("kt_lo"), s=KLO, b=b, h=h: e.dma_start(
                out=a[0:64, :], in_=s[h * 64:(h + 1) * 64, b * SEQ:(b + 1) * SEQ]),
                after=ktl_slot.dep_write(), tag=tag + "_dk2")
            dv1 = p.dma("sp", "vtl", lambda e, a=V("vsb_hi"), s=VHI, b=b, h=h: e.dma_start(
                out=a[:, :].rearrange("p (k c) -> p k c", c=65)[:, :, 0:64],
                in_=s.rearrange("(bb kt q) o -> bb q kt o", bb=2, q=128)[b][:, :, h * 64:(h + 1) * 64]),
                after=v_st + vsh_slot.dep_write(), tag=tag + "_dv")
            dv2 = p.dma("sp", "vtl", lambda e, a=V("vsb_lo"), s=VLO, b=b, h=h: e.dma_start(
                out=a[:, :].rearrange("p (k c) -> p k c", c=65)[:, :, 0:64],
                in_=s.rearrange("(bb kt q) o -> bb q kt o", bb=2, q=128)[b][:, :, h * 64:(h + 1) * 64]),
                after=vsl_slot.dep_write(), tag=tag + "_dv2")
            qth_slot.wrote(dq1); qtl_slot.wrote(dq2)
            kth_slot.wrote(dk1); ktl_slot.wrote(dk2)
            vsh_slot.wrote(dv1); vsl_slot.wrote(dv2)
            md = p.dma("sp", "mb", lambda e, a=V("mb64"), s=V("MASKF"), bh=bh: e.dma_start(
                out=a[:, :], in_=s.rearrange("(x i j) -> x i j", i=64, j=64)[bh]),
                after=[H["maskf"]] + mb64_slot.dep_write(), tag=tag + "_mbl")
            mb64_slot.wrote(md)
            km = p.op("dve", lambda d, a=V("kt_hi"), b=V("mb64"): d.tensor_copy(
                a[64:128, :], b[:, :].broadcast_to([64, 64, 32])),
                after=[md] + kth_slot.dep_write(), tag=tag + "_kmx")
            kth_slot.wrote(dk1, km)
            mb64_slot.read(km)
            kdep = [dk1, dk2, km] + (setup_ops if bh == 0 else [])
            mz = p.op("dve", lambda d, a=V("m8"): d.memset(a[:, :], 0.0),
                      after=m8_slot.dep_write(), tag=tag + "_m8z")
            m8_slot.wrote(mz)
            pv_prev = []
            mx_ops = []
            for kt in range(NKT):
                cast_ops, lo_ops = [], []
                for qc in range(4):
                    sps = pss[qc % 2]
                    dep = [dq1, dq2] + kdep + sps.dep_write()
                    ms = mm3(sps.ap[:, :], V("kt_hi")[:, kt * 128:(kt + 1) * 128],
                             V("kt_lo")[:, kt * 128:(kt + 1) * 128],
                             V("qt_hi")[:, qc * 512:(qc + 1) * 512],
                             V("qt_lo")[:, qc * 512:(qc + 1) * 512], True, True,
                             after=dep, tag=tag + "_smm")
                    qth_slot.read(*ms); qtl_slot.read(ms[1])
                    kth_slot.read(*ms); ktl_slot.read(ms[2])
                    psl = p32_slots[qc % 2]
                    ex = p.op("act", lambda s, a=psl.ap, b=sps.ap, c=call_src, bh=bh: s.activation(
                        a[:, :], b[:, :], AF.Exp, bias=c[:, bh:bh + 1], scale=1.0),
                        after=[ms[-1]] + psl.dep_write() + call_deps[bh], tag=tag + "_exp")
                    psl.wrote(ex)
                    sps.wrote(ex); sps.r = [ex]; sps.w = []
                    ch = p.op("pool", lambda gp, a=V("ptil_hi"), b=psl.ap, qc=qc: gp.tensor_copy(
                        a[:, qc * 512:(qc + 1) * 512], b[:, :]),
                        after=([ex] + pth_slot.dep_write()) if (qc == 0 and kt == 0) else ([ex] + pv_prev),
                        tag=tag + "_cst")
                    lo = p.op("dve", lambda d, a=V("ptil_lo"), b=psl.ap, c=V("ptil_hi"), qc=qc: d.scalar_tensor_tensor(
                        out=a[:, qc * 512:(qc + 1) * 512], in0=b[:, :], scalar=1.0,
                        in1=c[:, qc * 512:(qc + 1) * 512], op0=ALU.mult, op1=ALU.subtract),
                        after=[ex, ch] + (ptl_slot.dep_write() if (qc == 0 and kt == 0) else []),
                        tag=tag + "_plo")
                    mx = p.op("dve", lambda d, a=V("m8"), b=psl.ap, qc=qc: d.tensor_tensor(
                        out=a[:, qc * 512:(qc + 1) * 512], in0=a[:, qc * 512:(qc + 1) * 512],
                        in1=b[:, :], op=ALU.max), after=[ex, mz, lo], tag=tag + "_max")
                    psl.read(ch, lo, mx)
                    m8_slot.read(mx)
                    mx_ops.append(mx)
                    cast_ops.append(ch); lo_ops.append(lo)
                pv_ops = []
                for pi, (lp, rp) in enumerate(((V("vsb_hi"), V("ptil_hi")), (V("vsb_hi"), V("ptil_lo")),
                                               (V("vsb_lo"), V("ptil_hi")))):
                    for qc in range(4):
                        st = (kt == 0 and pi == 0)
                        sp = (kt == NKT - 1 and pi == 2)
                        dep = cast_ops + lo_ops + [dv1, dv2] + (obig[qc].dep_write() if st else [])
                        o = p.op("te", lambda t, ps=obig[qc].ap, lp=lp, rp=rp, kt=kt, qc=qc, st=st, sp=sp: t.matmul(
                            ps[0:65, :], lp[:, kt * 65:kt * 65 + 65], rp[:, qc * 512:(qc + 1) * 512],
                            start=st, stop=sp), after=dep, tag=tag + "_pv")
                        pv_ops.append(o)
                        vsh_slot.read(o); vsl_slot.read(o)
                        pth_slot.read(o); ptl_slot.read(o)
                pv_prev = pv_ops
                if kt == 0:
                    pth_slot.wrote(*cast_ops); ptl_slot.wrote(*lo_ops)
            xw = []
            for j in range(16):
                sl = pst_slots[j % 2]
                tr = p.op("te", lambda t, ps=sl.ap, a=V("m8"), i=V("c_id"), j=j: t.transpose(
                    ps[:, 0:128], a[:, j * 128:(j + 1) * 128], i[:, :]),
                    after=mx_ops[-4:] + sl.dep_write(), tag=tag + "_xtr")
                m8_slot.read(tr)
                sl.wrote(tr)
                rd = p.op("dve", lambda d, a=V("xbm"), b=sl.ap, j=j: d.reduce_max(
                    a[:, j:j + 1], b[:, 0:128], axis=mybir.AxisListType.X),
                    after=[tr] + (xbm_slot.dep_write() if j == 0 else []), tag=tag + "_xrd")
                sl.read(rd)
                xw.append(rd)
            xbm_slot.wrote(*xw)
            x2 = p.op("dve", lambda d, a=V("xbm"), b=V("ret"), bh=bh: d.tensor_tensor(
                out=a[:, :], in0=a[:, :], in1=b[:, bh * 16:(bh + 1) * 16], op=ALU.max),
                after=xw + [H["ret"][bh][1]], tag=tag + "_x2")
            xbm_slot.wrote(x2)
            oe = p.op("dve", lambda d, a=V("oev"), b=V("ps_big"), c=V("lot"), bh=bh: d.tensor_tensor(
                out=a[0:65, :], in0=b[0:65, :], in1=c[0:65, bh * 64:(bh + 1) * 64].broadcast_to([65, 64, 32]),
                op=ALU.add), after=pv_prev + [H["lot"][bh]] + oev_slot.dep_write(), tag=tag + "_oev")
            oev_slot.wrote(oe)
            for qcb in obig:
                qcb.wrote(oe); qcb.r = [oe]; qcb.w = []
            if bh == 0:
                d1_ = p.dma("pool", "dbg", lambda e, a=V("DBG_OEV"), b=V("oev"): e.dma_start(
                    out=a[:, :], in_=b[:, :]), after=[oe], tag="dbg_oev")
                oev_slot.read(d1_)
                d2_ = p.dma("pool", "dbg", lambda e, a=V("DBG_XBM"), b=V("xbm"): e.dma_start(
                    out=a[:, 0:16], in_=b[:, :]), after=[x2], tag="dbg_xbm")
                xbm_slot.read(d2_)
                d3_ = p.dma("pool", "dbg", lambda e, a=V("DBG_M8"), b=V("m8"): e.dma_start(
                    out=a[:, :], in_=b[:, :]), after=mx_ops[-4:], tag="dbg_m8")
                m8_slot.read(d3_)
                d4_ = p.dma("pool", "dbg", lambda e, a=V("DBG_PT"), b=V("ptil_hi"): e.dma_start(
                    out=a[:, 0:1024], in_=b[:, :].bitcast(F32)), after=pv_prev, tag="dbg_pt")
                pth_slot.read(d4_)
                d5_ = p.dma("pool", "dbg", lambda e, a=V("DBG_CALL"), b=V("call_a"): e.dma_start(
                    out=a[:, :], in_=b[:, :]), after=call_deps[0], tag="dbg_call")
                d6_ = p.dma("pool", "dbg", lambda e, a=V("DBG_RET"), b=V("ret"): e.dma_start(
                    out=a[:, :], in_=b[:, :]), after=[H["ret"][x][1] for x in range(BH)], tag="dbg_ret")
                d7_ = p.dma("pool", "dbg", lambda e, a=V("DBG_LOT"), b=V("lot"): e.dma_start(
                    out=a[:, :], in_=b[:, :]), after=[H["lot"][x] for x in range(BH)], tag="dbg_lot")
            for qt in range(16):
                sl = pst_slots[qt % 2]
                tr = p.op("te", lambda t, ps=sl.ap, a=V("oev"), i=V("c_id"), qt=qt: t.transpose(
                    ps[:, 0:65], a[0:65, qt * 128:(qt + 1) * 128], i[0:65, 0:65]),
                    after=[oe] + sl.dep_write(), tag=tag + "_otr")
                oev_slot.read(tr)
                sl.wrote(tr)
                ysl = yast_slots[qt % 2]
                SC = V("sc_c")
                d1 = p.op("dve", lambda d, a=SC, b=sl.ap, x=V("xbm"), qt=qt: d.tensor_scalar(
                    out=a[:, 0:1], in0=x[:, qt:qt + 1], scalar1=1e-6, scalar2=b[:, 64:65],
                    op0=ALU.mult, op1=ALU.add), after=[tr, x2] + [q for q in state.get("scc_r", [])],
                    tag=tag + "_den")
                d2 = p.op("dve", lambda d, a=SC: d.reciprocal(a[:, 1:2], a[:, 0:1]), after=[d1], tag=tag + "_rcp")
                d3 = p.op("dve", lambda d, a=ysl.ap, b=sl.ap, c=SC: d.tensor_scalar(
                    out=a[:, :], in0=b[:, 0:64], scalar1=c[:, 1:2], scalar2=None, op0=ALU.mult),
                    after=[d2] + ysl.dep_write(), tag=tag + "_mul")
                state["scc_r"] = [d2, d3]
                ysl.wrote(d3)
                sl.read(d1, d3)
                xbm_slot.read(d1)
                tt = b * 16 + qt
                st = p.dma("sp", "ya_st", lambda e, a=V("YA2"), b_=ysl.ap, tt=tt, h=h: e.dma_start(
                    out=a[tt * 128:(tt + 1) * 128, h * 64:(h + 1) * 64], in_=b_[:, :]),
                    after=[d3], tag=tag + "_yst")
                ya_st[b].append(st)
                ysl.read(st)
        return ya_st

    ya_st = attn_main(V("QA_HI"), V("QA_LO"), V("KA_HI"), V("KA_LO"), V("VA_HI"), V("VA_LO"),
                      H["qa_st"], H["ka_st"], H["va_st"], V("call_a"), call_ops_a, "a5")
    H["ya_st"] = ya_st

    def ffn_phase(srcs, dst, per_tg_dep, tag):
        stores = []
        for tg in range(8):
            split_ops, gsl = ln_group(tg, srcs, False, extra_dep=per_tg_dep(tg), tag=tag + "_ln")
            XG = gsl.ap
            for oc in range(4):
                wsl, (wh, wl, base), wld = stream_wgran(V("A_HI"), V("A_LO"), oc, 256,
                                                        H["a_done"], tag + "_wg")
                for k in range(4):
                    tt = tg * 4 + k
                    ps = next_psum()
                    last = None
                    for dt in range(8):
                        dep = wld + split_ops + (ps.dep_write() if dt == 0 else [])
                        ms = mm3(ps.ap[:, 0:256], V("xnT_hi")[:, dt * 512 + k * 128:dt * 512 + (k + 1) * 128],
                                 V("xnT_lo")[:, dt * 512 + k * 128:dt * 512 + (k + 1) * 128],
                                 wh[:, base + dt * 256:base + (dt + 1) * 256],
                                 wl[:, base + dt * 256:base + (dt + 1) * 256], dt == 0, False,
                                 after=dep, tag=tag + "_mm")
                        wsl.read(*ms)
                        last = ms[-1]
                    bm = p.op("te", lambda t, ps=ps.ap, b=V("brow_f"), o2=V("c_ones2"), oc=oc: t.matmul(
                        ps[:, 0:256], o2[:, 0:128], b[:, oc * 256:(oc + 1) * 256],
                        start=False, stop=True), after=H["bias_rows"], tag=tag + "_bias")
                    ev = p.op("dve", lambda d, a=xev_slot.ap, b=ps.ap, x=XG, k=k, oc=oc: d.scalar_tensor_tensor(
                        out=a[:, 0:256], in0=b[:, 0:256], scalar=1.0,
                        in1=x[:, k * D + oc * 256:k * D + (oc + 1) * 256], op0=ALU.mult, op1=ALU.add),
                        after=[bm] + xev_slot.dep_write(), tag=tag + "_ev")
                    gsl.read(ev)
                    ps.wrote(ev); ps.r = [ev]; ps.w = []
                    xev_slot.wrote(ev)
                    st = p.dma("sp", "ffn_st", lambda e, a=dst, b=xev_slot.ap, tt=tt, oc=oc: e.dma_start(
                        out=a[tt * 128:(tt + 1) * 128, oc * 256:(oc + 1) * 256], in_=b[:, 0:256]),
                        after=[ev], tag=tag + "_st")
                    xev_slot.read(st)
                    stores.append(st)
        return stores

    xev_slot.r += list(m8_slot.r) + list(m8_slot.w)
    state["xca_r_0"] = list(pth_slot.r) + list(pth_slot.w) + list(ptl_slot.r) + list(ptl_slot.w)
    state["xca_r_1"] = list(state["xca_r_0"])
    H["xs1_st"] = ffn_phase([V("XIN"), V("YA2")], V("XS1"),
                            lambda tg: ya_st[tg // 4], "f6")
    pth_slot.r += state["xca_r_0"]
    ptl_slot.r += state["xca_r_0"]

    H["qb_st"], H["kb_st"], H["vb_st"] = [], [], []
    state["xnh_ops"] = []
    for tg in range(8):
        deps = [s for s in H["xs1_st"]]
        split_ops, gsl = ln_group(tg, [V("XS1")], True, extra_dep=deps, tag="ln3")
        H["qb_st"] += proj_fmajor(tg, split_ops, V("WT_B_HI"), V("WT_B_LO"), None, 0.125,
                                  V("QB_HI"), V("QB_LO"), range(8), "qb_st", "pbq",
                                  wdep=H["wprep_store"][id(V("WT_B_HI"))])
        H["kb_st"] += proj_fmajor(tg, split_ops, V("WT_B_HI"), V("WT_B_LO"), None, 1.0,
                                  V("KB_HI"), V("KB_LO"), range(8, 16), "kb_st", "pbk",
                                  wdep=H["wprep_store"][id(V("WT_B_HI"))], oc_base=8)
        H["vb_st"] += proj_tmajor(tg, split_ops, V("WT_B_HI"), V("WT_B_LO"), 2048, None,
                                  V("VB_HI"), V("VB_LO"), "vb_st", "pbv",
                                  wdep=H["wprep_store"][id(V("WT_B_HI"))])
    H["xnh3"] = list(state["xnh_ops"])

    crow_ld = p.dma("sp", "crow", lambda e, a=V("crow_t"), b=V("RELB"): e.dma_start(
        out=a[0:1, :].rearrange("p (b h) -> p b h", b=2),
        in_=bass.AP(tensor=b.tensor, offset=256 * 16, ap=[[16, 1], [0, 2], [1, 16]])),
        tag="p7_crow")
    _, call_ops_b, _, _ = ph4(H["xnh3"], V("WT_B_HI"), V("WT_B_LO"), V("WT_B_HI"), V("WT_B_LO"),
                           V("WT_V_HI"), V("WT_V_LO"), None, None, None, V("call_b"),
                           full=False, tag="p7b", k_ocbase=8, crow_ld=crow_ld)

    yb_st = []
    b0_slot = Slot(V("b0t"))
    b1_slot = Slot(V("b1t"))
    rbe_slot = Slot(V("rbext"))
    rc_slot = Slot(V("rcs"))
    yt32_slot = Slot(V("yt32"))
    for h in range(NH):
        r1 = p.dma("sp", "rbc", lambda e, a=V("rbext"), b=V("RELB"), h=h: e.dma_start(
            out=a[0:1, 0:257], in_=b[:, h:h + 1].rearrange("r c -> c r"), allow_slow_non_contiguous=True),
            after=rbe_slot.dep_write(), tag="p8_rbc")
        r2 = p.op("dve", lambda d, a=V("rbext"): d.tensor_scalar(
            out=a[0:1, 257:384], in0=a[0:1, 256:257].broadcast_to([1, 1, 127]), scalar1=1.0,
            scalar2=None, op0=ALU.mult), after=[r1], tag="p8_pad")
        rbe_slot.wrote(r1, r2)
        sk0 = p.dma("pool", "skew", lambda e, a=V("b0t"), b=V("rbext"): e.dma_start(
            out=a[:, :], in_=bass.AP(tensor=b.tensor, offset=128, ap=[[512, 1], [-1, 128], [1, 128]])),
            after=[r1] + b0_slot.dep_write(), tag="p8_sk0")
        sk1 = p.dma("pool", "skew", lambda e, a=V("b1t"), b=V("rbext"): e.dma_start(
            out=a[:, :], in_=bass.AP(tensor=b.tensor, offset=256, ap=[[512, 1], [-1, 128], [1, 128]])),
            after=[r2] + b1_slot.dep_write(), tag="p8_sk1")
        sk0b = p.op("dve", lambda d, a=V("b0t"), c=V("c_tri0"): d.tensor_tensor(
            out=a[:, :], in0=a[:, :], in1=c[:, :], op=ALU.add),
            after=[sk0, const_ops["c_tri0"]], tag="p8_sk0b")
        b0_slot.wrote(sk0, sk0b)
        b1_slot.wrote(sk1)
        rbe_slot.read(sk0, sk1)
        for b in range(NSEQ):
            bh = b * NH + h
            dq1 = p.dma("sp", "qtl", lambda e, a=V("qt_hi"), s=V("QB_HI"), b=b, h=h: e.dma_start(
                out=a[0:64, :], in_=s[h * 64:(h + 1) * 64, b * SEQ:(b + 1) * SEQ]),
                after=H["qb_st"] + qth_slot.dep_write(), tag="p8_dq")
            dq2 = p.dma("sp", "qtl", lambda e, a=V("qt_lo"), s=V("QB_LO"), b=b, h=h: e.dma_start(
                out=a[0:64, :], in_=s[h * 64:(h + 1) * 64, b * SEQ:(b + 1) * SEQ]),
                after=qtl_slot.dep_write(), tag="p8_dq2")
            dk1 = p.dma("sp", "ktl", lambda e, a=V("kt_hi"), s=V("KB_HI"), b=b, h=h: e.dma_start(
                out=a[0:64, :], in_=s[h * 64:(h + 1) * 64, b * SEQ:(b + 1) * SEQ]),
                after=H["kb_st"] + kth_slot.dep_write(), tag="p8_dk")
            dk2 = p.dma("sp", "ktl", lambda e, a=V("kt_lo"), s=V("KB_LO"), b=b, h=h: e.dma_start(
                out=a[0:64, :], in_=s[h * 64:(h + 1) * 64, b * SEQ:(b + 1) * SEQ]),
                after=ktl_slot.dep_write(), tag="p8_dk2")
            dv1 = p.dma("sp", "vtl", lambda e, a=V("vsb_hi"), s=V("VB_HI"), b=b, h=h: e.dma_start(
                out=a[:, :].rearrange("p (k c) -> p k c", c=65)[:, :, 0:64],
                in_=s.rearrange("(bb kt q) o -> bb q kt o", bb=2, q=128)[b][:, :, h * 64:(h + 1) * 64]),
                after=H["vb_st"] + vsh_slot.dep_write(), tag="p8_dv")
            dv2 = p.dma("sp", "vtl", lambda e, a=V("vsb_lo"), s=V("VB_LO"), b=b, h=h: e.dma_start(
                out=a[:, :].rearrange("p (k c) -> p k c", c=65)[:, :, 0:64],
                in_=s.rearrange("(bb kt q) o -> bb q kt o", bb=2, q=128)[b][:, :, h * 64:(h + 1) * 64]),
                after=vsl_slot.dep_write(), tag="p8_dv2")
            qth_slot.wrote(dq1); qtl_slot.wrote(dq2)
            kth_slot.wrote(dk1); ktl_slot.wrote(dk2)
            vsh_slot.wrote(dv1); vsl_slot.wrote(dv2)
            bank_started = [False] * 4
            bank_last = {}
            for qc in range(4):
                bank_last[qc] = (min(4 * qc + 3, NKT - 1), 2)
            pv_prev = []
            for kt in range(NKT):
                qlo = kt * 128
                w = min(SEQ - qlo, 640)
                w1 = min(w, 512)
                dep = [dq1, dq2, dk1, dk2] + pss[0].dep_write()
                ms1 = mm3(pss[0].ap[:, 0:w1], V("kt_hi")[0:64, kt * 128:(kt + 1) * 128],
                          V("kt_lo")[0:64, kt * 128:(kt + 1) * 128],
                          V("qt_hi")[0:64, qlo:qlo + w1], V("qt_lo")[0:64, qlo:qlo + w1],
                          True, True, after=dep, tag="p8_smm")
                sc_ops = list(ms1)
                if w > 512:
                    dep2 = pst_slots[1].dep_write()
                    ms2 = mm3(pst_slots[1].ap[:, 0:w - 512], V("kt_hi")[0:64, kt * 128:(kt + 1) * 128],
                              V("kt_lo")[0:64, kt * 128:(kt + 1) * 128],
                              V("qt_hi")[0:64, qlo + 512:qlo + w], V("qt_lo")[0:64, qlo + 512:qlo + w],
                              True, True, after=dep2, tag="p8_smm2")
                    sc_ops += list(ms2)
                ba = []
                o1 = p.op("dve", lambda d, a=pss[0].ap, b=V("b0t"): d.tensor_tensor(
                    out=a[:, 0:128], in0=a[:, 0:128], in1=b[:, :], op=ALU.add),
                    after=[ms1[-1], sk0b], tag="p8_b0")
                ba.append(o1)
                b0_slot.read(o1)
                if w > 128:
                    o2 = p.op("dve", lambda d, a=pss[0].ap, b=V("b1t"): d.tensor_tensor(
                        out=a[:, 128:256], in0=a[:, 128:256], in1=b[:, :], op=ALU.add),
                        after=[ms1[-1], sk1], tag="p8_b1")
                    ba.append(o2)
                    b1_slot.read(o2)
                if w > 512:
                    o3 = p.op("dve", lambda d, a=pst_slots[1].ap, b=V("c_tri4"): d.tensor_tensor(
                        out=a[:, 0:128], in0=a[:, 0:128], in1=b[:, :], op=ALU.add),
                        after=[ms2[-1], const_ops["c_tri4"]], tag="p8_b4")
                    ba.append(o3)
                psl = Slot(V("p32"))
                e1 = p.op("act", lambda s, a=V("p32"), b=pss[0].ap, c=V("call_b"), bh=bh, w1=w1: s.activation(
                    a[:, 0:min(256, w1)], b[:, 0:min(256, w1)], AF.Exp, bias=c[:, bh:bh + 1], scale=1.0),
                    after=ba + sc_ops + call_ops_b[bh] + [x for x in state.get("p32_r", [])], tag="p8_e1")
                ex_ops = [e1]
                if w1 > 256:
                    e2 = p.op("act", lambda s, a=V("p32"), b=pss[0].ap, c=V("call_bc"), bh=bh, w1=w1: s.activation(
                        a[:, 256:w1], b[:, 256:w1], AF.Exp, bias=c[:, bh:bh + 1], scale=1.0),
                        after=ba + sc_ops + call_ops_b[bh], tag="p8_e2")
                    ex_ops.append(e2)
                if w > 512:
                    e3 = p.op("act", lambda s, a=V("p32"), b=pst_slots[1].ap, c=V("call_bc"), bh=bh, w=w: s.activation(
                        a[:, 512:w], b[:, 0:w - 512], AF.Exp, bias=c[:, bh:bh + 1], scale=1.0),
                        after=ba + sc_ops + call_ops_b[bh], tag="p8_e3")
                    ex_ops.append(e3)
                pss[0].wrote(*ex_ops); pss[0].r = list(ex_ops); pss[0].w = []
                if w > 512:
                    pst_slots[1].wrote(*ex_ops); pst_slots[1].r = list(ex_ops); pst_slots[1].w = []
                ch = p.op("pool", lambda gp, a=V("ptil_hi"), b=V("p32"), w=w: gp.tensor_copy(
                    a[:, 0:w], b[:, 0:w]), after=ex_ops + pv_prev, tag="p8_cst")
                lo = p.op("dve", lambda d, a=V("ptil_lo"), b=V("p32"), c=V("ptil_hi"), w=w: d.scalar_tensor_tensor(
                    out=a[:, 0:w], in0=b[:, 0:w], scalar=1.0, in1=c[:, 0:w], op0=ALU.mult, op1=ALU.subtract),
                    after=ex_ops + [ch], tag="p8_plo")
                state["p32_r"] = [ch, lo]
                pv_ops = []
                for pi, (lp, rp) in enumerate(((V("vsb_hi"), V("ptil_hi")), (V("vsb_hi"), V("ptil_lo")),
                                               (V("vsb_lo"), V("ptil_hi")))):
                    for qc in range(qlo // 512, min((qlo + w - 1) // 512 + 1, 4)):
                        lo_q = max(qlo, qc * 512)
                        hi_q = min(qlo + w, (qc + 1) * 512)
                        st = not bank_started[qc]
                        if st and pi == 0:
                            bank_started[qc] = True
                        sp = (bank_last[qc] == (kt, pi))
                        dep = [ch, lo, dv1, dv2] + (obig[qc].dep_write() if (st and pi == 0) else [])
                        o = p.op("te", lambda t, ps=obig[qc].ap, lp=lp, rp=rp, kt=kt, lo_q=lo_q, hi_q=hi_q,
                                 qc=qc, st=(st and pi == 0), sp=sp, qlo=qlo: t.matmul(
                            ps[0:65, lo_q - qc * 512:hi_q - qc * 512], lp[:, kt * 65:kt * 65 + 65],
                            rp[:, lo_q - qlo:hi_q - qlo], start=st, stop=sp),
                            after=dep, tag="p8_pv")
                        pv_ops.append(o)
                        vsh_slot.read(o); vsl_slot.read(o)
                pv_prev = pv_ops
                for qc in range(4):
                    if bank_last[qc][0] == kt:
                        dcp = p.op("dve", lambda d, a=V("rcs"), b=obig[qc].ap: d.tensor_copy(
                            a[0:1, :], b[64:65, :]), after=pv_ops + rc_slot.dep_write(), tag="p8_dcp")
                        rc_slot.wrote(dcp)
                        bsl = pst_slots[0]
                        bc = p.op("te", lambda t, ps=bsl.ap, o1=V("c_ones1f"), b=V("rcs"): t.matmul(
                            ps[0:64, 0:512], o1[0:1, 0:64], b[0:1, :], start=True, stop=True),
                            after=[dcp, const_ops["c_ones1f"]] + bsl.dep_write(), tag="p8_bc")
                        rc_slot.read(bc)
                        bsl.wrote(bc)
                        rcp = p.op("dve", lambda d, a=V("rrb"), b=bsl.ap: d.reciprocal(
                            a[:, :], b[0:64, 0:512]), after=[bc] + [x for x in state.get("rrb_r", [])],
                            tag="p8_rcp")
                        bsl.read(rcp)
                        ym = p.op("dve", lambda d, a=V("yt32"), b=obig[qc].ap, c=V("rrb"): d.tensor_tensor(
                            out=a[:, :], in0=b[0:64, :], in1=c[:, :], op=ALU.mult),
                            after=[rcp] + yt32_slot.dep_write(), tag="p8_ym")
                        state["rrb_r"] = [ym]
                        yt32_slot.wrote(ym)
                        obig[qc].wrote(ym); obig[qc].r = [ym, dcp]; obig[qc].w = []
                        pe = next_pev()
                        ph_, pl_, pbase = pe.ap
                        oh, ol = split_evac(V("yt32")[:, :], ph_[0:64, pbase:pbase + 512],
                                            pl_[0:64, pbase:pbase + 512], 1.0,
                                            [ym] + pe.dep_write(), tag="p8_ysp")
                        yt32_slot.read(oh, ol)
                        s1 = p.dma("sp", "yb_st", lambda e, a=V("YB_HI"), b_=ph_, bh=bh, qc=qc, pbase=pbase: e.dma_start(
                            out=a[bh][:, qc * 512:(qc + 1) * 512], in_=b_[0:64, pbase:pbase + 512]),
                            after=[oh], tag="p8_sth")
                        s2 = p.dma("sp", "yb_st", lambda e, a=V("YB_LO"), b_=pl_, bh=bh, qc=qc, pbase=pbase: e.dma_start(
                            out=a[bh][:, qc * 512:(qc + 1) * 512], in_=b_[0:64, pbase:pbase + 512]),
                            after=[ol], tag="p8_stl")
                        pe.w = []; pe.r = [s1, s2]
                        yb_st += [s1, s2]
    H["yb_st"] = yb_st

    ybc_h_slot = Slot(V("ybc_hi")); ybc_l_slot = Slot(V("ybc_lo"))
    xca_slot = Slot(V("xc_a"))
    xca_slot.r = list(pth_slot.r) + list(pth_slot.w) + list(ptl_slot.r) + list(ptl_slot.w)
    xs2_st = []
    for tt in range(NT):
        b = tt // 16
        tseq = (tt % 16) * 128
        ylds = []
        dep_h = ybc_h_slot.dep_write(); dep_l = ybc_l_slot.dep_write()
        for ht in range(8):
            for half in range(2):
                bh2 = b * NH + 2 * ht + half
                y1 = p.dma("sp", "ybl", lambda e, a=V("ybc_hi"), s=V("YB_HI"), bh2=bh2, ht=ht, half=half,
                           tseq=tseq: e.dma_start(
                    out=a[half * 64:(half + 1) * 64, ht * 128:(ht + 1) * 128],
                    in_=s[bh2][:, tseq:tseq + 128]), after=yb_st + dep_h, tag="p9_yl")
                y2 = p.dma("sp", "ybl", lambda e, a=V("ybc_lo"), s=V("YB_LO"), bh2=bh2, ht=ht, half=half,
                           tseq=tseq: e.dma_start(
                    out=a[half * 64:(half + 1) * 64, ht * 128:(ht + 1) * 128],
                    in_=s[bh2][:, tseq:tseq + 128]), after=dep_l, tag="p9_yl2")
                ylds += [y1, y2]
        ybc_h_slot.wrote(*[o for i, o in enumerate(ylds) if i % 2 == 0])
        ybc_l_slot.wrote(*[o for i, o in enumerate(ylds) if i % 2 == 1])
        xl = p.dma("sp", "x3l", lambda e, a=V("xc_a"), s=V("XS1"), tt=tt: e.dma_start(
            out=a[:, 0:D], in_=s[tt * 128:(tt + 1) * 128, :]),
            after=H["xs1_st"] + xca_slot.dep_write(), tag="p9_xl")
        xca_slot.wrote(xl)
        for dcq in range(4):
            wsl, (wh, wl, base), wld = stream_wgran(V("WT_P_HI"), V("WT_P_LO"), dcq, 256,
                                                    H["wprep_store"][id(V("WT_P_HI"))], "p9_wg")
            ps = next_psum()
            last = None
            for ht in range(8):
                dep = wld + ylds + (ps.dep_write() if ht == 0 else [])
                ms = mm3(ps.ap[:, 0:256], V("ybc_hi")[:, ht * 128:(ht + 1) * 128],
                         V("ybc_lo")[:, ht * 128:(ht + 1) * 128],
                         wh[:, base + ht * 256:base + (ht + 1) * 256],
                         wl[:, base + ht * 256:base + (ht + 1) * 256], ht == 0, False,
                         after=dep, tag="p9_mm")
                wsl.read(*ms)
                ybc_h_slot.read(ms[0], ms[1]); ybc_l_slot.read(ms[2])
                last = ms[-1]
            bm = p.op("te", lambda t, ps=ps.ap, b=V("brow_p"), o2=V("c_ones2"), dcq=dcq: t.matmul(
                ps[:, 0:256], o2[:, 0:128], b[:, dcq * 256:(dcq + 1) * 256], start=False, stop=True),
                after=H["bias_rows"], tag="p9_bias")
            ev = p.op("dve", lambda d, a=xev_slot.ap, b=ps.ap, x=V("xc_a"), dcq=dcq: d.scalar_tensor_tensor(
                out=a[:, 0:256], in0=b[:, 0:256], scalar=1.0, in1=x[:, dcq * 256:(dcq + 1) * 256],
                op0=ALU.mult, op1=ALU.add), after=[bm, xl] + xev_slot.dep_write(), tag="p9_ev")
            xca_slot.read(ev)
            ps.wrote(ev); ps.r = [ev]; ps.w = []
            xev_slot.wrote(ev)
            st = p.dma("sp", "xs2_st", lambda e, a=V("XS2"), b=xev_slot.ap, tt=tt, dcq=dcq: e.dma_start(
                out=a[tt * 128:(tt + 1) * 128, dcq * 256:(dcq + 1) * 256], in_=b[:, 0:256]),
                after=[ev], tag="p9_st")
            xev_slot.read(st)
            xs2_st.append(st)
    H["xs2_st"] = xs2_st

    out_st = ffn_phase([V("XS2")], V("OUT"), lambda tg: xs2_st, "f10")

    p.op("sp", lambda e: e.nop(), after=out_st, tag="final_wait")
    p.finalize()
    return nc


_NC_CACHE = {}


def _get_nc(debug_taps=()):
    key = tuple(sorted(debug_taps))
    if key not in _NC_CACHE:
        _NC_CACHE[key] = build_program(debug_taps=key)
    return _NC_CACHE[key]


def kernel(**inputs):
    def f32(x):
        return np.ascontiguousarray(np.asarray(x), dtype=None) if False else np.ascontiguousarray(np.asarray(x, dtype=np.float32))

    X = f32(inputs["L0_attn_a_X"])
    base = {
        "WQ": f32(inputs["L0_attn_a_Wq"]), "WK": f32(inputs["L0_attn_a_Wk"]),
        "WV": f32(inputs["L0_attn_a_Wv"]), "BQ": f32(inputs["L0_attn_a_bq"]),
        "BK": f32(inputs["L0_attn_a_bk"]), "BV": f32(inputs["L0_attn_a_bv"]),
        "WQKV": f32(inputs["L0_attn_b_w_qkv"]), "WPROJ": f32(inputs["L0_attn_b_w_proj"]),
        "BPROJ": f32(inputs["L0_attn_b_b_proj"]), "RELB": f32(inputs["L0_attn_b_rel_bias"]),
        "WFR": f32(inputs["L0_ffn_w_freq_real"]), "WFI": f32(inputs["L0_ffn_w_freq_imag"]),
        "BFFN": f32(inputs["L0_ffn_bias"]),
    }
    for k, v in CONSTS.items():
        base[k] = v
    nc = _get_nc(_DEBUG_TAPS)
    in_maps = []
    for c in range(8):
        m = dict(base)
        m["XIN"] = np.ascontiguousarray(X[2 * c:2 * c + 2].reshape(T, D))
        in_maps.append(m)
    res = run_bass_kernel_spmd(nc, in_maps, core_ids=list(range(8)))
    if _DEBUG_TAPS:
        global _DEBUG_RESULTS
        _DEBUG_RESULTS = res.results
    out = np.concatenate([np.asarray(r["OUT"]).reshape(NSEQ, SEQ, D) for r in res.results], axis=0)
    return out.astype(np.float32)


_DEBUG_TAPS = ()
_DEBUG_RESULTS = None


# version 06
# speedup vs baseline: 1.1179x; 1.1179x over previous
import math
import numpy as np
import ml_dtypes
import concourse.bass as bass
import concourse.mybir as mybir
from bass_rust import ReduceOp
from concourse.bass_utils import run_bass_kernel_spmd

F32 = mybir.dt.float32
BF16 = mybir.dt.bfloat16
AF = mybir.ActivationFunctionType
ALU = mybir.AluOpType

SEQ = 2048
NSEQ = 2
T = SEQ * NSEQ
D = 1024
NT = T // 128
HD = 64
NH = 16
BH = NSEQ * NH
NKT = SEQ // 128
NBLK = 64
BLK = 32
NUM_BLOCK = 1024
BIG = float(2.0 ** 30)
CMARGIN = 30.0
LN_EPS = 1e-5
WINDOW = 512
MAXREL = 128


def _split_bf16(x):
    x = np.asarray(x, np.float32)
    hi = x.astype(ml_dtypes.bfloat16)
    lo = (x - hi.astype(np.float32)).astype(ml_dtypes.bfloat16)
    return hi, lo


def host_consts():
    c = {}
    idx = np.arange(D, dtype=np.float64)
    ang = 2.0 * np.pi * np.outer(idx, idx) / D
    cos = np.cos(ang).astype(np.float32)
    sin = np.sin(ang).astype(np.float32)
    c["C_COS_HI"], c["C_COS_LO"] = _split_bf16(cos)
    c["C_SIN_HI"], c["C_SIN_LO"] = _split_bf16(sin)
    c["C_SINN_HI"], c["C_SINN_LO"] = _split_bf16(-sin)
    coss = (cos.astype(np.float64) / D).astype(np.float32)
    sinns = (-sin.astype(np.float64) / D).astype(np.float32)
    c["C_COSS_HI"], c["C_COSS_LO"] = _split_bf16(coss)
    c["C_SINNS_HI"], c["C_SINNS_LO"] = _split_bf16(sinns)
    ii = np.arange(NBLK)
    c["C_BAND"] = (5000.0 * (np.abs(ii[:, None] - ii[None, :]) <= 1)).astype(np.float32)
    t = np.arange(SEQ)
    c["C_ONEHOT"] = ((t[None, :] // BLK) == ii[:, None]).astype(ml_dtypes.bfloat16)
    c["C_ID128"] = np.eye(128, dtype=np.float32)
    c["C_ID128B"] = np.eye(128, dtype=ml_dtypes.bfloat16)
    q = np.arange(128)
    c["C_P4"] = ((q[:, None] // 4) == (q[None, :] // 4)).astype(np.float32)
    i64 = np.arange(NBLK)
    p128 = np.arange(128)
    c["C_E4"] = ((i64[:, None] % 4) == (p128[None, :] // 32)).astype(np.float32)
    c["C_SEL"] = ((i64[:, None] // 4) == np.arange(16)[None, :]).astype(np.float32)
    kk = np.arange(128)
    c["C_TRI4"] = np.where(kk[None, :] > kk[:, None], -BIG, 0.0).astype(np.float32)
    c["C_TRI0"] = np.where(kk[None, :] < kk[:, None], -BIG, 0.0).astype(np.float32)
    c["C_ONES2"] = np.ones((2, 512), dtype=ml_dtypes.bfloat16)
    c["C_ONES1"] = np.ones((1, 128), dtype=ml_dtypes.bfloat16)
    return c


CONSTS = host_consts()


ENGINES = ("te", "act", "dve", "pool", "sp")


class _Op:
    __slots__ = ("eng", "fn", "waits", "need_inc", "inc_idx", "dma_sem", "dma_val", "tag")

    def __init__(self, eng, fn, waits, dma_sem=None, tag=""):
        self.eng = eng
        self.fn = fn
        self.waits = list(waits)
        self.need_inc = False
        self.inc_idx = None
        self.dma_sem = dma_sem
        self.dma_val = None
        self.tag = tag


class Prog:
    def __init__(self, nc):
        self.nc = nc
        self.ops = {e: [] for e in ENGINES}
        self.all_ops = []
        self.dma_sems = {}
        self.eng_sem = {}
        self._sem_guards = []

    def _get_dma_sem(self, name):
        if name not in self.dma_sems:
            g = self.nc.semaphore("dsem_" + name)
            h = g.__enter__()
            self._sem_guards.append(g)
            self.dma_sems[name] = [h, 0]
        return self.dma_sems[name]

    def op(self, eng, fn, after=(), tag=""):
        o = _Op(eng, fn, after, tag=tag)
        self.ops[eng].append(o)
        self.all_ops.append(o)
        return o

    def dma(self, eng, sem_name, fn, after=(), tag=""):
        sem_name = eng + "_" + sem_name
        ent = self._get_dma_sem(sem_name)
        ent[1] += 16
        o = _Op(eng, fn, after, dma_sem=sem_name, tag=tag)
        o.dma_val = ent[1]
        self.ops[eng].append(o)
        self.all_ops.append(o)
        return o

    def finalize(self):
        nc = self.nc
        for o in self.all_ops:
            for w in o.waits:
                if isinstance(w, _Op) and w.dma_sem is None:
                    w.need_inc = True
        eng_counts = {e: 0 for e in ENGINES}
        for e in ENGINES:
            for o in self.ops[e]:
                if o.dma_sem is None and o.need_inc:
                    eng_counts[e] += 1
                    o.inc_idx = eng_counts[e]
        for e in ENGINES:
            if eng_counts[e] > 0:
                g = self.nc.semaphore("esem_" + e)
                self.eng_sem[e] = g.__enter__()
                self._sem_guards.append(g)

        def emit(e, engine_obj):
            highwater = {}
            for o in self.ops[e]:
                for w in o.waits:
                    if isinstance(w, _Op):
                        if w.dma_sem is not None:
                            sem, val = self.dma_sems[w.dma_sem][0], w.dma_val
                        else:
                            assert w.inc_idx is not None
                            sem, val = self.eng_sem[w.eng], w.inc_idx
                            if w.eng == e and w.inc_idx is not None and o.dma_sem is None:
                                pass
                    else:
                        sem, val = w
                    key = id(sem)
                    if highwater.get(key, -1) < val:
                        engine_obj.wait_ge(sem, val)
                        highwater[key] = val
                inst = o.fn(engine_obj)
                if o.dma_sem is not None:
                    inst.then_inc(self.dma_sems[o.dma_sem][0], 16)
                elif o.need_inc:
                    inst.then_inc(self.eng_sem[e], 1)

        with nc.Block() as block:
            @block.sync
            def _(sync):
                emit("sp", sync)

            @block.tensor
            def _(tensor):
                emit("te", tensor)

            @block.scalar
            def _(scalar):
                emit("act", scalar)

            @block.vector
            def _(vector):
                emit("dve", vector)

            @block.gpsimd
            def _(gpsimd):
                emit("pool", gpsimd)


def build_program(debug_taps=()):
    nc = bass.Bass("TRN2", target_bir_lowering=False)

    def dram(name, shape, dtype, kind="Internal"):
        if kind == "Internal" and name in debug_taps:
            kind = "ExternalOutput"
        return nc.dram_tensor(name, shape, dtype, kind=kind).ap()

    XIN = dram("XIN", [T, D], F32, kind="ExternalInput")
    WQ = dram("WQ", [D, D], F32, kind="ExternalInput")
    WK = dram("WK", [D, D], F32, kind="ExternalInput")
    WV = dram("WV", [D, D], F32, kind="ExternalInput")
    BQ = dram("BQ", [D], F32, kind="ExternalInput")
    BK = dram("BK", [D], F32, kind="ExternalInput")
    BV = dram("BV", [D], F32, kind="ExternalInput")
    WQKV = dram("WQKV", [3 * D, D], F32, kind="ExternalInput")
    WPROJ = dram("WPROJ", [D, D], F32, kind="ExternalInput")
    BPROJ = dram("BPROJ", [D], F32, kind="ExternalInput")
    RELB = dram("RELB", [257, NH], F32, kind="ExternalInput")
    WFR = dram("WFR", [D, D], F32, kind="ExternalInput")
    WFI = dram("WFI", [D, D], F32, kind="ExternalInput")
    BFFN = dram("BFFN", [D], F32, kind="ExternalInput")
    CIN = {k: dram(k, list(v.shape), mybir.dt.from_np(v.dtype) if v.dtype != np.float32 else F32,
                   kind="ExternalInput") for k, v in CONSTS.items()}
    OUT = dram("OUT", [T, D], F32, kind="ExternalOutput")

    WT_Q_HI = dram("WT_Q_HI", [D, D], BF16); WT_Q_LO = dram("WT_Q_LO", [D, D], BF16)
    WT_K_HI = dram("WT_K_HI", [D, D], BF16); WT_K_LO = dram("WT_K_LO", [D, D], BF16)
    WT_V_HI = dram("WT_V_HI", [D, D], BF16); WT_V_LO = dram("WT_V_LO", [D, D], BF16)
    WT_B_HI = dram("WT_B_HI", [D, 3 * D], BF16); WT_B_LO = dram("WT_B_LO", [D, 3 * D], BF16)
    WT_P_HI = dram("WT_P_HI", [D, D], BF16); WT_P_LO = dram("WT_P_LO", [D, D], BF16)
    A_HI = dram("A_HI", [D, D], BF16); A_LO = dram("A_LO", [D, D], BF16)
    TRT_HI = dram("TRT_HI", [D, D], BF16); TRT_LO = dram("TRT_LO", [D, D], BF16)
    TIT_HI = dram("TIT_HI", [D, D], BF16); TIT_LO = dram("TIT_LO", [D, D], BF16)
    QA_HI = dram("QA_HI", [D, T], BF16); QA_LO = dram("QA_LO", [D, T], BF16)
    KA_HI = dram("KA_HI", [D, T], BF16); KA_LO = dram("KA_LO", [D, T], BF16)
    VA_HI = dram("VA_HI", [T, D], BF16); VA_LO = dram("VA_LO", [T, D], BF16)
    QB_HI = dram("QB_HI", [D, T], BF16); QB_LO = dram("QB_LO", [D, T], BF16)
    KB_HI = dram("KB_HI", [D, T], BF16); KB_LO = dram("KB_LO", [D, T], BF16)
    VB_HI = dram("VB_HI", [T, D], BF16); VB_LO = dram("VB_LO", [T, D], BF16)
    YB_HI = dram("YB_HI", [BH, HD, SEQ], BF16); YB_LO = dram("YB_LO", [BH, HD, SEQ], BF16)
    YA2 = dram("YA2", [T, D], F32)
    XS1 = dram("XS1", [T, D], F32)
    XS2 = dram("XS2", [T, D], F32)
    NORMF = dram("NORMF", [BH * 4096], F32)
    DBG_CALL = dram("DBG_CALL", [128, BH], F32)
    DBG_RET = dram("DBG_RET", [128, BH * 16], F32)
    DBG_LOT = dram("DBG_LOT", [128, BH * 64], F32)
    DBG_OEV = dram("DBG_OEV", [128, SEQ], F32)
    DBG_XBM = dram("DBG_XBM", [128, 16], F32)
    DBG_M8 = dram("DBG_M8", [128, SEQ], F32)
    DBG_PT = dram("DBG_PT", [128, SEQ], F32)
    MASKF = dram("MASKF", [BH * 4096], BF16)
    Z32F = dram("Z32F", [BH * 4096], F32)

    sb = nc.alloc_sbuf_tensor
    c_oneh = sb("c_oneh", [64, SEQ], BF16).ap()
    c_id = sb("c_id", [128, 128], F32).ap()
    c_idb = sb("c_idb", [128, 128], BF16).ap()
    c_band = sb("c_band", [64, 64], F32).ap()
    c_p4 = sb("c_p4", [128, 128], F32).ap()
    c_e4 = sb("c_e4", [64, 128], F32).ap()
    c_sel = sb("c_sel", [64, 16], F32).ap()
    c_tri4 = sb("c_tri4", [128, 128], F32).ap()
    c_tri0 = sb("c_tri0", [128, 128], F32).ap()
    c_ones2 = sb("c_ones2", [2, 512], BF16).ap()
    c_ones1 = sb("c_ones1", [1, 128], BF16).ap()
    c_ones1f = sb("c_ones1f", [1, 128], F32).ap()
    xt_g = sb("xt_g", [128, 2 * D], F32).ap()
    xn_t = sb("xn_t", [128, 2 * D], F32).ap()
    t32 = xn_t[:, 0:512]
    vhs = xn_t[:, D:2 * D]
    xnT_hi = sb("xnT_hi", [128, 2 * 8 * 512], BF16).ap()
    xnT_lo = sb("xnT_lo", [128, 2 * 8 * 512], BF16).ap()
    st6 = sb("st6", [128, 12], F32).ap()
    st2 = sb("st2", [128, 2], F32).ap()
    lnsc = sb("lnsc", [128, 16], F32).ap()
    wg_hi = sb("wg_hi", [128, 2 * 8 * 256], BF16).ap()
    wg_lo = sb("wg_lo", [128, 2 * 8 * 256], BF16).ap()
    wstg = sb("wstg", [128, D], F32).ap()
    wtb_hi = sb("wtb_hi", [128, D], BF16).ap()
    wtb_lo = sb("wtb_lo", [128, D], BF16).ap()
    brow_q = sb("brow_q", [2, D], BF16).ap()
    brow_k = sb("brow_k", [2, D], BF16).ap()
    brow_v = sb("brow_v", [2, D], BF16).ap()
    brow_p = sb("brow_p", [2, D], BF16).ap()
    brow_f = sb("brow_f", [2, D], BF16).ap()

    pev_hi = sb("pev_hi", [128, 2 * 512], BF16).ap()
    pev_lo = sb("pev_lo", [128, 2 * 512], BF16).ap()
    wfr_hi = sb("wfr_hi", [128, 8 * 256], BF16).ap()
    wfr_lo = sb("wfr_lo", [128, 8 * 256], BF16).ap()
    wfi_hi = sb("wfi_hi", [128, 8 * 256], BF16).ap()
    wfi_lo = sb("wfi_lo", [128, 8 * 256], BF16).ap()
    omg = sb("omg", [128, 2048], BF16).ap()
    qt_hi = sb("qt_hi", [128, SEQ], BF16).ap()
    qt_lo = sb("qt_lo", [128, SEQ], BF16).ap()
    kt_hi = sb("kt_hi", [128, SEQ], BF16).ap()
    kt_lo = sb("kt_lo", [128, SEQ], BF16).ap()
    vsb_hi = sb("vsb_hi", [128, 16 * 65], BF16).ap()
    vsb_lo = sb("vsb_lo", [128, 16 * 65], BF16).ap()
    p32 = sb("p32", [128, 2 * 512], F32).ap()
    ptil = sb("ptil", [128, 4 * SEQ], BF16).ap()
    ptil_hi = ptil[:, 0:SEQ]
    ptil_lo = ptil[:, SEQ:2 * SEQ]
    ptil_hi2 = ptil[:, 2 * SEQ:3 * SEQ]
    ptil_lo2 = ptil[:, 3 * SEQ:4 * SEQ]
    lowall = ptil.bitcast(F32)[0:64, 0:SEQ]
    m8 = sb("m8", [128, SEQ], F32).ap()
    normall = m8[0:64, :]
    xbm = sb("xbm", [128, 16], F32).ap()
    oev = sb("oev", [128, SEQ], F32).ap()
    oev2 = None
    bisf = None
    mb64 = sb("mb64", [64, 64], BF16).ap()
    z64 = sb("z64", [64, 64], F32).ap()
    lot = sb("lot", [128, BH * 64], F32).ap()
    ret = sb("ret", [128, BH * 16], F32).ap()
    call_a = sb("call_a", [128, BH], F32).ap()
    call_b = sb("call_b", [128, BH], F32).ap()
    call_bc = sb("call_bc", [128, BH], F32).ap()
    qht = None
    kht = None
    xnh = sb("xnh", [128, 8 * 128], F32).ap()
    xnh_hi = sb("xnh_hi", [128, 8 * 128], BF16).ap()
    xnh_lo = sb("xnh_lo", [128, 8 * 128], BF16).ap()
    rmcol = sb("rmcol", [64, BH], F32).ap()
    bcols = sb("bcols", [128, 8], F32).ap()
    vha = sb("vha", [64, 65], F32).ap()
    lat = sb("lat", [64, 64], F32).ap()
    sc_a = sb("sc_a", [64, 64], F32).ap()
    sc_b = sb("sc_b", [64, 16], F32).ap()
    sc_c = sb("sc_c", [128, 65], F32).ap()
    sc_d = sb("sc_d", [64, 16], F32).ap()
    yab = sb("yab", [128, 16 * 64], F32).ap()
    rbext = sb("rbext", [1, 512], F32).ap()
    crow_t = sb("crow_t", [1, 32], F32).ap()
    b0t = sb("b0t", [128, 128], F32).ap()
    b1t = sb("b1t", [128, 128], F32).ap()
    rcs = sb("rcs", [1, 512], F32).ap()
    ybc_hi = sb("ybc_hi", [128, 8 * 128], BF16).ap()
    ybc_lo = sb("ybc_lo", [128, 8 * 128], BF16).ap()
    yt32 = sb("yt32", [64, 512], F32).ap()
    xc_a = ptil.bitcast(F32)
    xc_b = m8

    qht = kt_hi.bitcast(F32)
    kht = kt_lo.bitcast(F32)
    rrb = m8[0:64, 0:512]
    bisf = oev[:, 0:1024]
    m01 = oev[:, 1024:2048]
    bisj = qt_hi[:, 0:1024]
    mbf = qt_hi[:, 1024:2048]
    z32f_s = qt_lo.bitcast(F32)

    ps_big = nc.alloc_psum_tensor("ps_big", [128, 2048], F32).ap()
    ps_s0 = nc.alloc_psum_tensor("ps_s0", [128, 512], F32).ap()
    ps_s1 = nc.alloc_psum_tensor("ps_s1", [128, 512], F32).ap()
    ps_t = nc.alloc_psum_tensor("ps_t", [128, 512], F32).ap()
    ps_m = nc.alloc_psum_tensor("ps_m", [128, 512], F32).ap()

    p = Prog(nc)
    return _build_phases(nc, p, locals())


class Slot:
    __slots__ = ("ap", "w", "r")

    def __init__(self, ap):
        self.ap = ap
        self.w = []
        self.r = []

    def dep_write(self):
        return list(self.r) + list(self.w)

    def wrote(self, *ops):
        self.w = list(ops)
        self.r = []

    def read(self, *ops):
        self.r.extend(ops)


def _build_phases(nc, p, v):
    g = dict(v)

    def V(name):
        return g[name]

    H = {}

    def mm3(ps_ap, lhi, llo, rhi, rlo, start, stop, after=(), tag=""):
        o1 = p.op("te", lambda t, a=ps_ap, b=lhi, c=rhi: t.matmul(a, b, c, start=start, stop=False),
                  after=after, tag=tag + "/hh")
        o2 = p.op("te", lambda t, a=ps_ap, b=lhi, c=rlo: t.matmul(a, b, c, start=False, stop=False),
                  tag=tag + "/hl")
        o3 = p.op("te", lambda t, a=ps_ap, b=llo, c=rhi: t.matmul(a, b, c, start=False, stop=stop),
                  tag=tag + "/lh")
        return [o1, o2, o3]

    def split_evac(ps_ap, hi_ap, lo_ap, scale, after, tag=""):
        oh = p.op("act", lambda s, a=hi_ap, b=ps_ap, sc=scale: s.activation(a, b, AF.Identity, bias=0.0, scale=sc),
                  after=after, tag=tag + "/hi")
        ol = p.op("dve", lambda d, a=lo_ap, b=ps_ap, c=hi_ap, sc=scale: d.scalar_tensor_tensor(
            out=a, in0=b, scalar=sc, in1=c, op0=ALU.mult, op1=ALU.subtract),
            after=after + [oh], tag=tag + "/lo")
        return oh, ol

    const_ops = {}
    for name in ("C_ONEHOT", "C_ID128", "C_ID128B", "C_BAND", "C_P4", "C_E4", "C_SEL",
                 "C_TRI4", "C_TRI0", "C_ONES2", "C_ONES1"):
        dst = {"C_ONEHOT": "c_oneh", "C_ID128": "c_id", "C_ID128B": "c_idb",
               "C_BAND": "c_band", "C_P4": "c_p4", "C_E4": "c_e4", "C_SEL": "c_sel",
               "C_TRI4": "c_tri4", "C_TRI0": "c_tri0", "C_ONES2": "c_ones2",
               "C_ONES1": "c_ones1"}[name]
        const_ops[dst] = p.dma("sp", "consts", lambda e, a=V(dst), b=V("CIN")[name]: e.dma_start(out=a[:, :], in_=b[:, :]),
                               tag="ld_" + dst)
    o = p.op("pool", lambda gp: gp.memset(V("c_ones1f")[:, :], 1.0), tag="ones1f")
    const_ops["c_ones1f"] = o

    bias_srcs = [("BQ", "brow_q"), ("BK", "brow_k"), ("BV", "brow_v"), ("BPROJ", "brow_p"),
                 ("BFFN", "brow_f")]
    wstg_slot = Slot(V("wstg"))
    wtbh_slot = Slot(V("wtb_hi"))
    wtbl_slot = Slot(V("wtb_lo"))
    H["bias_rows"] = []
    for i, (srcn, dst) in enumerate(bias_srcs):
        ld = p.dma("sp", "bias", lambda e, a=V("wstg"), b=V(srcn): e.dma_start(
            out=a[0:1, :], in_=b[:].rearrange("(a b) -> a b", a=1)), after=wstg_slot.dep_write(),
            tag="ldb")
        wstg_slot.wrote(ld)
        oh = p.op("act", lambda s, a=V("wtb_hi"), b=V("wstg"): s.activation(
            a[0:1, :], b[0:1, :], AF.Identity, bias=0.0, scale=1.0),
            after=[ld] + wtbh_slot.dep_write(), tag="bsp_h")
        ol = p.op("dve", lambda d, a=V("wtb_lo"), b=V("wstg"), c=V("wtb_hi"): d.scalar_tensor_tensor(
            out=a[0:1, :], in0=b[0:1, :], scalar=1.0, in1=c[0:1, :],
            op0=ALU.mult, op1=ALU.subtract), after=[ld, oh] + wtbl_slot.dep_write(), tag="bsp_l")
        wstg_slot.read(oh, ol)
        wtbh_slot.wrote(oh)
        wtbl_slot.wrote(ol)
        m1 = p.dma("pool", "bias_m", lambda e, a=V(dst), b=V("wtb_hi"): e.dma_start(
            out=a[0:1, :], in_=b[0:1, :]), after=[oh], tag="bm_h")
        m2 = p.dma("pool", "bias_m", lambda e, a=V(dst), b=V("wtb_lo"): e.dma_start(
            out=a[1:2, :], in_=b[0:1, :]), after=[ol], tag="bm_l")
        wtbh_slot.read(m1)
        wtbl_slot.read(m2)
        H["bias_rows"] += [m1, m2]

    wjobs = [
        (V("WQ"), D, V("WT_Q_HI"), V("WT_Q_LO"), 0),
        (V("WK"), D, V("WT_K_HI"), V("WT_K_LO"), 0),
        (V("WV"), D, V("WT_V_HI"), V("WT_V_LO"), 0),
        (V("WQKV"), 3 * D, V("WT_B_HI"), V("WT_B_LO"), 0),
        (V("WPROJ"), D, V("WT_P_HI"), V("WT_P_LO"), 0),
    ]
    pst_slots = [Slot(V("ps_t")), Slot(V("ps_m"))]
    H["wprep_store"] = {id(V("WT_Q_HI")): [], id(V("WT_K_HI")): [], id(V("WT_V_HI")): [],
                        id(V("WT_B_HI")): [], id(V("WT_P_HI")): []}
    idc = const_ops["c_id"]
    for src, orows, dhi, dlo, _ in wjobs:
        for oc in range(orows // 128):
            ld = p.dma("sp", "wstg", lambda e, a=V("wstg"), b=src, oc=oc: e.dma_start(
                out=a[:, :], in_=b[oc * 128:(oc + 1) * 128, :]), after=wstg_slot.dep_write(), tag="wp_ld")
            wstg_slot.wrote(ld)
            st_h, st_l = [], []
            for dt in range(8):
                sl = pst_slots[dt % 2]
                tr = p.op("te", lambda t, ps=sl.ap, a=V("wstg"), i=V("c_id"), dt=dt: t.transpose(
                    ps[:, 0:128], a[:, dt * 128:(dt + 1) * 128], i[:, :]),
                    after=[ld, idc] + sl.dep_write(), tag="wp_tr")
                sl.wrote(tr)
                oh, ol = split_evac(sl.ap[:, 0:128], V("wtb_hi")[:, dt * 128:(dt + 1) * 128],
                                    V("wtb_lo")[:, dt * 128:(dt + 1) * 128], 1.0,
                                    [tr] + wtbh_slot.dep_write() + wtbl_slot.dep_write(), tag="wp_sp")
                sl.read(oh, ol)
                s1 = p.dma("sp", "wtst", lambda e, a=dhi, b=V("wtb_hi"), dt=dt, oc=oc: e.dma_start(
                    out=a[dt * 128:(dt + 1) * 128, oc * 128:(oc + 1) * 128], in_=b[:, dt * 128:(dt + 1) * 128]),
                    after=[oh], tag="wp_sth")
                s2 = p.dma("sp", "wtst", lambda e, a=dlo, b=V("wtb_lo"), dt=dt, oc=oc: e.dma_start(
                    out=a[dt * 128:(dt + 1) * 128, oc * 128:(oc + 1) * 128], in_=b[:, dt * 128:(dt + 1) * 128]),
                    after=[ol], tag="wp_stl")
                st_h.append(s1); st_l.append(s2)
                H["wprep_store"][id(dhi)] += [s1, s2]
            wtbh_slot.wrote(*st_h)
            wtbh_slot.r = []
            wtbl_slot.wrote(*st_l)
            wtbl_slot.r = []
            wtbh_slot.w = []
            wtbh_slot.r = st_h
            wtbl_slot.w = []
            wtbl_slot.r = st_l

    wf_slots = [Slot(V(n)) for n in ("wfr_hi", "wfr_lo", "wfi_hi", "wfi_lo")]
    pss = [Slot(V("ps_s0")), Slot(V("ps_s1"))]
    t32_slot = Slot(V("t32"))
    CIN = V("CIN")
    trt_st = []
    for gc in range(4):
        wf_w = []
        for mi, (srcw, bh_, bl_) in enumerate([(V("WFR"), V("wfr_hi"), V("wfr_lo")),
                                               (V("WFI"), V("wfi_hi"), V("wfi_lo"))]):
            for ft in range(8):
                ld = p.dma("sp", "wfs", lambda e, a=V("wstg"), b=srcw, ft=ft, gc=gc: e.dma_start(
                    out=a[:, 0:256], in_=b[ft * 128:(ft + 1) * 128, gc * 256:(gc + 1) * 256]),
                    after=wstg_slot.dep_write(), tag="wf_ld")
                wstg_slot.wrote(ld)
                oh, ol = split_evac(V("wstg")[:, 0:256], bh_[:, ft * 256:(ft + 1) * 256],
                                    bl_[:, ft * 256:(ft + 1) * 256], 1.0,
                                    [ld] + wf_slots[2 * mi].dep_write() + wf_slots[2 * mi + 1].dep_write(),
                                    tag="wf_sp")
                wstg_slot.read(oh, ol)
                wf_w += [oh, ol]
        for s in wf_slots:
            s.wrote(*wf_w)
        omg_slots = [Slot(V("omg")[:, 0:1024]), Slot(V("omg")[:, 1024:2048])]
        for dc in range(8):
            first = True
            last_r = last_i = None
            for ft in range(8):
                osl = omg_slots[ft % 2]
                sl = (ft % 2) * 8
                OM = V("omg")
                oms = []
                for k, cn in enumerate(("C_COS_HI", "C_COS_LO", "C_SIN_HI", "C_SIN_LO",
                                        "C_SINN_HI", "C_SINN_LO")):
                    o = p.dma("sp", "omg", lambda e, a=OM, b=CIN[cn], ft=ft, dc=dc, sl=sl, k=k: e.dma_start(
                        out=a[:, (sl + k) * 128:(sl + k + 1) * 128],
                        in_=b[ft * 128:(ft + 1) * 128, dc * 128:(dc + 1) * 128]),
                        after=osl.dep_write(), tag="om_ld")
                    oms.append(o)
                osl.wrote(*oms)
                ch = OM[:, (sl + 0) * 128:(sl + 1) * 128]
                cl = OM[:, (sl + 1) * 128:(sl + 2) * 128]
                sh = OM[:, (sl + 2) * 128:(sl + 3) * 128]
                slo = OM[:, (sl + 3) * 128:(sl + 4) * 128]
                snh = OM[:, (sl + 4) * 128:(sl + 5) * 128]
                snl = OM[:, (sl + 5) * 128:(sl + 6) * 128]
                wrh = V("wfr_hi")[:, ft * 256:(ft + 1) * 256]
                wrl = V("wfr_lo")[:, ft * 256:(ft + 1) * 256]
                wih = V("wfi_hi")[:, ft * 256:(ft + 1) * 256]
                wil = V("wfi_lo")[:, ft * 256:(ft + 1) * 256]
                dep = oms + wf_w + (pss[0].dep_write() if first else [])
                a1 = mm3(pss[0].ap[:, 0:256], ch, cl, wrh, wrl, first, False, after=dep, tag="s1r_c")
                a2 = mm3(pss[0].ap[:, 0:256], sh, slo, wih, wil, False, ft == 7, tag="s1r_s")
                dep = (pss[1].dep_write() if first else [])
                a3 = mm3(pss[1].ap[:, 0:256], ch, cl, wih, wil, first, False, after=dep, tag="s1i_c")
                a5 = mm3(pss[1].ap[:, 0:256], snh, snl, wrh, wrl, False, ft == 7, tag="s1i_s")
                first = False
                last_r = a2[-1]
                last_i = a5[-1]
                osl.r = list(a1 + a2 + a3 + a5)
                osl.w = oms
            cp_r = p.op("dve", lambda d, a=V("t32"), b=pss[0].ap: d.tensor_copy(a[:, 0:256], b[:, 0:256]),
                        after=[last_r] + t32_slot.dep_write(), tag="s1ev_r")
            cp_i = p.op("dve", lambda d, a=V("t32"), b=pss[1].ap: d.tensor_copy(a[:, 256:512], b[:, 0:256]),
                        after=[last_i], tag="s1ev_i")
            pss[0].wrote(cp_r); pss[0].r = [cp_r]; pss[0].w = []
            pss[1].wrote(cp_i); pss[1].r = [cp_i]; pss[1].w = []
            t32_slot.wrote(cp_r, cp_i)
            for mi, (dsth, dstl, coff) in enumerate([(V("TRT_HI"), V("TRT_LO"), 0),
                                                     (V("TIT_HI"), V("TIT_LO"), 256)]):
                for j in range(2):
                    gt2 = gc * 256 + j * 128
                    sl2 = pst_slots[(j * 2 + mi) % 2]
                    tr = p.op("te", lambda t, ps=sl2.ap, a=V("t32"), i=V("c_id"), j=j, coff=coff: t.transpose(
                        ps[:, 0:128], a[:, coff + j * 128:coff + (j + 1) * 128], i[:, :]),
                        after=[cp_r, cp_i] + sl2.dep_write(), tag="s1_tr")
                    sl2.wrote(tr)
                    t32_slot.read(tr)
                    oh, ol = split_evac(sl2.ap[:, 0:128], V("wtb_hi")[:, 0:128], V("wtb_lo")[:, 0:128],
                                        1.0, [tr] + wtbh_slot.dep_write() + wtbl_slot.dep_write(),
                                        tag="s1_tsp")
                    sl2.read(oh, ol)
                    wtbh_slot.wrote(oh)
                    wtbl_slot.wrote(ol)
                    s1_ = p.dma("sp", "trt_st", lambda e, a=dsth, b=V("wtb_hi"), gt2=gt2, dc=dc: e.dma_start(
                        out=a[gt2:gt2 + 128, dc * 128:(dc + 1) * 128], in_=b[:, 0:128]),
                        after=[oh], tag="s1_sth")
                    s2_ = p.dma("sp", "trt_st", lambda e, a=dstl, b=V("wtb_lo"), gt2=gt2, dc=dc: e.dma_start(
                        out=a[gt2:gt2 + 128, dc * 128:(dc + 1) * 128], in_=b[:, 0:128]),
                        after=[ol], tag="s1_stl")
                    wtbh_slot.read(s1_)
                    wtbl_slot.read(s2_)
                    trt_st += [s1_, s2_]
    a_store = []
    pev_slot = Slot((V("pev_hi"), V("pev_lo"), 0))
    pev_slot2 = Slot((V("pev_hi"), V("pev_lo"), 512))
    psum8 = [Slot(V("ps_big")[:, i * 512:(i + 1) * 512]) for i in range(4)] + \
            [pss[0], pss[1], pst_slots[0], pst_slots[1]]
    wg_slots = [Slot((V("wg_hi"), V("wg_lo"), 0)), Slot((V("wg_hi"), V("wg_lo"), 2048))]
    omg_slot = Slot(V("omg"))
    pevs = [pev_slot, pev_slot2]
    pev_i = [0]
    for oc in range(2):
        ev_prev = []
        mm_last = {}
        for gt in range(8):
            wsl0, wsl1 = wg_slots
            wh0, wl0, b0_ = wsl0.ap
            wh1, wl1, b1_ = wsl1.ap
            l1 = p.dma("sp", "wg", lambda e, a=wh0, b=V("TRT_HI"), gt=gt, b0_=b0_: e.dma_start(
                out=a[:, b0_:b0_ + 1024], in_=b[gt * 128:(gt + 1) * 128, :]),
                after=wsl0.dep_write() + (trt_st if (gt == 0 and oc == 0) else []), tag="s2_l1")
            l2 = p.dma("sp", "wg", lambda e, a=wl0, b=V("TRT_LO"), gt=gt, b0_=b0_: e.dma_start(
                out=a[:, b0_:b0_ + 1024], in_=b[gt * 128:(gt + 1) * 128, :]), tag="s2_l2")
            l3 = p.dma("sp", "wg", lambda e, a=wh1, b=V("TIT_HI"), gt=gt, b1_=b1_: e.dma_start(
                out=a[:, b1_:b1_ + 1024], in_=b[gt * 128:(gt + 1) * 128, :]),
                after=wsl1.dep_write(), tag="s2_l3")
            l4 = p.dma("sp", "wg", lambda e, a=wl1, b=V("TIT_LO"), gt=gt, b1_=b1_: e.dma_start(
                out=a[:, b1_:b1_ + 1024], in_=b[gt * 128:(gt + 1) * 128, :]), tag="s2_l4")
            wsl0.wrote(l1, l2); wsl1.wrote(l3, l4)
            oms = []
            for k, cn in enumerate(("C_COSS_HI", "C_COSS_LO", "C_SINNS_HI", "C_SINNS_LO")):
                o = p.dma("sp", "omg", lambda e, a=V("omg"), b=CIN[cn], gt=gt, oc=oc, k=k: e.dma_start(
                    out=a[:, k * 512:(k + 1) * 512],
                    in_=b[gt * 128:(gt + 1) * 128, oc * 512:(oc + 1) * 512]),
                    after=(omg_slot.dep_write() if k == 0 else []), tag="s2_om")
                oms.append(o)
            omg_slot.wrote(*oms)
            OM = V("omg")
            for dc in range(8):
                ps = psum8[dc]
                dep = [l1, l2, l3, l4] + oms + ((ps.dep_write() + ev_prev) if gt == 0 else [])
                m1 = mm3(ps.ap[:, :], wh0[:, b0_ + dc * 128:b0_ + (dc + 1) * 128],
                         wl0[:, b0_ + dc * 128:b0_ + (dc + 1) * 128],
                         OM[:, 0:512], OM[:, 512:1024], gt == 0, False, after=dep, tag="s2_r")
                m2 = mm3(ps.ap[:, :], wh1[:, b1_ + dc * 128:b1_ + (dc + 1) * 128],
                         wl1[:, b1_ + dc * 128:b1_ + (dc + 1) * 128],
                         OM[:, 1024:1536], OM[:, 1536:2048], False, gt == 7, tag="s2_i")
                wsl0.read(*m1); wsl1.read(*m2)
                omg_slot.read(*(m1 + m2))
                mm_last[dc] = m2[-1]
        ev_prev = []
        for dc in range(8):
            ps = psum8[dc]
            pe = pevs[pev_i[0]]; pev_i[0] ^= 1
            ph_, pl_, pbase = pe.ap
            oh, ol = split_evac(ps.ap[:, :], ph_[:, pbase:pbase + 512], pl_[:, pbase:pbase + 512],
                                1.0, [mm_last[dc]] + pe.dep_write(), tag="s2_sp")
            ps.wrote(oh); ps.r = [oh, ol]; ps.w = []
            s1 = p.dma("sp", "a_st", lambda e, a=V("A_HI"), b=ph_, dc=dc, oc=oc, pbase=pbase: e.dma_start(
                out=a[dc * 128:(dc + 1) * 128, oc * 512:(oc + 1) * 512], in_=b[:, pbase:pbase + 512]),
                after=[oh], tag="a_sth")
            s2 = p.dma("sp", "a_st", lambda e, a=V("A_LO"), b=pl_, dc=dc, oc=oc, pbase=pbase: e.dma_start(
                out=a[dc * 128:(dc + 1) * 128, oc * 512:(oc + 1) * 512], in_=b[:, pbase:pbase + 512]),
                after=[ol], tag="a_stl")
            pe.w = []; pe.r = [s1, s2]
            a_store += [s1, s2]
            ev_prev += [oh, ol]
    H["a_done"] = a_store
    xtg_slots = [Slot(V("xt_g")[:, 0:D]), Slot(V("xt_g")[:, D:2 * D])]
    xnt_slots = [Slot(V("xn_t")[:, 0:D]), Slot(V("xn_t")[:, D:2 * D])]
    xnt_slots[0].r += list(t32_slot.r) + list(t32_slot.w)
    xnT_slots = [Slot((V("xnT_hi"), V("xnT_lo"), 0)), Slot((V("xnT_hi"), V("xnT_lo"), 4096))]
    psum_rot = psum8[:6]
    xnh_slot = Slot(V("xnh"))
    xev_slot = Slot(V("xc_b"))

    state = {"wg_i": 0, "pev_i": 0, "ps_i": 0, "xnh_ops": []}

    def ln_group(tg, srcs, want_xnh, extra_dep=(), tag="ln", consume=None):
        def load_tile(k):
            tt = tg * 4 + k
            gsl = xtg_slots[tt % 2]
            XGk = gsl.ap
            o1 = p.dma("sp", "xin", lambda e, a=XGk, b=srcs[0], tt=tt: e.dma_start(
                out=a[:, :], in_=b[tt * 128:(tt + 1) * 128, :]),
                after=gsl.dep_write() + list(extra_dep), tag=tag + "_ldx")
            dep_k = o1
            if len(srcs) > 1:
                o2 = p.dma("sp", "xin2", lambda e, a=V("xc_a"), b=srcs[1], k=k, tt=tt: e.dma_start(
                    out=a[:, (k % 2) * D:((k % 2) + 1) * D], in_=b[tt * 128:(tt + 1) * 128, :]),
                    after=[x for x in state.get("xca_r_%d" % (k % 2), [])] + list(extra_dep),
                    tag=tag + "_ldy")
                o3 = p.op("dve", lambda d, a=XGk, b=V("xc_a"), k=k: d.tensor_tensor(
                    out=a[:, :], in0=a[:, :], in1=b[:, (k % 2) * D:((k % 2) + 1) * D], op=ALU.add),
                    after=[o1, o2], tag=tag + "_addy")
                state["xca_r_%d" % (k % 2)] = [o3]
                dep_k = o3
            gsl.wrote(dep_k)
            return gsl, dep_k

        tslots = [None] * 4
        lds = [None] * 4
        tslots[0], lds[0] = load_tile(0)
        tslots[1], lds[1] = load_tile(1)
        split_ops = []
        xTsl = xnT_slots[tg % 2]
        xbase = xTsl.ap[2]
        xT_dep = xTsl.dep_write()
        for k in range(4):
            tt = tg * 4 + k
            xsl = xnt_slots[k % 2]
            xin_ap = tslots[k].ap
            dk = [lds[k]]
            LS = V("lnsc")
            s1 = p.op("dve", lambda d, a=V("st6"), b=xin_ap: d.bn_stats(a[:, 0:6], b[:, 0:512]),
                      after=dk + [(x) for x in state.get("st_r", [])], tag=tag + "_bs1")
            s2 = p.op("dve", lambda d, a=V("st6"), b=xin_ap: d.bn_stats(a[:, 6:12], b[:, 512:1024]),
                      after=dk, tag=tag + "_bs2")
            s3 = p.op("dve", lambda d, a=V("st2"), b=V("st6"): d.bn_aggr(a[:, :], b[:, 0:12]),
                      after=[s1, s2], tag=tag + "_ba")
            s4 = p.op("dve", lambda d, a=LS, b=V("st2"): d.tensor_scalar(
                out=a[:, 0:1], in0=b[:, 1:2], scalar1=LN_EPS, scalar2=None, op0=ALU.add),
                after=[s3], tag=tag + "_veps")
            s5 = p.op("dve", lambda d, a=LS: d.reciprocal(a[:, 1:2], a[:, 0:1]), after=[s4], tag=tag + "_rc")
            s6 = p.op("act", lambda s, a=LS: s.activation(a[:, 2:3], a[:, 1:2], AF.Ln), after=[s5], tag=tag + "_ln")
            s7 = p.op("act", lambda s, a=LS: s.activation(a[:, 3:4], a[:, 2:3], AF.Exp, bias=0.0, scale=0.5),
                      after=[s6], tag=tag + "_ex")
            s8 = p.op("dve", lambda d, a=LS: d.tensor_tensor(out=a[:, 4:5], in0=a[:, 3:4], in1=a[:, 3:4],
                                                             op=ALU.mult), after=[s7], tag=tag + "_n1")
            s9 = p.op("dve", lambda d, a=LS: d.tensor_tensor(out=a[:, 5:6], in0=a[:, 4:5], in1=a[:, 0:1],
                                                             op=ALU.mult), after=[s8, s4], tag=tag + "_n2")
            s10 = p.op("dve", lambda d, a=LS: d.tensor_scalar(
                out=a[:, 6:7], in0=a[:, 5:6], scalar1=-0.5, scalar2=1.5, op0=ALU.mult, op1=ALU.add),
                after=[s9], tag=tag + "_n3")
            s11 = p.op("dve", lambda d, a=LS: d.tensor_tensor(out=a[:, 7:8], in0=a[:, 3:4], in1=a[:, 6:7],
                                                              op=ALU.mult), after=[s10], tag=tag + "_n4")
            s12 = p.op("dve", lambda d, a=LS, b=V("st2"): d.tensor_tensor(out=a[:, 8:9], in0=b[:, 0:2][:, 0:1],
                                                                          in1=a[:, 7:8], op=ALU.mult),
                       after=[s11, s3], tag=tag + "_mr")
            s13 = p.op("dve", lambda d, a=LS: d.tensor_scalar(out=a[:, 9:10], in0=a[:, 8:9], scalar1=-1.0,
                                                              scalar2=None, op0=ALU.mult), after=[s12], tag=tag + "_nm")
            s14 = p.op("act", lambda s, a=xsl.ap, b=xin_ap, c=LS: s.activation(
                a[:, :], b[:, :], AF.Identity, bias=c[:, 9:10], scale=c[:, 7:8]),
                after=[s13, s11] + xsl.dep_write(), tag=tag + "_nrm")
            state["st_r"] = [s3, s12, s14]
            xsl.wrote(s14)
            tslots[k].read(s14)
            for dt in range(8):
                sl = pst_slots[dt % 2]
                tr = p.op("te", lambda t, ps=sl.ap, a=xsl.ap, i=V("c_id"), dt=dt: t.transpose(
                    ps[:, 0:128], a[:, dt * 128:(dt + 1) * 128], i[:, :]),
                    after=[s14] + sl.dep_write(), tag=tag + "_tr")
                sl.wrote(tr)
                xsl.read(tr)
                co = xbase + dt * 512 + k * 128
                oh, ol = split_evac(sl.ap[:, 0:128], V("xnT_hi")[:, co:co + 128],
                                    V("xnT_lo")[:, co:co + 128], 1.0, [tr] + xT_dep, tag=tag + "_sp")
                sl.read(oh, ol)
                split_ops += [oh, ol]
                if want_xnh:
                    tt_g = tg * 4 + k
                    rd = p.op("dve", lambda d, a=V("xnh"), b=sl.ap, dt=dt, tt_g=tt_g: d.tensor_reduce(
                        out=a[:, dt * 128 + tt_g * 4:dt * 128 + tt_g * 4 + 4],
                        in_=b[:, 0:128].rearrange("p (s k) -> p s k", k=32),
                        axis=mybir.AxisListType.X, op=ALU.add),
                        after=[tr] + (xnh_slot.dep_write() if (tt_g == 0 and dt == 0) else []),
                        tag=tag + "_xnh")
                    sl.read(rd)
                    state["xnh_ops"].append(rd)
            if consume is not None:
                consume(k, tslots[k], split_ops[-16:], xbase, xTsl)
            if k + 2 < 4:
                tslots[k + 2], lds[k + 2] = load_tile(k + 2)
        xTsl.wrote(*split_ops)
        return split_ops, tslots, xbase, xTsl

    def stream_wgran(src_hi, src_lo, oc, width, dep, tag):
        i = state["wg_i"]; state["wg_i"] ^= 1
        sl = wg_slots[i]
        wh, wl, base = sl.ap
        d1 = p.dma("sp", "wg", lambda e, a=wh, b=src_hi, oc=oc, base=base, width=width: e.dma_start(
            out=a[:, base:base + 8 * width].rearrange("p (dt o) -> p dt o", o=width),
            in_=b.rearrange("(dt q) o -> q dt o", q=128)[:, :, oc * width:(oc + 1) * width]),
            after=sl.dep_write() + list(dep), tag=tag + "_wh")
        d2 = p.dma("sp", "wg", lambda e, a=wl, b=src_lo, oc=oc, base=base, width=width: e.dma_start(
            out=a[:, base:base + 8 * width].rearrange("p (dt o) -> p dt o", o=width),
            in_=b.rearrange("(dt q) o -> q dt o", q=128)[:, :, oc * width:(oc + 1) * width]),
            after=[], tag=tag + "_wl")
        sl.wrote(d1, d2)
        return sl, (wh, wl, base), [d1, d2]

    def next_psum():
        i = state["ps_i"]; state["ps_i"] = (state["ps_i"] + 1) % len(psum_rot)
        return psum_rot[i]

    def next_pev():
        i = state["pev_i"]; state["pev_i"] ^= 1
        return pevs[i]

    def proj_fmajor(tg, split_ops, src_hi, src_lo, brow, scale, dst_hi, dst_lo, ocs, st_sem, tag, wdep=(), oc_base=0, xbase=0, xTsl=None):
        tglob = tg * 512
        stores = []
        for oc in ocs:
            wsl, (wh, wl, base), wld = stream_wgran(src_hi, src_lo, oc, 128, wdep, tag + "_wg%d" % oc)
            ps = next_psum()
            last = None
            for dt in range(8):
                dep = wld + split_ops + (ps.dep_write() if dt == 0 else [])
                ms = mm3(ps.ap[:, :], wh[:, base + dt * 128:base + (dt + 1) * 128],
                         wl[:, base + dt * 128:base + (dt + 1) * 128],
                         V("xnT_hi")[:, xbase + dt * 512:xbase + (dt + 1) * 512],
                         V("xnT_lo")[:, xbase + dt * 512:xbase + (dt + 1) * 512], dt == 0,
                         (dt == 7 and brow is None),
                         after=dep, tag=tag + "_mm")
                wsl.read(*ms)
                if xTsl is not None:
                    xTsl.read(*ms)
                last = ms[-1]
            if brow is not None:
                bm = p.op("te", lambda t, ps=ps.ap, b=brow, o2=V("c_ones2"), oc=oc: t.matmul(
                    ps[:, :], b[:, oc * 128:(oc + 1) * 128], o2[:, :], start=False, stop=True),
                    after=H["bias_rows"], tag=tag + "_bias")
                last = bm
            pe = next_pev()
            ph_, pl_, pbase = pe.ap
            oh, ol = split_evac(ps.ap[:, :], ph_[:, pbase:pbase + 512], pl_[:, pbase:pbase + 512],
                                scale, [last] + pe.dep_write(), tag=tag + "_ev")
            ps.wrote(oh); ps.r = [oh, ol]; ps.w = []
            od = oc - oc_base
            s1 = p.dma("sp", st_sem, lambda e, a=dst_hi, b=ph_, od=od, tglob=tglob, pbase=pbase: e.dma_start(
                out=a[od * 128:(od + 1) * 128, tglob:tglob + 512], in_=b[:, pbase:pbase + 512]),
                after=[oh], tag=tag + "_sth")
            s2 = p.dma("sp", st_sem, lambda e, a=dst_lo, b=pl_, od=od, tglob=tglob, pbase=pbase: e.dma_start(
                out=a[od * 128:(od + 1) * 128, tglob:tglob + 512], in_=b[:, pbase:pbase + 512]),
                after=[ol], tag=tag + "_stl")
            pe.w = []; pe.r = [s1, s2]
            stores += [s1, s2]
        return stores

    def proj_tmajor(tg, split_ops, src_hi, src_lo, o_off, brow, dst_hi, dst_lo, st_sem, tag,
                    n_oc=4, width=256, wdep=(), xbase=0, xTsl=None):
        stores = []
        for oc in range(n_oc):
            oco = (o_off // width) + oc
            wsl, (wh, wl, base), wld = stream_wgran(src_hi, src_lo, oco, width, wdep, tag + "_wg%d" % oc)
            for k in range(4):
                tt = tg * 4 + k
                ps = next_psum()
                last = None
                for dt in range(8):
                    dep = wld + split_ops + (ps.dep_write() if dt == 0 else [])
                    ms = mm3(ps.ap[:, 0:width],
                             V("xnT_hi")[:, xbase + dt * 512 + k * 128:xbase + dt * 512 + (k + 1) * 128],
                             V("xnT_lo")[:, xbase + dt * 512 + k * 128:xbase + dt * 512 + (k + 1) * 128],
                             wh[:, base + dt * width:base + (dt + 1) * width],
                             wl[:, base + dt * width:base + (dt + 1) * width], dt == 0,
                             (dt == 7 and brow is None),
                             after=dep, tag=tag + "_mm")
                    wsl.read(*ms)
                    if xTsl is not None:
                        xTsl.read(*ms)
                    last = ms[-1]
                if brow is not None:
                    bm = p.op("te", lambda t, ps=ps.ap, b=brow, o2=V("c_ones2"), oco=oco, width=width: t.matmul(
                        ps[:, 0:width], o2[:, 0:128], b[:, oco * width:(oco + 1) * width],
                        start=False, stop=True), after=H["bias_rows"], tag=tag + "_bias")
                    last = bm
                pe = next_pev()
                ph_, pl_, pbase = pe.ap
                oh, ol = split_evac(ps.ap[:, 0:width], ph_[:, pbase:pbase + width],
                                    pl_[:, pbase:pbase + width], 1.0,
                                    [last] + pe.dep_write(), tag=tag + "_ev")
                ps.wrote(oh); ps.r = [oh, ol]; ps.w = []
                s1 = p.dma("sp", st_sem, lambda e, a=dst_hi, b=ph_, tt=tt, odc=oc, pbase=pbase, width=width: e.dma_start(
                    out=a[tt * 128:(tt + 1) * 128, odc * width:(odc + 1) * width], in_=b[:, pbase:pbase + width]),
                    after=[oh], tag=tag + "_sth")
                s2 = p.dma("sp", st_sem, lambda e, a=dst_lo, b=pl_, tt=tt, odc=oc, pbase=pbase, width=width: e.dma_start(
                    out=a[tt * 128:(tt + 1) * 128, odc * width:(odc + 1) * width], in_=b[:, pbase:pbase + width]),
                    after=[ol], tag=tag + "_stl")
                pe.w = []; pe.r = [s1, s2]
                stores += [s1, s2]
        return stores

    H["qa_st"], H["ka_st"], H["va_st"] = [], [], []
    state["xnh_ops"] = []
    for tg in range(8):
        split_ops, _, xbase, xTsl = ln_group(tg, [V("XIN")], True, tag="ln1")
        H["qa_st"] += proj_fmajor(tg, split_ops, V("WT_Q_HI"), V("WT_Q_LO"), V("brow_q"), 0.125,
                                  V("QA_HI"), V("QA_LO"), range(8), "qa_st", "pq",
                                  wdep=H["wprep_store"][id(V("WT_Q_HI"))], xbase=xbase, xTsl=xTsl)
        H["ka_st"] += proj_fmajor(tg, split_ops, V("WT_K_HI"), V("WT_K_LO"), V("brow_k"), 1.0,
                                  V("KA_HI"), V("KA_LO"), range(8), "ka_st", "pk",
                                  wdep=H["wprep_store"][id(V("WT_K_HI"))], xbase=xbase, xTsl=xTsl)
        H["va_st"] += proj_tmajor(tg, split_ops, V("WT_V_HI"), V("WT_V_LO"), 0, V("brow_v"),
                                  V("VA_HI"), V("VA_LO"), "va_st", "pv",
                                  wdep=H["wprep_store"][id(V("WT_V_HI"))], xbase=xbase, xTsl=xTsl)
    H["xnh1"] = list(state["xnh_ops"])

    def ph4(xnh_dep, wq_hi, wq_lo, wk_hi, wk_lo, wv_hi, wv_lo, bq, bk, bv, call_dst,
            full=True, tag="p4", k_ocbase=0, crow_ld=None, q_slot=None, k_slot=None,
            vh_slot=None):
        oh = p.op("act", lambda s, a=V("xnh_hi"), b=V("xnh"): s.activation(
            a[:, :], b[:, :], AF.Identity, bias=0.0, scale=1.0 / 32.0), after=xnh_dep, tag=tag + "_xh")
        ol = p.op("dve", lambda d, a=V("xnh_lo"), b=V("xnh"), c=V("xnh_hi"): d.scalar_tensor_tensor(
            out=a[:, :], in0=b[:, :], scalar=1.0 / 32.0, in1=c[:, :], op0=ALU.mult, op1=ALU.subtract),
            after=xnh_dep + [oh], tag=tag + "_xl")
        xnh_sp = [oh, ol]
        qk_ev = []
        for (srch, srcl, brw, dst, scale, wdep, ocb) in (
                (wq_hi, wq_lo, bq, V("qht"), 0.125, H["wprep_store"][id(wq_hi)], 0),
                (wk_hi, wk_lo, bk, V("kht"), 1.0, H["wprep_store"][id(wk_hi)], k_ocbase)):
            for ocr in range(8):
                oc = ocb + ocr
                wsl, (wh, wl, base), wld = stream_wgran(srch, srcl, oc, 128, wdep, tag + "_qkw")
                ps = next_psum()
                last = None
                for dt in range(8):
                    dep = wld + xnh_sp + (ps.dep_write() if dt == 0 else [])
                    ms = mm3(ps.ap[:, 0:128], wh[:, base + dt * 128:base + (dt + 1) * 128],
                             wl[:, base + dt * 128:base + (dt + 1) * 128],
                             V("xnh_hi")[:, dt * 128:(dt + 1) * 128],
                             V("xnh_lo")[:, dt * 128:(dt + 1) * 128], dt == 0, brw is None and dt == 7,
                             after=dep, tag=tag + "_qkmm")
                    wsl.read(*ms)
                    last = ms[-1]
                if brw is not None:
                    last = p.op("te", lambda t, ps=ps.ap, b=brw, o2=V("c_ones2"), oc=oc: t.matmul(
                        ps[:, 0:128], b[:, oc * 128:(oc + 1) * 128], o2[:, 0:128], start=False, stop=True),
                        after=H["bias_rows"], tag=tag + "_qkb")
                osl_ = q_slot if dst is V("qht") else k_slot
                ev = p.op("act", lambda s, a=dst, b=ps.ap, ocr=ocr, sc=scale: s.activation(
                    a[:, ocr * 128:(ocr + 1) * 128], b[:, 0:128], AF.Identity, bias=0.0, scale=sc),
                    after=[last] + (osl_.dep_write() if (osl_ is not None and ocr == 0) else []),
                    tag=tag + "_qkev")
                if osl_ is not None:
                    if ocr == 0:
                        osl_.wrote(ev)
                    else:
                        osl_.w.append(ev)
                ps.wrote(ev); ps.r = [ev]; ps.w = []
                qk_ev.append(ev)
        vh_ev = []
        for oc in (range(4) if full else []):
            wsl, (wh, wl, base), wld = stream_wgran(wv_hi, wv_lo, oc, 256,
                                                    H["wprep_store"][id(wv_hi)], tag + "_vw")
            ps = next_psum()
            last = None
            for dt in range(8):
                dep = wld + xnh_sp + (ps.dep_write() if dt == 0 else [])
                ms = mm3(ps.ap[:, 0:256], V("xnh_hi")[:, dt * 128:(dt + 1) * 128],
                         V("xnh_lo")[:, dt * 128:(dt + 1) * 128],
                         wh[:, base + dt * 256:base + (dt + 1) * 256],
                         wl[:, base + dt * 256:base + (dt + 1) * 256], dt == 0, bv is None and dt == 7,
                         after=dep, tag=tag + "_vmm")
                wsl.read(*ms)
                last = ms[-1]
            if bv is not None:
                last = p.op("te", lambda t, ps=ps.ap, b=bv, o2=V("c_ones2"), oc=oc: t.matmul(
                    ps[:, 0:256], o2[:, 0:128], b[:, oc * 256:(oc + 1) * 256], start=False, stop=True),
                    after=H["bias_rows"], tag=tag + "_vb")
            ev = p.op("act", lambda s, a=V("vhs"), b=ps.ap, oc=oc: s.activation(
                a[:, oc * 256:(oc + 1) * 256], b[:, 0:256], AF.Identity, bias=0.0, scale=1.0),
                after=[last] + (vh_slot.dep_write() if (vh_slot is not None and oc == 0) else []),
                tag=tag + "_vev")
            if vh_slot is not None:
                if oc == 0:
                    vh_slot.wrote(ev)
                else:
                    vh_slot.w.append(ev)
            ps.wrote(ev); ps.r = [ev]; ps.w = []
            vh_ev.append(ev)
        norm_ops = []
        call_ops = {}
        low_cp = {}
        for bh in range(BH):
            b, h = bh // NH, bh % NH
            pr = 64 * (h % 2)
            co = (h // 2) * 128 + b * 64
            ps = next_psum()
            lm = p.op("te", lambda t, ps=ps.ap, q=V("qht"), k=V("kht"), pr=pr, co=co: t.matmul(
                ps[0:64, 0:64], q[pr:pr + 64, co:co + 64], k[pr:pr + 64, co:co + 64],
                start=True, stop=True), after=qk_ev + ps.dep_write(), tag=tag + "_low")
            if q_slot is not None:
                q_slot.read(lm)
            if k_slot is not None:
                k_slot.read(lm)
            rm = p.op("dve", lambda d, a=V("rmcol"), b=ps.ap, bh=bh: d.reduce_max(
                a[0:64, bh:bh + 1], b[0:64, 0:64], axis=mybir.AxisListType.X), after=[lm], tag=tag + "_rm")
            cp = p.op("dve", lambda d, a=V("lowall"), b=ps.ap, bh=bh: d.tensor_copy(
                a[:, bh * 64:(bh + 1) * 64], b[0:64, 0:64]), after=[lm], tag=tag + "_lcp")
            low_cp[bh] = cp
            ps4 = next_psum()
            art = p.op("te", lambda t, ps=ps4.ap, b=V("rmcol"), i=V("c_id"), bh=bh: t.transpose(
                ps[0:1, 0:64], b[0:64, bh:bh + 1], i[0:64, 0:64]),
                after=[rm] + ps4.dep_write() + [x for x in call_ops.get(bh - 1, [])], tag=tag + "_art")
            ar = p.op("dve", lambda d, a=V("sc_b"), ps=ps4.ap: d.reduce_max(
                a[0:1, 0:1], ps[0:1, 0:64], axis=mybir.AxisListType.X), after=[art], tag=tag + "_ar")
            ps4.wrote(ar); ps4.r = [ar]; ps4.w = []
            ng = p.op("dve", lambda d, a=V("sc_b"): d.tensor_scalar(
                out=a[0:1, 1:2], in0=a[0:1, 0:1], scalar1=-1.0, scalar2=-CMARGIN,
                op0=ALU.mult, op1=ALU.add), after=[ar], tag=tag + "_ng")
            ps2 = next_psum()
            bc = p.op("te", lambda t, ps=ps2.ap, o1=V("c_ones1f"), b=V("sc_b"): t.matmul(
                ps[:, 0:1], o1[:, :], b[0:1, 1:2], start=True, stop=True),
                after=[ng, const_ops["c_ones1f"]] + ps2.dep_write(), tag=tag + "_bc")
            cc = p.op("dve", lambda d, a=call_dst, b=ps2.ap, bh=bh: d.tensor_copy(
                a[:, bh:bh + 1], b[:, 0:1]), after=[bc], tag=tag + "_cc")
            ps2.wrote(cc); ps2.r = [cc]; ps2.w = []
            call_ops[bh] = [cc, ng]
            if crow_ld is not None:
                ng2 = p.op("dve", lambda d, a=V("sc_b"), c=V("crow_t"), bh=bh: d.tensor_tensor(
                    out=a[0:1, 2:3], in0=a[0:1, 1:2], in1=c[0:1, bh:bh + 1], op=ALU.add),
                    after=[ng, crow_ld], tag=tag + "_ng2")
                ps3b = next_psum()
                bc2 = p.op("te", lambda t, ps=ps3b.ap, o1=V("c_ones1f"), b=V("sc_b"): t.matmul(
                    ps[:, 0:1], o1[:, :], b[0:1, 2:3], start=True, stop=True),
                    after=[ng2] + ps3b.dep_write(), tag=tag + "_bc2")
                cc2 = p.op("dve", lambda d, a=V("call_bc"), b=ps3b.ap, bh=bh: d.tensor_copy(
                    a[:, bh:bh + 1], b[:, 0:1]), after=[bc2], tag=tag + "_cc2")
                ps3b.wrote(cc2); ps3b.r = [cc2]; ps3b.w = []
                call_ops[bh] = [cc, ng, cc2]
            if full:
                n1 = p.op("dve", lambda d, a=V("normall"), b=ps.ap, r=V("rmcol"), bh=bh: d.tensor_scalar(
                    out=a[:, bh * 64:(bh + 1) * 64], in0=b[0:64, 0:64], scalar1=r[0:64, bh:bh + 1],
                    scalar2=None, op0=ALU.subtract), after=[lm, rm], tag=tag + "_nrm")
                n2 = p.op("dve", lambda d, a=V("normall"), c=V("c_band"), bh=bh: d.tensor_tensor(
                    out=a[:, bh * 64:(bh + 1) * 64], in0=a[:, bh * 64:(bh + 1) * 64], in1=c[:, :],
                    op=ALU.add), after=[n1, const_ops["c_band"]], tag=tag + "_bnd")
                norm_ops.append(n2)
                ps.wrote(n2); ps.r = [rm, cp, n1]; ps.w = []
            else:
                ps.wrote(cp); ps.r = [rm, cp]; ps.w = []
        return norm_ops, call_ops, low_cp, vh_ev

    qth_slot = Slot(V("qt_hi")); qtl_slot = Slot(V("qt_lo"))
    kth_slot = Slot(V("kt_hi")); ktl_slot = Slot(V("kt_lo"))
    vsh_slot = Slot(V("vsb_hi")); vsl_slot = Slot(V("vsb_lo"))
    p32_slots = [Slot(V("p32")[:, 0:512]), Slot(V("p32")[:, 512:1024])]
    pt_slots = [Slot((V("ptil_hi"), V("ptil_lo"))), Slot((V("ptil_hi2"), V("ptil_lo2")))]
    pth_slot = pt_slots[0]; ptl_slot = pt_slots[0]
    m8_slot = Slot(V("m8"))
    xbm_slot = Slot(V("xbm"))
    oev_slot = Slot(V("oev"))
    mb64_slot = Slot(V("mb64"))
    yab_slot = Slot(V("yab"))
    obig = psum_rot[0:4]
    norm_ops, call_ops_a, low_cp, vh_ev = ph4(H["xnh1"], V("WT_Q_HI"), V("WT_Q_LO"), V("WT_K_HI"),
                                       V("WT_K_LO"), V("WT_V_HI"), V("WT_V_LO"),
                                       V("brow_q"), V("brow_k"), V("brow_v"), V("call_a"),
                                       full=True, tag="p4a", q_slot=kth_slot, k_slot=ktl_slot,
                                       vh_slot=xnt_slots[1])
    nf = p.dma("sp", "normf", lambda e, a=V("NORMF"), b=V("normall"): e.dma_start(
        out=a.rearrange("(bh i j) -> i bh j", i=64, j=64),
        in_=b[:, :].rearrange("i (bh j) -> i bh j", j=64)), after=norm_ops, tag="p4_nf")
    bf = p.dma("sp", "normf", lambda e, a=V("bisf"), b=V("NORMF"): e.dma_start(
        out=a[:, :], in_=b.rearrange("(q f) -> q f", f=1024)), after=[nf], tag="p4_bf")
    BC = V("bcols")
    i_lo = p.op("dve", lambda d, a=BC: d.memset(a[:, 0:1], -200.0), tag="bi_lo")
    i_hi = p.op("dve", lambda d, a=BC: d.memset(a[:, 1:2], 1.0), tag="bi_hi")
    prev = [i_lo, i_hi, bf]
    bisj_slot = Slot(V("bisj"))
    for it in range(41):
        m1 = p.op("dve", lambda d, a=BC: d.tensor_tensor(out=a[:, 2:3], in0=a[:, 0:1], in1=a[:, 1:2],
                                                         op=ALU.add), after=prev, tag="bi_m1")
        m2 = p.op("dve", lambda d, a=BC: d.tensor_scalar(out=a[:, 2:3], in0=a[:, 2:3], scalar1=0.5,
                                                         scalar2=None, op0=ALU.mult), after=[m1], tag="bi_m2")
        ct = p.op("dve", lambda d, a=V("bisj"), b=V("bisf"), c=BC: d.tensor_scalar(
            out=a[:, :], in0=b[:, :], scalar1=c[:, 2:3], scalar2=0.0, op0=ALU.is_ge, op1=ALU.add,
            accum_out=c[:, 3:4]), after=[m2, bf] + bisj_slot.dep_write(), tag="bi_ct")
        bisj_slot.wrote(ct)
        psm = next_psum()
        c4 = p.op("te", lambda t, ps=psm.ap, p4=V("c_p4"), c=BC: t.matmul(
            ps[:, 0:1], p4[:, :], c[:, 3:4], start=True, stop=True),
            after=[ct, const_ops["c_p4"]] + psm.dep_write(), tag="bi_c4")
        dr = p.op("dve", lambda d, a=BC, b=psm.ap: d.tensor_scalar(
            out=a[:, 4:5], in0=b[:, 0:1], scalar1=float(NUM_BLOCK), scalar2=None, op0=ALU.is_ge),
            after=[c4], tag="bi_dr")
        psm.wrote(dr); psm.r = [dr]; psm.w = []
        u1 = p.op("dve", lambda d, a=BC: d.tensor_tensor(out=a[:, 5:6], in0=a[:, 2:3], in1=a[:, 0:1],
                                                         op=ALU.subtract), after=[dr], tag="bi_u1")
        u2 = p.op("dve", lambda d, a=BC: d.tensor_tensor(out=a[:, 6:7], in0=a[:, 5:6], in1=a[:, 4:5],
                                                         op=ALU.mult), after=[u1], tag="bi_u2")
        u3 = p.op("dve", lambda d, a=BC: d.tensor_tensor(out=a[:, 0:1], in0=a[:, 0:1], in1=a[:, 6:7],
                                                         op=ALU.add), after=[u2], tag="bi_u3")
        u4 = p.op("dve", lambda d, a=BC: d.tensor_tensor(out=a[:, 5:6], in0=a[:, 1:2], in1=a[:, 2:3],
                                                         op=ALU.subtract), after=[u3], tag="bi_u4")
        u5 = p.op("dve", lambda d, a=BC: d.tensor_tensor(out=a[:, 6:7], in0=a[:, 5:6], in1=a[:, 4:5],
                                                         op=ALU.mult), after=[u4], tag="bi_u5")
        u6 = p.op("dve", lambda d, a=BC: d.tensor_tensor(out=a[:, 1:2], in0=a[:, 2:3], in1=a[:, 6:7],
                                                         op=ALU.add), after=[u5], tag="bi_u6")
        prev = [u6]
    thr_op = prev[0]
    mk1 = p.op("dve", lambda d, a=V("m01"), b=V("bisf"), c=BC: d.tensor_scalar(
        out=a[:, :], in0=b[:, :], scalar1=c[:, 0:1], scalar2=None, op0=ALU.is_ge),
        after=[thr_op], tag="p4_m01")
    mk2 = p.op("dve", lambda d, a=V("mbf"), b=V("m01"): d.tensor_scalar(
        out=a[:, :], in0=b[:, :], scalar1=BIG, scalar2=-BIG, op0=ALU.mult, op1=ALU.add),
        after=[mk1], tag="p4_mbf")
    mk3 = p.op("dve", lambda d, a=V("z32f_s"), b=V("m01"): d.tensor_scalar(
        out=a[:, :], in0=b[:, :], scalar1=-32.0, scalar2=32.0, op0=ALU.mult, op1=ALU.add),
        after=[mk1], tag="p4_z32")
    mf = p.dma("sp", "maskf", lambda e, a=V("MASKF"), b=V("mbf"): e.dma_start(
        out=a.rearrange("(q f) -> q f", f=1024), in_=b[:, :]), after=[mk2], tag="p4_mf")
    zf = p.dma("sp", "maskf", lambda e, a=V("Z32F"), b=V("z32f_s"): e.dma_start(
        out=a.rearrange("(q f) -> q f", f=1024), in_=b[:, :]), after=[mk3], tag="p4_zf")
    H["maskf"] = mf
    m8_slot.w = list(norm_ops); m8_slot.r = [nf]
    pt_slots[0].w = [low_cp[x] for x in low_cp]; pt_slots[0].r = []
    qth_slot.w = [mk2]; qth_slot.r = [mf]
    qtl_slot.w = [mk3]; qtl_slot.r = [zf]
    oev_slot.w = [mk1]; oev_slot.r = [mk2, mk3]
    vha_ones = p.op("pool", lambda gp, a=V("vha"): gp.memset(a[:, 64:65], 1.0), tag="p4_vo")
    lot_ops = {}
    ret_ops = {}
    z64_slot = Slot(V("z64"))
    sca_slot = Slot(V("sc_a"))
    lat_slot = Slot(V("lat"))
    vha_slot = Slot(V("vha"))
    scd_slot = Slot(V("sc_d"))
    for bh in range(BH):
        b, h = bh // NH, bh % NH
        zl = p.dma("sp", "z64", lambda e, a=V("z64"), b=V("Z32F"), bh=bh: e.dma_start(
            out=a[:, :], in_=b.rearrange("(bh i j) -> bh i j", i=64, j=64)[bh]),
            after=[zf] + z64_slot.dep_write(), tag="p5_zl")
        z64_slot.wrote(zl)
        le = p.op("act", lambda s, a=V("sc_a"), b=V("lowall"), c=V("call_a"), bh=bh: s.activation(
            a[:, :], b[:, bh * 64:(bh + 1) * 64], AF.Exp, bias=c[0:64, bh:bh + 1], scale=1.0),
            after=[low_cp[bh]] + call_ops_a[bh] + sca_slot.dep_write(), tag="p5_le")
        pt_slots[0].read(le)
        la = p.op("dve", lambda d, a=V("sc_a"), b=V("z64"): d.tensor_tensor(
            out=a[:, :], in0=a[:, :], in1=b[:, :], op=ALU.mult), after=[le, zl], tag="p5_la")
        sca_slot.wrote(le, la)
        z64_slot.read(la)
        ps = next_psum()
        tr = p.op("te", lambda t, ps=ps.ap, a=V("sc_a"), i=V("c_id"): t.transpose(
            ps[0:64, 0:64], a[:, :], i[0:64, 0:64]), after=[la] + ps.dep_write(), tag="p5_lt")
        sca_slot.read(tr)
        lc = p.op("dve", lambda d, a=V("lat"), b=ps.ap: d.tensor_copy(a[:, :], b[0:64, 0:64]),
                  after=[tr] + lat_slot.dep_write(), tag="p5_ltc")
        lat_slot.wrote(lc)
        ps.wrote(lc); ps.r = [lc]; ps.w = []
        vc = p.op("dve", lambda d, a=V("vha"), b=V("vhs"), b_=b, h=h: d.tensor_copy(
            a[:, 0:64], b[b_ * 64:b_ * 64 + 64, h * 64:(h + 1) * 64]),
            after=vh_ev + vha_slot.dep_write(), tag="p5_vc")
        xnt_slots[1].read(vc)
        vha_slot.wrote(vc)
        ps2 = next_psum()
        lo_mm = p.op("te", lambda t, ps=ps2.ap, a=V("vha"), b=V("lat"): t.matmul(
            ps[0:65, 0:64], a[:, :], b[:, :], start=True, stop=True),
            after=[vc, lc, vha_ones] + ps2.dep_write(), tag="p5_lmm")
        vha_slot.read(lo_mm)
        lat_slot.read(lo_mm)
        lotc = p.op("dve", lambda d, a=V("lot"), b=ps2.ap, bh=bh: d.tensor_copy(
            a[0:65, bh * 64:(bh + 1) * 64], b[0:65, 0:64]), after=[lo_mm], tag="p5_lot")
        ps2.wrote(lotc); ps2.r = [lotc]; ps2.w = []
        lot_ops[bh] = lotc
        re1 = p.op("act", lambda s, a=V("sc_b"), b=V("rmcol"), c=V("call_a"), bh=bh: s.activation(
            a[0:64, 2:3], b[0:64, bh:bh + 1], AF.Exp, bias=c[0:64, bh:bh + 1], scale=1.0),
            after=call_ops_a[bh] + [x for x in ret_ops.get(bh - 1, [])], tag="p5_re1")
        re2 = p.op("dve", lambda d, a=V("sc_d"), b=V("c_sel"), c=V("sc_b"): d.tensor_tensor(
            out=a[:, :], in0=b[:, :], in1=c[0:64, 2:3].broadcast_to([64, 1, 16]), op=ALU.mult),
            after=[re1, const_ops["c_sel"]] + scd_slot.dep_write(), tag="p5_re2")
        scd_slot.wrote(re2)
        ps3 = next_psum()
        re3 = p.op("te", lambda t, ps=ps3.ap, a=V("c_e4"), b=V("sc_d"): t.matmul(
            ps[:, 0:16], a[:, :], b[:, :], start=True, stop=True),
            after=[re2, const_ops["c_e4"]] + ps3.dep_write(), tag="p5_re3")
        scd_slot.read(re3)
        re4 = p.op("dve", lambda d, a=V("ret"), b=ps3.ap, bh=bh: d.tensor_copy(
            a[:, bh * 16:(bh + 1) * 16], b[:, 0:16]), after=[re3], tag="p5_re4")
        ps3.wrote(re4); ps3.r = [re4]; ps3.w = []
        ret_ops[bh] = [re1, re4]
    H["lot"] = lot_ops
    H["ret"] = ret_ops


    def emit_pv(prev, dv1, dv2, is_last, tag):
        ops_dep, sslot, kt = prev
        PH = V("ptil_hi") if sslot == 0 else V("ptil_hi2")
        PL = V("ptil_lo") if sslot == 0 else V("ptil_lo2")
        ptsl = pt_slots[sslot]
        pv_ops = []
        for pi, (lp, rp) in enumerate(((V("vsb_hi"), PH), (V("vsb_hi"), PL), (V("vsb_lo"), PH))):
            for qc in range(4):
                st = (kt == 0 and pi == 0)
                sp = (is_last and pi == 2)
                dep = ops_dep + [dv1, dv2] + (obig[qc].dep_write() if st else [])
                o = p.op("te", lambda t, ps=obig[qc].ap, lp=lp, rp=rp, kt=kt, qc=qc, st=st, sp=sp: t.matmul(
                    ps[0:65, :], lp[:, kt * 65:kt * 65 + 65], rp[:, qc * 512:(qc + 1) * 512],
                    start=st, stop=sp), after=dep, tag=tag + "_pv")
                pv_ops.append(o)
                vsh_slot.read(o); vsl_slot.read(o)
                ptsl.read(o)
        return pv_ops

    def attn_main(QHI, QLO, KHI, KLO, VHI, VLO, q_st, k_st, v_st, call_src, call_deps, tag):
        ya_st = {0: [], 1: []}
        su1 = p.op("dve", lambda d, a=V("qt_hi"), c=V("c_oneh"): d.tensor_copy(
            a[64:128, :], c[:, :]), after=[const_ops["c_oneh"]] + qth_slot.dep_write(),
            tag=tag + "_su1")
        su2 = p.op("dve", lambda d, a=V("qt_lo"): d.memset(a[64:128, :], 0.0),
                   after=qtl_slot.dep_write(), tag=tag + "_su2")
        su3 = p.op("dve", lambda d, a=V("kt_lo"): d.memset(a[64:128, :], 0.0),
                   after=ktl_slot.dep_write(), tag=tag + "_su3")
        su4 = p.op("dve", lambda d, a=V("vsb_hi"): d.memset(
            a[:, :].rearrange("p (k c) -> p k c", c=65)[:, :, 64:65], 1.0),
            after=vsh_slot.dep_write(), tag=tag + "_su4")
        su5 = p.op("dve", lambda d, a=V("vsb_lo"): d.memset(
            a[:, :].rearrange("p (k c) -> p k c", c=65)[:, :, 64:65], 0.0),
            after=vsl_slot.dep_write(), tag=tag + "_su5")
        qth_slot.wrote(su1); qtl_slot.wrote(su2); ktl_slot.wrote(su3)
        vsh_slot.wrote(su4); vsl_slot.wrote(su5)
        setup_ops = [su1, su2, su3, su4, su5]
        for bh in range(BH):
            b, h = bh // NH, bh % NH
            dq1 = p.dma("sp", "qtl", lambda e, a=V("qt_hi"), s=QHI, b=b, h=h: e.dma_start(
                out=a[0:64, :], in_=s[h * 64:(h + 1) * 64, b * SEQ:(b + 1) * SEQ]),
                after=q_st + qth_slot.dep_write(), tag=tag + "_dq")
            dq2 = p.dma("sp", "qtl", lambda e, a=V("qt_lo"), s=QLO, b=b, h=h: e.dma_start(
                out=a[0:64, :], in_=s[h * 64:(h + 1) * 64, b * SEQ:(b + 1) * SEQ]),
                after=qtl_slot.dep_write(), tag=tag + "_dq2")
            dk1 = p.dma("sp", "ktl", lambda e, a=V("kt_hi"), s=KHI, b=b, h=h: e.dma_start(
                out=a[0:64, :], in_=s[h * 64:(h + 1) * 64, b * SEQ:(b + 1) * SEQ]),
                after=k_st + kth_slot.dep_write(), tag=tag + "_dk")
            dk2 = p.dma("sp", "ktl", lambda e, a=V("kt_lo"), s=KLO, b=b, h=h: e.dma_start(
                out=a[0:64, :], in_=s[h * 64:(h + 1) * 64, b * SEQ:(b + 1) * SEQ]),
                after=ktl_slot.dep_write(), tag=tag + "_dk2")
            dv1 = p.dma("sp", "vtl", lambda e, a=V("vsb_hi"), s=VHI, b=b, h=h: e.dma_start(
                out=a[:, :].rearrange("p (k c) -> p k c", c=65)[:, :, 0:64],
                in_=s.rearrange("(bb kt q) o -> bb q kt o", bb=2, q=128)[b][:, :, h * 64:(h + 1) * 64]),
                after=v_st + vsh_slot.dep_write(), tag=tag + "_dv")
            dv2 = p.dma("sp", "vtl", lambda e, a=V("vsb_lo"), s=VLO, b=b, h=h: e.dma_start(
                out=a[:, :].rearrange("p (k c) -> p k c", c=65)[:, :, 0:64],
                in_=s.rearrange("(bb kt q) o -> bb q kt o", bb=2, q=128)[b][:, :, h * 64:(h + 1) * 64]),
                after=vsl_slot.dep_write(), tag=tag + "_dv2")
            qth_slot.wrote(dq1); qtl_slot.wrote(dq2)
            kth_slot.wrote(dk1); ktl_slot.wrote(dk2)
            vsh_slot.wrote(dv1); vsl_slot.wrote(dv2)
            md = p.dma("sp", "mb", lambda e, a=V("mb64"), s=V("MASKF"), bh=bh: e.dma_start(
                out=a[:, :], in_=s.rearrange("(x i j) -> x i j", i=64, j=64)[bh]),
                after=[H["maskf"]] + mb64_slot.dep_write(), tag=tag + "_mbl")
            mb64_slot.wrote(md)
            km = p.op("dve", lambda d, a=V("kt_hi"), b=V("mb64"): d.tensor_copy(
                a[64:128, :], b[:, :].broadcast_to([64, 64, 32])),
                after=[md] + kth_slot.dep_write(), tag=tag + "_kmx")
            kth_slot.wrote(dk1, km)
            mb64_slot.read(km)
            kdep = [dk1, dk2, km] + (setup_ops if bh == 0 else [])
            mz = p.op("dve", lambda d, a=V("m8"): d.memset(a[:, :], 0.0),
                      after=m8_slot.dep_write(), tag=tag + "_m8z")
            m8_slot.wrote(mz)
            pv_prev = []
            mx_ops = []
            prev = None
            for kt in range(NKT):
                sslot = kt % 2
                PH = V("ptil_hi") if sslot == 0 else V("ptil_hi2")
                PL = V("ptil_lo") if sslot == 0 else V("ptil_lo2")
                ptsl = pt_slots[sslot]
                cast_ops, lo_ops = [], []
                for qc in range(4):
                    sps = pss[qc % 2]
                    dep = [dq1, dq2] + kdep + sps.dep_write()
                    ms = mm3(sps.ap[:, :], V("kt_hi")[:, kt * 128:(kt + 1) * 128],
                             V("kt_lo")[:, kt * 128:(kt + 1) * 128],
                             V("qt_hi")[:, qc * 512:(qc + 1) * 512],
                             V("qt_lo")[:, qc * 512:(qc + 1) * 512], True, True,
                             after=dep, tag=tag + "_smm")
                    qth_slot.read(*ms); qtl_slot.read(ms[1])
                    kth_slot.read(*ms); ktl_slot.read(ms[2])
                    psl = p32_slots[qc % 2]
                    ex = p.op("act", lambda s, a=psl.ap, b=sps.ap, c=call_src, bh=bh: s.activation(
                        a[:, :], b[:, :], AF.Exp, bias=c[:, bh:bh + 1], scale=1.0),
                        after=[ms[-1]] + psl.dep_write() + call_deps[bh], tag=tag + "_exp")
                    psl.wrote(ex)
                    sps.wrote(ex); sps.r = [ex]; sps.w = []
                    ch = p.op("pool", lambda gp, a=PH, b=psl.ap, qc=qc: gp.tensor_copy(
                        a[:, qc * 512:(qc + 1) * 512], b[:, :]),
                        after=[ex] + (ptsl.dep_write() if qc == 0 else []), tag=tag + "_cst")
                    lo = p.op("dve", lambda d, a=PL, b=psl.ap, c=PH, qc=qc: d.scalar_tensor_tensor(
                        out=a[:, qc * 512:(qc + 1) * 512], in0=b[:, :], scalar=1.0,
                        in1=c[:, qc * 512:(qc + 1) * 512], op0=ALU.mult, op1=ALU.subtract),
                        after=[ex, ch], tag=tag + "_plo")
                    mx = p.op("dve", lambda d, a=V("m8"), b=psl.ap, qc=qc: d.tensor_tensor(
                        out=a[:, qc * 512:(qc + 1) * 512], in0=a[:, qc * 512:(qc + 1) * 512],
                        in1=b[:, :], op=ALU.max), after=[ex, mz, lo], tag=tag + "_max")
                    psl.read(ch, lo, mx)
                    m8_slot.read(mx)
                    mx_ops.append(mx)
                    cast_ops.append(ch); lo_ops.append(lo)
                ptsl.wrote(*(cast_ops + lo_ops))
                if prev is not None:
                    pv_prev = emit_pv(prev, dv1, dv2, False, tag)
                prev = (cast_ops + lo_ops, sslot, kt)
            pv_prev = emit_pv(prev, dv1, dv2, True, tag)
            xw = []
            for j in range(16):
                sl = pst_slots[j % 2]
                tr = p.op("te", lambda t, ps=sl.ap, a=V("m8"), i=V("c_id"), j=j: t.transpose(
                    ps[:, 0:128], a[:, j * 128:(j + 1) * 128], i[:, :]),
                    after=mx_ops[-4:] + sl.dep_write(), tag=tag + "_xtr")
                m8_slot.read(tr)
                sl.wrote(tr)
                rd = p.op("dve", lambda d, a=V("xbm"), b=sl.ap, j=j: d.reduce_max(
                    a[:, j:j + 1], b[:, 0:128], axis=mybir.AxisListType.X),
                    after=[tr] + (xbm_slot.dep_write() if j == 0 else []), tag=tag + "_xrd")
                sl.read(rd)
                xw.append(rd)
            xbm_slot.wrote(*xw)
            x2 = p.op("dve", lambda d, a=V("xbm"), b=V("ret"), bh=bh: d.tensor_tensor(
                out=a[:, :], in0=a[:, :], in1=b[:, bh * 16:(bh + 1) * 16], op=ALU.max),
                after=xw + [H["ret"][bh][1]], tag=tag + "_x2")
            xbm_slot.wrote(x2)
            oe = p.op("dve", lambda d, a=V("oev"), b=V("ps_big"), c=V("lot"), bh=bh: d.tensor_tensor(
                out=a[0:65, :], in0=b[0:65, :], in1=c[0:65, bh * 64:(bh + 1) * 64].broadcast_to([65, 64, 32]),
                op=ALU.add), after=pv_prev + [H["lot"][bh]] + oev_slot.dep_write(), tag=tag + "_oev")
            oev_slot.wrote(oe)
            for qcb in obig:
                qcb.wrote(oe); qcb.r = [oe]; qcb.w = []
            if bh == 0:
                d1_ = p.dma("pool", "dbg", lambda e, a=V("DBG_OEV"), b=V("oev"): e.dma_start(
                    out=a[:, :], in_=b[:, :]), after=[oe], tag="dbg_oev")
                oev_slot.read(d1_)
                d2_ = p.dma("pool", "dbg", lambda e, a=V("DBG_XBM"), b=V("xbm"): e.dma_start(
                    out=a[:, 0:16], in_=b[:, :]), after=[x2], tag="dbg_xbm")
                xbm_slot.read(d2_)
                d3_ = p.dma("pool", "dbg", lambda e, a=V("DBG_M8"), b=V("m8"): e.dma_start(
                    out=a[:, :], in_=b[:, :]), after=mx_ops[-4:], tag="dbg_m8")
                m8_slot.read(d3_)
                d4_ = p.dma("pool", "dbg", lambda e, a=V("DBG_PT"), b=V("ptil_hi"): e.dma_start(
                    out=a[:, 0:1024], in_=b[:, :].bitcast(F32)), after=pv_prev, tag="dbg_pt")
                pth_slot.read(d4_)
                d5_ = p.dma("pool", "dbg", lambda e, a=V("DBG_CALL"), b=V("call_a"): e.dma_start(
                    out=a[:, :], in_=b[:, :]), after=call_deps[0], tag="dbg_call")
                d6_ = p.dma("pool", "dbg", lambda e, a=V("DBG_RET"), b=V("ret"): e.dma_start(
                    out=a[:, :], in_=b[:, :]), after=[H["ret"][x][1] for x in range(BH)], tag="dbg_ret")
                d7_ = p.dma("pool", "dbg", lambda e, a=V("DBG_LOT"), b=V("lot"): e.dma_start(
                    out=a[:, :], in_=b[:, :]), after=[H["lot"][x] for x in range(BH)], tag="dbg_lot")
            d3s = []
            for qt in range(16):
                sl = pst_slots[qt % 2]
                tr = p.op("te", lambda t, ps=sl.ap, a=V("oev"), i=V("c_id"), qt=qt: t.transpose(
                    ps[:, 0:65], a[0:65, qt * 128:(qt + 1) * 128], i[0:65, 0:65]),
                    after=[oe] + sl.dep_write(), tag=tag + "_otr")
                oev_slot.read(tr)
                sl.wrote(tr)
                SC = V("sc_c")
                d1 = p.op("dve", lambda d, a=SC, b=sl.ap, x=V("xbm"), qt=qt: d.tensor_scalar(
                    out=a[:, 0:1], in0=x[:, qt:qt + 1], scalar1=1e-6, scalar2=b[:, 64:65],
                    op0=ALU.mult, op1=ALU.add), after=[tr, x2] + [q for q in state.get("scc_r", [])],
                    tag=tag + "_den")
                d2 = p.op("dve", lambda d, a=SC: d.reciprocal(a[:, 1:2], a[:, 0:1]), after=[d1], tag=tag + "_rcp")
                d3 = p.op("dve", lambda d, a=V("yab"), b=sl.ap, c=SC, qt=qt: d.tensor_scalar(
                    out=a[:, qt * 64:(qt + 1) * 64], in0=b[:, 0:64], scalar1=c[:, 1:2],
                    scalar2=None, op0=ALU.mult),
                    after=[d2] + (yab_slot.dep_write() if qt == 0 else []), tag=tag + "_mul")
                state["scc_r"] = [d2, d3]
                if qt == 0:
                    yab_slot.wrote(d3)
                else:
                    yab_slot.w.append(d3)
                sl.read(d1, d3)
                xbm_slot.read(d1)
                d3s.append(d3)
            st = p.dma("sp", "ya_st", lambda e, a=V("YA2"), b_=V("yab"), b=b, h=h: e.dma_start(
                out=bass.AP(tensor=a.tensor, offset=b * 16 * 128 * 1024 + h * 64,
                            ap=[[1024, 128], [131072, 16], [1, 64]]),
                in_=b_[:, :].rearrange("p (t c) -> p t c", c=64)),
                after=d3s, tag=tag + "_yst")
            ya_st[b].append(st)
            yab_slot.read(st)
        return ya_st

    ya_st = attn_main(V("QA_HI"), V("QA_LO"), V("KA_HI"), V("KA_LO"), V("VA_HI"), V("VA_LO"),
                      H["qa_st"], H["ka_st"], H["va_st"], V("call_a"), call_ops_a, "a5")
    H["ya_st"] = ya_st

    def ffn_phase(srcs, dst, per_tg_dep, tag):
        stores = []
        for tg in range(8):
            def consume(k, tsl, tile_splits, xbase, xTsl, tg=tg):
                tt = tg * 4 + k
                for oc in range(4):
                    wsl, (wh, wl, base), wld = stream_wgran(V("A_HI"), V("A_LO"), oc, 256,
                                                            H["a_done"], tag + "_wg")
                    ps = next_psum()
                    last = None
                    for dt in range(8):
                        dep = wld + tile_splits + (ps.dep_write() if dt == 0 else [])
                        ms = mm3(ps.ap[:, 0:256],
                                 V("xnT_hi")[:, xbase + dt * 512 + k * 128:xbase + dt * 512 + (k + 1) * 128],
                                 V("xnT_lo")[:, xbase + dt * 512 + k * 128:xbase + dt * 512 + (k + 1) * 128],
                                 wh[:, base + dt * 256:base + (dt + 1) * 256],
                                 wl[:, base + dt * 256:base + (dt + 1) * 256], dt == 0, False,
                                 after=dep, tag=tag + "_mm")
                        wsl.read(*ms)
                        xTsl.read(*ms)
                        last = ms[-1]
                    bm = p.op("te", lambda t, ps=ps.ap, b=V("brow_f"), o2=V("c_ones2"), oc=oc: t.matmul(
                        ps[:, 0:256], o2[:, 0:128], b[:, oc * 256:(oc + 1) * 256],
                        start=False, stop=True), after=H["bias_rows"], tag=tag + "_bias")
                    ev = p.op("dve", lambda d, a=xev_slot.ap, b=ps.ap, x=tsl.ap, oc=oc: d.scalar_tensor_tensor(
                        out=a[:, 0:256], in0=b[:, 0:256], scalar=1.0,
                        in1=x[:, oc * 256:(oc + 1) * 256], op0=ALU.mult, op1=ALU.add),
                        after=[bm] + xev_slot.dep_write(), tag=tag + "_ev")
                    tsl.read(ev)
                    ps.wrote(ev); ps.r = [ev]; ps.w = []
                    xev_slot.wrote(ev)
                    st = p.dma("sp", "ffn_st", lambda e, a=dst, b=xev_slot.ap, tt=tt, oc=oc: e.dma_start(
                        out=a[tt * 128:(tt + 1) * 128, oc * 256:(oc + 1) * 256], in_=b[:, 0:256]),
                        after=[ev], tag=tag + "_st")
                    xev_slot.read(st)
                    stores.append(st)

            ln_group(tg, srcs, False, extra_dep=per_tg_dep(tg), tag=tag + "_ln", consume=consume)
        return stores

    xev_slot.r += list(m8_slot.r) + list(m8_slot.w)
    state["xca_r_0"] = list(pth_slot.r) + list(pth_slot.w) + list(ptl_slot.r) + list(ptl_slot.w)
    state["xca_r_1"] = list(state["xca_r_0"])
    H["xs1_st"] = ffn_phase([V("XIN"), V("YA2")], V("XS1"),
                            lambda tg: ya_st[tg // 4], "f6")
    pth_slot.r += state["xca_r_0"]
    ptl_slot.r += state["xca_r_0"]

    H["qb_st"], H["kb_st"], H["vb_st"] = [], [], []
    state["xnh_ops"] = []
    for tg in range(8):
        deps = [s for s in H["xs1_st"]]
        split_ops, _, xbase, xTsl = ln_group(tg, [V("XS1")], True, extra_dep=deps, tag="ln3")
        H["qb_st"] += proj_fmajor(tg, split_ops, V("WT_B_HI"), V("WT_B_LO"), None, 0.125,
                                  V("QB_HI"), V("QB_LO"), range(8), "qb_st", "pbq",
                                  wdep=H["wprep_store"][id(V("WT_B_HI"))], xbase=xbase, xTsl=xTsl)
        H["kb_st"] += proj_fmajor(tg, split_ops, V("WT_B_HI"), V("WT_B_LO"), None, 1.0,
                                  V("KB_HI"), V("KB_LO"), range(8, 16), "kb_st", "pbk",
                                  wdep=H["wprep_store"][id(V("WT_B_HI"))], oc_base=8, xbase=xbase, xTsl=xTsl)
        H["vb_st"] += proj_tmajor(tg, split_ops, V("WT_B_HI"), V("WT_B_LO"), 2048, None,
                                  V("VB_HI"), V("VB_LO"), "vb_st", "pbv",
                                  wdep=H["wprep_store"][id(V("WT_B_HI"))], xbase=xbase, xTsl=xTsl)
    H["xnh3"] = list(state["xnh_ops"])

    crow_ld = p.dma("sp", "crow", lambda e, a=V("crow_t"), b=V("RELB"): e.dma_start(
        out=a[0:1, :].rearrange("p (b h) -> p b h", b=2),
        in_=bass.AP(tensor=b.tensor, offset=256 * 16, ap=[[16, 1], [0, 2], [1, 16]])),
        tag="p7_crow")
    _, call_ops_b, _, _ = ph4(H["xnh3"], V("WT_B_HI"), V("WT_B_LO"), V("WT_B_HI"), V("WT_B_LO"),
                           V("WT_V_HI"), V("WT_V_LO"), None, None, None, V("call_b"),
                           full=False, tag="p7b", k_ocbase=8, crow_ld=crow_ld)

    yb_st = []
    state["rrb_r"] = list(xev_slot.r) + list(xev_slot.w) + list(m8_slot.r) + list(m8_slot.w)
    b0_slot = Slot(V("b0t"))
    b1_slot = Slot(V("b1t"))
    rbe_slot = Slot(V("rbext"))
    rc_slot = Slot(V("rcs"))
    yt32_slot = Slot(V("yt32"))
    for h in range(NH):
        r1 = p.dma("sp", "rbc", lambda e, a=V("rbext"), b=V("RELB"), h=h: e.dma_start(
            out=a[0:1, 0:257], in_=b[:, h:h + 1].rearrange("r c -> c r"), allow_slow_non_contiguous=True),
            after=rbe_slot.dep_write(), tag="p8_rbc")
        r2 = p.op("dve", lambda d, a=V("rbext"): d.tensor_scalar(
            out=a[0:1, 257:384], in0=a[0:1, 256:257].broadcast_to([1, 1, 127]), scalar1=1.0,
            scalar2=None, op0=ALU.mult), after=[r1], tag="p8_pad")
        rbe_slot.wrote(r1, r2)
        sk0 = p.dma("pool", "skew", lambda e, a=V("b0t"), b=V("rbext"): e.dma_start(
            out=a[:, :], in_=bass.AP(tensor=b.tensor, offset=128, ap=[[512, 1], [-1, 128], [1, 128]])),
            after=[r1] + b0_slot.dep_write(), tag="p8_sk0")
        sk1 = p.dma("pool", "skew", lambda e, a=V("b1t"), b=V("rbext"): e.dma_start(
            out=a[:, :], in_=bass.AP(tensor=b.tensor, offset=256, ap=[[512, 1], [-1, 128], [1, 128]])),
            after=[r2] + b1_slot.dep_write(), tag="p8_sk1")
        sk0b = p.op("dve", lambda d, a=V("b0t"), c=V("c_tri0"): d.tensor_tensor(
            out=a[:, :], in0=a[:, :], in1=c[:, :], op=ALU.add),
            after=[sk0, const_ops["c_tri0"]], tag="p8_sk0b")
        b0_slot.wrote(sk0, sk0b)
        b1_slot.wrote(sk1)
        rbe_slot.read(sk0, sk1)
        for b in range(NSEQ):
            bh = b * NH + h
            dq1 = p.dma("sp", "qtl", lambda e, a=V("qt_hi"), s=V("QB_HI"), b=b, h=h: e.dma_start(
                out=a[0:64, :], in_=s[h * 64:(h + 1) * 64, b * SEQ:(b + 1) * SEQ]),
                after=H["qb_st"] + qth_slot.dep_write(), tag="p8_dq")
            dq2 = p.dma("sp", "qtl", lambda e, a=V("qt_lo"), s=V("QB_LO"), b=b, h=h: e.dma_start(
                out=a[0:64, :], in_=s[h * 64:(h + 1) * 64, b * SEQ:(b + 1) * SEQ]),
                after=qtl_slot.dep_write(), tag="p8_dq2")
            dk1 = p.dma("sp", "ktl", lambda e, a=V("kt_hi"), s=V("KB_HI"), b=b, h=h: e.dma_start(
                out=a[0:64, :], in_=s[h * 64:(h + 1) * 64, b * SEQ:(b + 1) * SEQ]),
                after=H["kb_st"] + kth_slot.dep_write(), tag="p8_dk")
            dk2 = p.dma("sp", "ktl", lambda e, a=V("kt_lo"), s=V("KB_LO"), b=b, h=h: e.dma_start(
                out=a[0:64, :], in_=s[h * 64:(h + 1) * 64, b * SEQ:(b + 1) * SEQ]),
                after=ktl_slot.dep_write(), tag="p8_dk2")
            dv1 = p.dma("sp", "vtl", lambda e, a=V("vsb_hi"), s=V("VB_HI"), b=b, h=h: e.dma_start(
                out=a[:, :].rearrange("p (k c) -> p k c", c=65)[:, :, 0:64],
                in_=s.rearrange("(bb kt q) o -> bb q kt o", bb=2, q=128)[b][:, :, h * 64:(h + 1) * 64]),
                after=H["vb_st"] + vsh_slot.dep_write(), tag="p8_dv")
            dv2 = p.dma("sp", "vtl", lambda e, a=V("vsb_lo"), s=V("VB_LO"), b=b, h=h: e.dma_start(
                out=a[:, :].rearrange("p (k c) -> p k c", c=65)[:, :, 0:64],
                in_=s.rearrange("(bb kt q) o -> bb q kt o", bb=2, q=128)[b][:, :, h * 64:(h + 1) * 64]),
                after=vsl_slot.dep_write(), tag="p8_dv2")
            qth_slot.wrote(dq1); qtl_slot.wrote(dq2)
            kth_slot.wrote(dk1); ktl_slot.wrote(dk2)
            vsh_slot.wrote(dv1); vsl_slot.wrote(dv2)
            bank_started = [False] * 4
            bank_last = {}
            for qc in range(4):
                bank_last[qc] = min(4 * qc + 3, NKT - 1)

            def pv_and_evac(prevst, is_last):
                ops_dep, sslot, kt, qlo, w = prevst
                PH = V("ptil_hi") if sslot == 0 else V("ptil_hi2")
                PL = V("ptil_lo") if sslot == 0 else V("ptil_lo2")
                ptsl = pt_slots[sslot]
                pv_ops = []
                for pi, (lp, rp) in enumerate(((V("vsb_hi"), PH), (V("vsb_hi"), PL),
                                               (V("vsb_lo"), PH))):
                    for qc in range(qlo // 512, min((qlo + w - 1) // 512 + 1, 4)):
                        lo_q = max(qlo, qc * 512)
                        hi_q = min(qlo + w, (qc + 1) * 512)
                        st = (not bank_started[qc]) and pi == 0
                        if st:
                            bank_started[qc] = True
                        sp = (bank_last[qc] == kt and pi == 2)
                        dep = ops_dep + [dv1, dv2] + (obig[qc].dep_write() if st else [])
                        o = p.op("te", lambda t, ps=obig[qc].ap, lp=lp, rp=rp, kt=kt, lo_q=lo_q,
                                 hi_q=hi_q, qc=qc, st=st, sp=sp, qlo=qlo: t.matmul(
                            ps[0:65, lo_q - qc * 512:hi_q - qc * 512], lp[:, kt * 65:kt * 65 + 65],
                            rp[:, lo_q - qlo:hi_q - qlo], start=st, stop=sp),
                            after=dep, tag="p8_pv")
                        pv_ops.append(o)
                        vsh_slot.read(o); vsl_slot.read(o)
                        ptsl.read(o)
                for qc in range(4):
                    if bank_last[qc] == kt:
                        dcp = p.op("dve", lambda d, a=V("rcs"), b=obig[qc].ap: d.tensor_copy(
                            a[0:1, :], b[64:65, :]), after=pv_ops + rc_slot.dep_write(), tag="p8_dcp")
                        rc_slot.wrote(dcp)
                        bsl = pst_slots[qc % 2]
                        bc = p.op("te", lambda t, ps=bsl.ap, o1=V("c_ones1f"), b=V("rcs"): t.matmul(
                            ps[0:64, 0:512], o1[0:1, 0:64], b[0:1, :], start=True, stop=True),
                            after=[dcp, const_ops["c_ones1f"]] + bsl.dep_write(), tag="p8_bc")
                        rc_slot.read(bc)
                        bsl.wrote(bc)
                        rcp = p.op("dve", lambda d, a=V("rrb"), b=bsl.ap: d.reciprocal(
                            a[:, :], b[0:64, 0:512]), after=[bc] + [x for x in state.get("rrb_r", [])],
                            tag="p8_rcp")
                        bsl.read(rcp)
                        ym = p.op("dve", lambda d, a=V("yt32"), b=obig[qc].ap, c=V("rrb"): d.tensor_tensor(
                            out=a[:, :], in0=b[0:64, :], in1=c[:, :], op=ALU.mult),
                            after=[rcp] + yt32_slot.dep_write(), tag="p8_ym")
                        state["rrb_r"] = [ym]
                        yt32_slot.wrote(ym)
                        obig[qc].wrote(ym); obig[qc].r = [ym, dcp]; obig[qc].w = []
                        pe = next_pev()
                        ph_, pl_, pbase = pe.ap
                        oh, ol = split_evac(V("yt32")[:, :], ph_[0:64, pbase:pbase + 512],
                                            pl_[0:64, pbase:pbase + 512], 1.0,
                                            [ym] + pe.dep_write(), tag="p8_ysp")
                        yt32_slot.read(oh, ol)
                        s1 = p.dma("sp", "yb_st", lambda e, a=V("YB_HI"), b_=ph_, bh=bh, qc=qc, pbase=pbase: e.dma_start(
                            out=a[bh][:, qc * 512:(qc + 1) * 512], in_=b_[0:64, pbase:pbase + 512]),
                            after=[oh], tag="p8_sth")
                        s2 = p.dma("sp", "yb_st", lambda e, a=V("YB_LO"), b_=pl_, bh=bh, qc=qc, pbase=pbase: e.dma_start(
                            out=a[bh][:, qc * 512:(qc + 1) * 512], in_=b_[0:64, pbase:pbase + 512]),
                            after=[ol], tag="p8_stl")
                        pe.w = []; pe.r = [s1, s2]
                        yb_st.extend([s1, s2])
                return pv_ops

            prevst = None
            for kt in range(NKT):
                qlo = kt * 128
                w = min(SEQ - qlo, 640)
                w1 = min(w, 512)
                sslot = kt % 2
                PH = V("ptil_hi") if sslot == 0 else V("ptil_hi2")
                PL = V("ptil_lo") if sslot == 0 else V("ptil_lo2")
                ptsl = pt_slots[sslot]
                sps = pss[kt % 2]
                dep = [dq1, dq2, dk1, dk2] + sps.dep_write()
                ms1 = mm3(sps.ap[:, 0:w1], V("kt_hi")[0:64, kt * 128:(kt + 1) * 128],
                          V("kt_lo")[0:64, kt * 128:(kt + 1) * 128],
                          V("qt_hi")[0:64, qlo:qlo + w1], V("qt_lo")[0:64, qlo:qlo + w1],
                          True, True, after=dep, tag="p8_smm")
                sc_ops = list(ms1)
                d4sl = None
                if w > 512:
                    d4sl = pst_slots[kt % 2]
                    ms2 = mm3(d4sl.ap[:, 0:w - 512], V("kt_hi")[0:64, kt * 128:(kt + 1) * 128],
                              V("kt_lo")[0:64, kt * 128:(kt + 1) * 128],
                              V("qt_hi")[0:64, qlo + 512:qlo + w], V("qt_lo")[0:64, qlo + 512:qlo + w],
                              True, True, after=d4sl.dep_write(), tag="p8_smm2")
                    sc_ops += list(ms2)
                ba = []
                o1 = p.op("dve", lambda d, a=sps.ap, b=V("b0t"): d.tensor_tensor(
                    out=a[:, 0:128], in0=a[:, 0:128], in1=b[:, :], op=ALU.add),
                    after=[ms1[-1], sk0b], tag="p8_b0")
                ba.append(o1)
                b0_slot.read(o1)
                if w > 128:
                    o2 = p.op("dve", lambda d, a=sps.ap, b=V("b1t"): d.tensor_tensor(
                        out=a[:, 128:256], in0=a[:, 128:256], in1=b[:, :], op=ALU.add),
                        after=[ms1[-1], sk1], tag="p8_b1")
                    ba.append(o2)
                    b1_slot.read(o2)
                if w > 512:
                    o3 = p.op("dve", lambda d, a=d4sl.ap, b=V("c_tri4"): d.tensor_tensor(
                        out=a[:, 0:128], in0=a[:, 0:128], in1=b[:, :], op=ALU.add),
                        after=[ms2[-1], const_ops["c_tri4"]], tag="p8_b4")
                    ba.append(o3)
                e1 = p.op("act", lambda s, a=V("p32"), b=sps.ap, c=V("call_b"), bh=bh, w1=w1: s.activation(
                    a[:, 0:min(256, w1)], b[:, 0:min(256, w1)], AF.Exp, bias=c[:, bh:bh + 1], scale=1.0),
                    after=ba + sc_ops + call_ops_b[bh] + [x for x in state.get("p32_r", [])], tag="p8_e1")
                ex_ops = [e1]
                if w1 > 256:
                    e2 = p.op("act", lambda s, a=V("p32"), b=sps.ap, c=V("call_bc"), bh=bh, w1=w1: s.activation(
                        a[:, 256:w1], b[:, 256:w1], AF.Exp, bias=c[:, bh:bh + 1], scale=1.0),
                        after=ba + sc_ops + call_ops_b[bh], tag="p8_e2")
                    ex_ops.append(e2)
                if w > 512:
                    e3 = p.op("act", lambda s, a=V("p32"), b=d4sl.ap, c=V("call_bc"), bh=bh, w=w: s.activation(
                        a[:, 512:w], b[:, 0:w - 512], AF.Exp, bias=c[:, bh:bh + 1], scale=1.0),
                        after=ba + sc_ops + call_ops_b[bh], tag="p8_e3")
                    ex_ops.append(e3)
                sps.wrote(*ex_ops); sps.r = list(ex_ops); sps.w = []
                if w > 512:
                    d4sl.wrote(*ex_ops); d4sl.r = list(ex_ops); d4sl.w = []
                ch = p.op("pool", lambda gp, a=PH, b=V("p32"), w=w: gp.tensor_copy(
                    a[:, 0:w], b[:, 0:w]), after=ex_ops + ptsl.dep_write(), tag="p8_cst")
                lo = p.op("dve", lambda d, a=PL, b=V("p32"), c=PH, w=w: d.scalar_tensor_tensor(
                    out=a[:, 0:w], in0=b[:, 0:w], scalar=1.0, in1=c[:, 0:w], op0=ALU.mult,
                    op1=ALU.subtract), after=ex_ops + [ch], tag="p8_plo")
                state["p32_r"] = [ch, lo]
                ptsl.wrote(ch, lo)
                if prevst is not None:
                    pv_and_evac(prevst, False)
                prevst = (state["p32_r"], sslot, kt, qlo, w)
            pv_and_evac(prevst, True)
    H["yb_st"] = yb_st
    xev_slot.r += state.get("rrb_r", [])

    ybc_h_slot = Slot(V("ybc_hi")); ybc_l_slot = Slot(V("ybc_lo"))
    xca_slot = Slot(V("xc_a"))
    xca_slot.r = list(pth_slot.r) + list(pth_slot.w) + list(ptl_slot.r) + list(ptl_slot.w)
    xs2_st = []
    for tt in range(NT):
        b = tt // 16
        tseq = (tt % 16) * 128
        ylds = []
        dep_h = ybc_h_slot.dep_write(); dep_l = ybc_l_slot.dep_write()
        for half in range(2):
            for (srcy, dsty, dpp) in ((V("YB_HI"), V("ybc_hi"), dep_h), (V("YB_LO"), V("ybc_lo"), dep_l)):
                yld = p.dma("sp", "ybl", lambda e, a=dsty, s=srcy, b=b, half=half, tseq=tseq: e.dma_start(
                    out=a[half * 64:(half + 1) * 64, :].rearrange("r (ht c) -> r ht c", c=128),
                    in_=bass.AP(tensor=s.tensor, offset=(b * 16 + half) * 131072 + tseq,
                                ap=[[2048, 64], [262144, 8], [1, 128]])),
                    after=yb_st + dpp, tag="p9_yl")
                ylds.append(yld)
        ybc_h_slot.wrote(ylds[0], ylds[2])
        ybc_l_slot.wrote(ylds[1], ylds[3])
        xl = p.dma("sp", "x3l", lambda e, a=V("xc_a"), s=V("XS1"), tt=tt: e.dma_start(
            out=a[:, 0:D], in_=s[tt * 128:(tt + 1) * 128, :]),
            after=H["xs1_st"] + xca_slot.dep_write(), tag="p9_xl")
        xca_slot.wrote(xl)
        for dcq in range(4):
            wsl, (wh, wl, base), wld = stream_wgran(V("WT_P_HI"), V("WT_P_LO"), dcq, 256,
                                                    H["wprep_store"][id(V("WT_P_HI"))], "p9_wg")
            ps = next_psum()
            last = None
            for ht in range(8):
                dep = wld + ylds + (ps.dep_write() if ht == 0 else [])
                ms = mm3(ps.ap[:, 0:256], V("ybc_hi")[:, ht * 128:(ht + 1) * 128],
                         V("ybc_lo")[:, ht * 128:(ht + 1) * 128],
                         wh[:, base + ht * 256:base + (ht + 1) * 256],
                         wl[:, base + ht * 256:base + (ht + 1) * 256], ht == 0, False,
                         after=dep, tag="p9_mm")
                wsl.read(*ms)
                ybc_h_slot.read(ms[0], ms[1]); ybc_l_slot.read(ms[2])
                last = ms[-1]
            bm = p.op("te", lambda t, ps=ps.ap, b=V("brow_p"), o2=V("c_ones2"), dcq=dcq: t.matmul(
                ps[:, 0:256], o2[:, 0:128], b[:, dcq * 256:(dcq + 1) * 256], start=False, stop=True),
                after=H["bias_rows"], tag="p9_bias")
            ev = p.op("dve", lambda d, a=xev_slot.ap, b=ps.ap, x=V("xc_a"), dcq=dcq: d.scalar_tensor_tensor(
                out=a[:, 0:256], in0=b[:, 0:256], scalar=1.0, in1=x[:, dcq * 256:(dcq + 1) * 256],
                op0=ALU.mult, op1=ALU.add), after=[bm, xl] + xev_slot.dep_write(), tag="p9_ev")
            xca_slot.read(ev)
            ps.wrote(ev); ps.r = [ev]; ps.w = []
            xev_slot.wrote(ev)
            st = p.dma("sp", "xs2_st", lambda e, a=V("XS2"), b=xev_slot.ap, tt=tt, dcq=dcq: e.dma_start(
                out=a[tt * 128:(tt + 1) * 128, dcq * 256:(dcq + 1) * 256], in_=b[:, 0:256]),
                after=[ev], tag="p9_st")
            xev_slot.read(st)
            xs2_st.append(st)
    H["xs2_st"] = xs2_st

    out_st = ffn_phase([V("XS2")], V("OUT"), lambda tg: xs2_st, "f10")

    p.op("sp", lambda e: e.nop(), after=out_st, tag="final_wait")
    p.finalize()
    return nc


_NC_CACHE = {}


def _get_nc(debug_taps=()):
    key = tuple(sorted(debug_taps))
    if key not in _NC_CACHE:
        _NC_CACHE[key] = build_program(debug_taps=key)
    return _NC_CACHE[key]


def kernel(**inputs):
    def f32(x):
        return np.ascontiguousarray(np.asarray(x), dtype=None) if False else np.ascontiguousarray(np.asarray(x, dtype=np.float32))

    X = f32(inputs["L0_attn_a_X"])
    base = {
        "WQ": f32(inputs["L0_attn_a_Wq"]), "WK": f32(inputs["L0_attn_a_Wk"]),
        "WV": f32(inputs["L0_attn_a_Wv"]), "BQ": f32(inputs["L0_attn_a_bq"]),
        "BK": f32(inputs["L0_attn_a_bk"]), "BV": f32(inputs["L0_attn_a_bv"]),
        "WQKV": f32(inputs["L0_attn_b_w_qkv"]), "WPROJ": f32(inputs["L0_attn_b_w_proj"]),
        "BPROJ": f32(inputs["L0_attn_b_b_proj"]), "RELB": f32(inputs["L0_attn_b_rel_bias"]),
        "WFR": f32(inputs["L0_ffn_w_freq_real"]), "WFI": f32(inputs["L0_ffn_w_freq_imag"]),
        "BFFN": f32(inputs["L0_ffn_bias"]),
    }
    for k, v in CONSTS.items():
        base[k] = v
    nc = _get_nc(_DEBUG_TAPS)
    in_maps = []
    for c in range(8):
        m = dict(base)
        m["XIN"] = np.ascontiguousarray(X[2 * c:2 * c + 2].reshape(T, D))
        in_maps.append(m)
    res = run_bass_kernel_spmd(nc, in_maps, core_ids=list(range(8)))
    if _DEBUG_TAPS:
        global _DEBUG_RESULTS
        _DEBUG_RESULTS = res.results
    out = np.concatenate([np.asarray(r["OUT"]).reshape(NSEQ, SEQ, D) for r in res.results], axis=0)
    return out.astype(np.float32)


_DEBUG_TAPS = ()
_DEBUG_RESULTS = None
